# version 16
# baseline (speedup 1.0000x reference)
import math
from contextlib import ExitStack
from dataclasses import dataclass

import numpy as np

import concourse.bass as bass
import concourse.bacc as bacc
import concourse.mybir as mybir
from concourse.bass_utils import run_bass_kernel_spmd

F32 = mybir.dt.float32
BF16 = mybir.dt.bfloat16
U32 = mybir.dt.uint32
AF = mybir.ActivationFunctionType
OP = mybir.AluOpType

N_CORES = 8
QSCALE = 1.0 / math.sqrt(128.0)


@dataclass(frozen=True)
class Cfg:
    D: int = 4096
    S: int = 2048
    HC: int = 16
    RANK: int = 1024
    INTER: int = 11008
    eps: float = 1e-5

    @property
    def KC(self): return self.D // 128

    @property
    def SC(self): return self.S // 128

    @property
    def TB(self): return self.S // 512

    @property
    def TOWN(self): return self.S // 2

    @property
    def TBO(self): return self.TOWN // 512

    @property
    def RC(self): return self.RANK // 128

    @property
    def IC(self): return self.INTER // 128

    @property
    def NH(self): return self.IC // 2

    @property
    def OCQKV(self): return 3 * self.HC


REAL = Cfg()


class Sem:
    def __init__(self, h):
        self.h = h
        self.n = 0

    def inc(self, instr, amt=1):
        instr.then_inc(self.h, amt)
        self.n += amt
        return self.n


def build_program(c: Cfg):
    nc = bacc.Bacc("TRN2", target_bir_lowering=False)

    din = lambda n, shp, dt=F32: nc.dram_tensor(n, shp, dt, kind="ExternalInput").ap()
    xT = din("xT", [c.D, c.S])
    xresT = din("xresT", [c.D, c.TOWN])
    wqkvT = din("wqkvT", [c.OCQKV, 128, c.KC, 128])
    bqkv = din("bqkv", [128, c.OCQKV])
    woutT = din("woutT", [c.KC, 128, c.KC, 128])
    bout = din("bout", [128, c.KC])
    u1T = din("u1T", [c.RC, 128, c.KC, 128])
    v1T = din("v1T", [2 * c.IC, 128, c.RC, 128])
    u2T = din("u2T", [c.RC, 128, c.IC, 128])
    v2T = din("v2T", [c.KC, 128, c.RC, 128])
    b1 = din("b1", [128, 2 * c.IC])
    b2 = din("b2", [128, c.KC])
    abias = din("abias", [128, c.HC * c.SC])
    nalibi = din("nalibi", [c.HC, c.S])
    trineg = din("trineg", [128, 128])
    othoff = din("othoff", [1, 1], U32)
    ownoff = din("ownoff", [1, 1], U32)

    outT = nc.dram_tensor("outT", [c.D, c.TOWN], F32, kind="ExternalOutput").ap()

    q_scr = nc.dram_tensor("q_scr", [c.HC, 128, c.S], BF16, kind="Internal").ap()
    k_scr = nc.dram_tensor("k_scr", [c.HC, 128, c.S], BF16, kind="Internal").ap()
    v_scr = nc.dram_tensor("v_scr", [c.HC, 128, c.S], BF16, kind="Internal").ap()
    o1_scr = nc.dram_tensor("o1_scr", [c.KC, 128, c.TOWN], F32, kind="Internal").ap()

    with ExitStack() as top:
        sem = lambda name: Sem(top.enter_context(nc.semaphore(name)))
        sem2 = lambda name: [sem(name + "a"), sem(name + "b")]
        MM = sem("MM"); ACT = sem("ACT"); VEC = sem("VEC"); CPI = sem("CPI")
        CLD = sem("CLD")
        XF = sem("XF")
        WQ = sem2("WQ")
        SG = sem2("SG")
        QHs = sem2("QHa") + sem2("QHb")
        KHs = sem2("KHa") + sem2("KHb")
        VHs = sem2("VHa") + sem2("VHb")
        NLs = sem2("NL")
        SD = sem2("SD")
        AO = sem2("AO")
        AOTH = sem("AOTH")
        XL = sem2("XL")
        HSL = sem("HSL")
        RDY = sem("RDY")
        XCH = sem("XCH")
        PREP = sem("PREP")
        OSL = sem2("OSL")
        XR = sem2("XR")
        O1S = sem2("O1S")
        XS = sem2("XS")
        U1S = sem2("U1S")
        V1S = sem2("V1Sa") + sem2("V1Sb")
        U2S = sem2("U2S")
        V2S = sem2("V2S")
        RS = sem2("RS")
        OF = sem2("OF")

        SB = lambda n_, s_, d_: top.enter_context(nc.sbuf_tensor(n_, s_, d_))
        ones_col = SB("ones_col", [1, 128], BF16)
        ones128 = SB("ones128", [128, 1], BF16)
        identb = SB("identb", [128, 128], BF16)
        trineg_sb = SB("trineg_sb", [128, 128], F32)
        abias_sb = SB("abias_sb", [128, c.HC * c.SC], F32)
        bqkv_sb = SB("bqkv_sb", [128, c.OCQKV], F32)
        bqs_sb = SB("bqs_sb", [128, c.HC], F32)
        bout_sb = SB("bout_sb", [128, c.KC], F32)
        b1_sb = SB("b1_sb", [128, 2 * c.IC], F32)
        b2_sb = SB("b2_sb", [128, c.KC], F32)
        own_sb = SB("own_sb", [1, 1], U32)
        oth_sb = SB("oth_sb", [1, 1], U32)

        with nc.Block():
            g, sy, t, sc, v = nc.gpsimd, nc.sync, nc.tensor, nc.scalar, nc.vector

            def fence(*dma_sems):
                for eng in (g, sy, t, sc, v):
                    eng.wait_ge(MM.h, MM.n)
                    eng.wait_ge(ACT.h, ACT.n)
                    eng.wait_ge(VEC.h, VEC.n)
                    for sm in dma_sems:
                        if sm.n:
                            eng.wait_ge(sm.h, sm.n)

            for dst, src in ((trineg_sb[:, :], trineg), (abias_sb[:, :], abias),
                             (bqkv_sb[:, :], bqkv), (bout_sb[:, :], bout),
                             (b1_sb[:, :], b1), (b2_sb[:, :], b2),
                             (own_sb[:, :], ownoff), (oth_sb[:, :], othoff)):
                CLD.inc(g.dma_start(out=dst, in_=src), 16)
            v.memset(ones_col[:, :], 1.0)
            v.memset(ones128[:, :], 1.0)
            ones_done = VEC.inc(v.drain())
            CPI.inc(g.memset(identb[:, :], 0.0))
            g.wait_ge(CPI.h, CPI.n)
            ident_done = CPI.inc(g.affine_select(
                out=identb[:, :], in_=identb[:, :], compare_op=OP.not_equal,
                fill=1.0, base=0, pattern=[[-1, 128]], channel_multiplier=1))
            sc.wait_ge(CLD.h, CLD.n)
            bqs_done = ACT.inc(sc.mul(bqs_sb[:, :], bqkv_sb[:, 0:c.HC], QSCALE))

            with ExitStack() as p12:
                SB2 = lambda n_, s_, d_: p12.enter_context(nc.sbuf_tensor(n_, s_, d_))
                xhat = SB2("xhat", [128, c.KC, c.S], BF16)
                x2t = [SB2(f"x2t{i}", [128, c.S], BF16) for i in range(2)]
                ntmp = SB2("ntmp", [128, c.S], BF16)
                absb = SB2("absb", [128, c.S], BF16)
                bbsb = SB2("bbsb", [128, c.S], BF16)
                wslab = [SB2(f"wslab{i}", [128, c.KC, 128], BF16) for i in range(2)]
                stg = [SB2(f"stg{i}", [128, 512], BF16) for i in range(2)]
                r0 = SB2("r0", [1, c.S], F32)
                r1 = SB2("r1", [1, c.S], F32)
                r2 = SB2("r2", [1, c.S], F32)
                a_bf = SB2("a_bf", [1, c.S], BF16)
                b_bf = SB2("b_bf", [1, c.S], BF16)

                for kc in range(c.KC):
                    XF.inc(g.dma_start(out=xhat[:, kc, :],
                                       in_=xT[kc * 128:(kc + 1) * 128, :]), 16)
                x_ld = XF.n

                with ExitStack() as pst:
                    s_ps = pst.enter_context(nc.psum_tensor("s_ps", [1, c.S], F32))
                    q_ps = pst.enter_context(nc.psum_tensor("q_ps", [1, c.S], F32))
                    x2_guard = [0, 0]
                    sc.wait_ge(XF.h, x_ld)
                    for kc in range(c.KC):
                        if x2_guard[kc % 2]:
                            sc.wait_ge(MM.h, x2_guard[kc % 2])
                        ACT.inc(sc.activation(x2t[kc % 2][:, :], xhat[:, kc, :],
                                              AF.Square))
                        t.wait_ge(ACT.h, ACT.n)
                        if kc == 0:
                            t.wait_ge(VEC.h, ones_done)
                        for tb in range(c.TB):
                            sl = slice(tb * 512, (tb + 1) * 512)
                            t.matmul(s_ps[0:1, sl], ones128[:, :], xhat[:, kc, sl],
                                     start=(kc == 0), stop=(kc == c.KC - 1))
                            mi = t.matmul(q_ps[0:1, sl], ones128[:, :], x2t[kc % 2][:, sl],
                                          start=(kc == 0), stop=(kc == c.KC - 1))
                        MM.inc(mi)
                        x2_guard[kc % 2] = MM.n

                    def ln_rows(sp, qp, T):
                        v.wait_ge(MM.h, MM.n)
                        v.tensor_scalar_mul(r0[:, 0:T], sp, 1.0 / c.D)
                        v.drain()
                        v.tensor_scalar_mul(r1[:, 0:T], qp, 1.0 / c.D)
                        VEC.inc(v.drain())
                        sc.wait_ge(VEC.h, VEC.n)
                        ACT.inc(sc.activation(r2[:, 0:T], r0[:, 0:T], AF.Square))
                        v.wait_ge(ACT.h, ACT.n)
                        v.tensor_sub(r1[:, 0:T], r1[:, 0:T], r2[:, 0:T])
                        v.drain()
                        v.tensor_scalar_add(r1[:, 0:T], r1[:, 0:T], c.eps)
                        VEC.inc(v.drain())
                        sc.wait_ge(VEC.h, VEC.n)
                        ACT.inc(sc.activation(r2[:, 0:T], r1[:, 0:T], AF.Ln))
                        sc.wait_ge(ACT.h, ACT.n)
                        ACT.inc(sc.activation(r1[:, 0:T], r2[:, 0:T], AF.Exp,
                                              scale=-0.5))
                        v.wait_ge(ACT.h, ACT.n)
                        v.scalar_tensor_tensor(r2[:, 0:T], r0[:, 0:T], -1.0, r1[:, 0:T],
                                               OP.mult, OP.mult)
                        v.drain()
                        v.tensor_copy(a_bf[:, 0:T], r1[:, 0:T])
                        v.drain()
                        v.tensor_copy(b_bf[:, 0:T], r2[:, 0:T])
                        return VEC.inc(v.drain())

                    rows_done = ln_rows(s_ps[0:1, :], q_ps[0:1, :], c.S)

                with ExitStack() as pbc:
                    ab_ps = pbc.enter_context(nc.psum_tensor("ab_ps", [128, 512], F32))
                    bb_ps = pbc.enter_context(nc.psum_tensor("bb_ps", [128, 512], F32))
                    t.wait_ge(VEC.h, rows_done)
                    for tb in range(c.TB):
                        sl = slice(tb * 512, (tb + 1) * 512)
                        t.matmul(ab_ps[:, :], ones_col[:, :], a_bf[0:1, sl],
                                 start=True, stop=True)
                        MM.inc(t.matmul(bb_ps[:, :], ones_col[:, :], b_bf[0:1, sl],
                                        start=True, stop=True))
                        sc.wait_ge(MM.h, MM.n)
                        sc.copy(absb[:, sl], ab_ps[:, :])
                        ACT.inc(sc.copy(bbsb[:, sl], bb_ps[:, :]))
                        t.wait_ge(ACT.h, ACT.n)
                    bcast_done = ACT.n

                v.wait_ge(ACT.h, bcast_done)
                for kc in range(c.KC):
                    v.tensor_mul(ntmp[:, :], xhat[:, kc, :], absb[:, :])
                    v.drain()
                    v.tensor_add(xhat[:, kc, :], ntmp[:, :], bbsb[:, :])
                    VEC.inc(v.drain())
                norm_done = VEC.n

                with ExitStack() as ppq:
                    pq = [ppq.enter_context(nc.psum_tensor(f"pq{i}", [128, 512], F32))
                          for i in range(2 * c.TB)]
                    WQ[0].inc(g.dma_start(out=wslab[0][:, :, :], in_=wqkvT[0]), 16)
                    evac_act = [0] * (2 * c.TB)
                    stg_act = [0, 0]
                    oc_mm = [0, 0]
                    for oc in range(c.OCQKV):
                        if oc + 1 < c.OCQKV:
                            par = (oc + 1) % 2
                            if oc_mm[par]:
                                g.wait_ge(MM.h, oc_mm[par])
                            WQ[par].inc(g.dma_start(out=wslab[par][:, :, :],
                                                    in_=wqkvT[oc + 1]), 16)
                        t.wait_ge(WQ[oc % 2].h, 16 * (oc // 2 + 1))
                        if oc == 0:
                            t.wait_ge(VEC.h, norm_done)
                            t.wait_ge(ACT.h, bqs_done)
                        banks = [(oc % 2) * c.TB + tb for tb in range(c.TB)]
                        for bk in banks:
                            if evac_act[bk]:
                                t.wait_ge(ACT.h, evac_act[bk])
                        for kc in range(c.KC):
                            for tb in range(c.TB):
                                sl = slice(tb * 512, (tb + 1) * 512)
                                mi = t.matmul(pq[banks[tb]][:, :], wslab[oc % 2][:, kc, :],
                                              xhat[:, kc, sl],
                                              start=(kc == 0), stop=(kc == c.KC - 1))
                        MM.inc(mi)
                        oc_mm[oc % 2] = MM.n
                        is_q = oc < c.HC
                        hl = oc % c.HC
                        scr = (q_scr, k_scr, v_scr)[oc // c.HC]
                        sc.wait_ge(MM.h, MM.n)
                        for tb in range(c.TB):
                            sbn = tb % 2
                            if SG[sbn].n:
                                sc.wait_ge(SG[sbn].h, SG[sbn].n)
                            if is_q:
                                si = sc.activation(stg[sbn][:, :], pq[banks[tb]][:, :],
                                                   AF.Identity, bias=bqs_sb[:, oc:oc + 1],
                                                   scale=QSCALE)
                            else:
                                si = sc.activation(stg[sbn][:, :], pq[banks[tb]][:, :],
                                                   AF.Identity, bias=bqkv_sb[:, oc:oc + 1])
                            ACT.inc(si)
                            evac_act[banks[tb]] = ACT.n
                            stg_act[sbn] = ACT.n
                            sy.wait_ge(ACT.h, ACT.n)
                            SG[sbn].inc(sy.dma_start(
                                out=scr[hl, :, tb * 512:(tb + 1) * 512],
                                in_=stg[sbn][:, :]), 16)
                    qkv_mm = MM.n
                    qkv_act = ACT.n
                    qkv_sg = (SG[0].n, SG[1].n)

            fence(SG[0], SG[1], WQ[0], WQ[1], XF, CLD)
            PREP.inc(g.remote_sem_update_broadcast(
                RDY.h, HSL.h, rdests=[(0, 1)] + [None] * 7, sim_remote_sem_inc=2))
            g.wait_ge(PREP.h, PREP.n)
            g.trigger_dma(count=1)
            HSL.n += 16

            with ExitStack() as pA:
                SBA = lambda n_, s_, d_: pA.enter_context(nc.sbuf_tensor(n_, s_, d_))
                attn_own = SBA("attn_own", [128, c.HC, c.TOWN], BF16)
                recv_oth = SBA("recv_oth", [128, c.HC, c.TOWN], BF16)
                attn_send = SBA("attn_send", [128, c.HC, c.TOWN], BF16)

                with ExitStack() as p3:
                    SB3 = lambda n_, s_, d_: p3.enter_context(nc.sbuf_tensor(n_, s_, d_))
                    qhb = [SB3(f"qh{i}", [128, c.S], BF16) for i in range(4)]
                    khb = [SB3(f"kh{i}", [128, c.S], BF16) for i in range(4)]
                    vhb = [SB3(f"vh{i}", [128, c.S], BF16) for i in range(2)]
                    vtb = [SB3(f"vt{i}", [128, c.SC, 128], BF16) for i in range(4)]
                    qkel1 = SB3("qkel1", [128, 1024], BF16)
                    mnegs = [SB3(f"mneg{i}", [1, c.S], BF16) for i in range(2)]
                    nalrow = [SB3(f"nal{i}", [1, c.S], F32) for i in range(2)]
                    PT = 3
                    ptb = [SB3(f"pt{i}", [128, PT, 512], BF16) for i in range(2)]
                    attn_h = [SB3(f"attn_h{i}", [128, c.S], BF16) for i in range(2)]
                    rbsb = [SB3(f"rbsb{i}", [128, 512], F32) for i in range(2)]
                    rr32 = [SB3(f"rr32_{i}", [1, 512], F32) for i in range(2)]

                    with ExitStack() as P3s:
                        PS3 = lambda n_, s_, d_=F32: P3s.enter_context(
                            nc.psum_tensor(n_, s_, d_))
                        vt_ps = PS3("vt_ps", [128, 128], BF16)
                        sps = [PS3(f"sps{i}", [128, 512]) for i in range(3)]
                        sum_ps = [PS3(f"sum_ps{i}", [1, 512]) for i in range(2)]
                        o_ps = [PS3(f"o_ps{i}", [128, 512]) for i in range(2)]

                        own_reg = g.value_load(own_sb[0:1, 0:1], min_val=0, max_val=1)
                        oth_reg = g.value_load(oth_sb[0:1, 0:1], min_val=0, max_val=1)

                        NPAIR = c.HC // 2
                        pair_mm = [0] * (NPAIR + 2)
                        pair_vec = [0] * (NPAIR + 2)
                        dsec_done = [0, 0]
                        recip_vec = [0, 0]
                        recip_act = [0, 0]
                        norm_vec = [0, 0]
                        pb_cnt = [0, 0]
                        send_cpy = [0, 0]
                        head_cpy = [0, 0]
                        sps_act = [0, 0, 0]
                        mstep = [0]
                        vt_queue = []
                        vt_state = {"evac": 0}
                        qkel_state = {"mm": 0}
                        vt_mm = [0, 0]
                        PB = [Sem(top.enter_context(nc.semaphore("PBa"))),
                              Sem(top.enter_context(nc.semaphore("PBb")))]

                        def emit_loads(h):
                            sl4 = h % 4
                            gidx = h // 2
                            if gidx >= 2:
                                sy.wait_ge(MM.h, pair_mm[gidx - 2])
                                sy.wait_ge(VEC.h, pair_vec[gidx - 2])
                            QHs[sl4].inc(sy.dma_start(out=qhb[sl4][:, :],
                                                       in_=q_scr[h]), 16)
                            KHs[sl4].inc(sy.dma_start(out=khb[sl4][:, :],
                                                      in_=k_scr[h]), 16)
                            if vt_mm[h % 2]:
                                sy.wait_ge(MM.h, vt_mm[h % 2])
                            VHs[sl4].inc(sy.dma_start(out=vhb[h % 2][:, :],
                                                      in_=v_scr[h]), 16)
                            if dsec_done[h % 2]:
                                sy.wait_ge(VEC.h, dsec_done[h % 2])
                            NLs[h % 2].inc(sy.dma_start(
                                out=nalrow[h % 2][:, :], in_=nalibi[h:h + 1, :]), 16)

                        ld_cnt = lambda h: 16 * (h // 4 + 1)

                        def emit_vt_one():
                            if not vt_queue:
                                return
                            h, i = vt_queue.pop(0)
                            sl4 = h % 4
                            if i == 0:
                                t.wait_ge(VHs[sl4].h, ld_cnt(h))
                                t.wait_ge(CPI.h, ident_done)
                            if vt_state["evac"]:
                                t.wait_ge(VEC.h, vt_state["evac"])
                            MM.inc(t.transpose(vt_ps[:, :],
                                               vhb[h % 2][:, i * 128:(i + 1) * 128],
                                               identb[:, :]))
                            if i == c.SC - 1:
                                vt_mm[h % 2] = MM.n
                            v.wait_ge(MM.h, MM.n)
                            VEC.inc(v.tensor_copy(vtb[sl4][:, i, :], vt_ps[:, :]))
                            vt_state["evac"] = VEC.n

                        def emit_dsec(h):
                            sl = h % 2
                            sl4 = h % 4
                            gidx = h // 2
                            v.wait_ge(QHs[sl4].h, ld_cnt(h))
                            v.wait_ge(KHs[sl4].h, ld_cnt(h))
                            if gidx >= 1:
                                v.wait_ge(MM.h, pair_mm[gidx - 1])
                            v.wait_ge(NLs[sl].h, 16 * (h // 2 + 1))
                            for tb in range(c.TB):
                                slc = slice(tb * 512, (tb + 1) * 512)
                                hslc = slice((tb % 2) * 512, (tb % 2) * 512 + 512)
                                if qkel_state["mm"]:
                                    v.wait_ge(MM.h, qkel_state["mm"])
                                VEC.inc(v.tensor_mul(qkel1[:, hslc],
                                                     qhb[sl4][:, slc],
                                                     khb[sl4][:, slc]))
                                qkel_vec = VEC.n
                                t.wait_ge(VEC.h, qkel_vec)
                                if tb > 0:
                                    t.wait_ge(VEC.h, VEC.n)
                                if recip_vec[sl]:
                                    t.wait_ge(VEC.h, recip_vec[sl])
                                MM.inc(t.matmul(sum_ps[sl][0:1, :], ones128[:, :],
                                                qkel1[:, hslc],
                                                start=True, stop=True))
                                qkel_state["mm"] = MM.n
                                v.wait_ge(MM.h, MM.n)
                                VEC.inc(v.tensor_sub(mnegs[sl][:, slc],
                                                     nalrow[sl][:, slc],
                                                     sum_ps[sl][0:1, :]))
                            dsec_done[sl] = VEC.n

                        sy.wait_ge(SG[0].h, qkv_sg[0])
                        sy.wait_ge(SG[1].h, qkv_sg[1])
                        emit_loads(0)
                        emit_loads(1)
                        vt_queue.extend((h, i) for h in range(min(2, c.HC))
                                        for i in range(c.SC))
                        for _ in range(len(vt_queue)):
                            emit_vt_one()
                        emit_dsec(0)
                        emit_dsec(1)
                        for h in (2, 3):
                            if h < c.HC:
                                emit_loads(h)

                        for gidx in range(NPAIR):
                            hA, hB = 2 * gidx, 2 * gidx + 1
                            slots4 = (hA % 4, hB % 4)
                            if gidx + 1 < NPAIR:
                                vt_queue.extend(
                                    (h, i) for h in (hA + 2, hB + 2)
                                    for i in range(c.SC))
                            exp_act = [{}, {}]
                            for qb in range(c.TB):
                                nkc = (qb + 1) * 4
                                q0 = qb * 512

                                def inline_ba(sl, kc2, first, last):
                                    voff2 = max(0, kc2 * 128 - q0)
                                    if first and recip_vec[sl]:
                                        t.wait_ge(VEC.h, recip_vec[sl])
                                    if first and norm_vec[sl]:
                                        t.wait_ge(VEC.h, norm_vec[sl])
                                    MM.inc(t.matmul(
                                        sum_ps[sl][0:1, voff2:512], ones128[:, :],
                                        ptb[sl][:, kc2 % PT, voff2:512],
                                        start=first, stop=last))
                                    MM.inc(t.matmul(
                                        o_ps[sl][:, voff2:512],
                                        vtb[slots4[sl]][:, kc2, :],
                                        ptb[sl][:, kc2 % PT, voff2:512],
                                        start=first, stop=last))

                                if qb == 0:
                                    t.wait_ge(VEC.h, max(dsec_done))
                                for kc in range(nkc):
                                    voff = max(0, kc * 128 - q0)
                                    for sl in (0, 1):
                                        h = hA + sl
                                        sl4 = slots4[sl]
                                        mstep[0] += 1
                                        bk = mstep[0] % 3
                                        if sps_act[bk]:
                                            t.wait_ge(ACT.h, sps_act[bk])
                                        t.matmul(sps[bk][:, voff:512],
                                                 khb[sl4][:, kc * 128:(kc + 1) * 128],
                                                 qhb[sl4][:, q0 + voff:q0 + 512],
                                                 start=True, stop=False)
                                        MM.inc(t.matmul(
                                            sps[bk][:, voff:512], ones_col[:, :],
                                            mnegs[sl][0:1, q0 + voff:q0 + 512],
                                            start=False, stop=True))
                                        if kc * 128 >= q0:
                                            v.wait_ge(MM.h, MM.n)
                                            VEC.inc(v.tensor_add(
                                                sps[bk][:, voff:voff + 128],
                                                sps[bk][:, voff:voff + 128],
                                                trineg_sb[:, :]))
                                            sc.wait_ge(VEC.h, VEC.n)
                                        else:
                                            sc.wait_ge(MM.h, MM.n)
                                        col = h * c.SC + kc
                                        ACT.inc(sc.activation(
                                            ptb[sl][:, kc % PT, voff:512],
                                            sps[bk][:, voff:512],
                                            AF.Exp, bias=abias_sb[:, col:col + 1]))
                                        sps_act[bk] = ACT.n
                                        exp_act[sl][kc] = ACT.n
                                        if kc >= 2:
                                            kc2 = kc - 2
                                            t.wait_ge(ACT.h, exp_act[sl][kc2])
                                            inline_ba(sl, kc2, kc2 == 0, False)
                                        if qb == c.TB - 1:
                                            emit_vt_one()
                                for sl in (0, 1):
                                    h = hA + sl
                                    for kc2 in range(max(0, nkc - 2), nkc):
                                        t.wait_ge(ACT.h, exp_act[sl][kc2])
                                        inline_ba(sl, kc2, kc2 == 0, kc2 == nkc - 1)
                                    av_mm = MM.n
                                    v.wait_ge(MM.h, av_mm)
                                    if pb_cnt[sl]:
                                        v.wait_ge(PB[sl].h, pb_cnt[sl])
                                    VEC.inc(v.reciprocal(rr32[sl][0:1, :],
                                                         sum_ps[sl][0:1, :]))
                                    recip_vec[sl] = VEC.n
                                    g.wait_ge(VEC.h, recip_vec[sl])
                                    if pb_cnt[sl]:
                                        g.wait_ge(VEC.h, norm_vec[sl])
                                    PB[sl].inc(g.partition_broadcast(
                                        rbsb[sl][:, :], rr32[sl][0:1, :]))
                                    pb_cnt[sl] = PB[sl].n
                                    v.wait_ge(PB[sl].h, pb_cnt[sl])
                                    if gidx >= 1 and qb == 0:
                                        v.wait_ge(SD[sl].h, send_cpy[sl])
                                        v.wait_ge(AO[sl].h, head_cpy[sl])
                                    VEC.inc(v.tensor_mul(
                                        attn_h[sl][:, q0:q0 + 512],
                                        o_ps[sl][:, :], rbsb[sl][:, :]))
                                    norm_vec[sl] = VEC.n

                            pair_mm[gidx] = MM.n
                            pair_vec[gidx] = VEC.n

                            if hA + 2 < c.HC:
                                for _ in range(len(vt_queue)):
                                    emit_vt_one()
                                emit_dsec(hA + 2)
                                emit_dsec(hB + 2)
                            for h in (hA + 4, hB + 4):
                                if h < c.HC:
                                    emit_loads(h)

                            for sl in (0, 1):
                                h = hA + sl
                                g.wait_ge(VEC.h, norm_vec[sl])
                                SD[sl].inc(g.dma_start(
                                    out=attn_send[:, h, :],
                                    in_=attn_h[sl][:, bass.ts(oth_reg, c.TOWN)]), 16)
                                send_cpy[sl] = SD[sl].n
                                AO[sl].inc(g.dma_start(
                                    out=attn_own[:, h, :],
                                    in_=attn_h[sl][:, bass.ts(own_reg, c.TOWN)]), 16)
                                head_cpy[sl] = AO[sl].n

                        g.wait_ge(SD[0].h, SD[0].n)
                        g.wait_ge(SD[1].h, SD[1].n)
                        g.wait_ge(RDY.h, 2)
                        for h in range(c.HC):
                            PREP.inc(g.remote_dma_broadcast(
                                recv_oth[:, h, :], attn_send[:, h, :],
                                XCH.h, XL[0].h,
                                rdests=[(0, 1)] + [None] * 7,
                                sim_remote_sem_inc=2))
                            g.wait_ge(PREP.h, PREP.n)
                            g.trigger_dma(count=1)
                            XL[0].n += 16
                        ao_final = (AO[0].n, AO[1].n)
                        p3_mm = pair_mm[NPAIR - 1]
                        p3_vec = pair_vec[NPAIR - 1]

                fence(AO[0], AO[1], SD[0], SD[1],
                      *QHs, *KHs, *VHs, NLs[0], NLs[1])

                with ExitStack() as p4:
                    SB4 = lambda n_, s_, d_: p4.enter_context(nc.sbuf_tensor(n_, s_, d_))
                    oslab = [SB4(f"oslab{i}", [128, c.KC, 128], BF16) for i in range(2)]
                    xres = [SB4(f"xres{i}", [128, 512], F32) for i in range(2)]
                    t1 = [SB4(f"t1_{i}", [128, 512], F32) for i in range(2)]
                    o1f = [SB4(f"o1f{i}", [128, 512], F32) for i in range(2)]
                    with ExitStack() as po4:
                        po = [po4.enter_context(nc.psum_tensor(f"po{i}", [128, 512], F32))
                              for i in range(4)]
                        OSL[0].inc(g.dma_start(out=oslab[0][:, :, :], in_=woutT[0]), 16)
                        evac_o = [0] * 4
                        o1_vec = [0, 0]
                        xres_vec = [0, 0]
                        oc_mm4 = [0, 0]
                        for oc in range(c.KC):
                            if oc + 1 < c.KC:
                                par = (oc + 1) % 2
                                if oc_mm4[par]:
                                    g.wait_ge(MM.h, oc_mm4[par])
                                OSL[par].inc(g.dma_start(out=oslab[par][:, :, :],
                                                         in_=woutT[oc + 1]), 16)
                            t.wait_ge(OSL[oc % 2].h, 16 * (oc // 2 + 1))
                            if oc == 0:
                                t.wait_ge(AO[0].h, ao_final[0])
                                t.wait_ge(AO[1].h, ao_final[1])
                                t.wait_ge(XCH.h, 2 * c.HC)
                            banks = [(oc % 2) * 2 + tb for tb in range(c.TBO)]
                            for bk in banks:
                                if evac_o[bk]:
                                    t.wait_ge(ACT.h, evac_o[bk])
                            for kc in range(c.KC):
                                src = attn_own if kc < c.HC else recv_oth
                                ksl = kc if kc < c.HC else kc - c.HC
                                for tb in range(c.TBO):
                                    sl = slice(tb * 512, (tb + 1) * 512)
                                    mi = t.matmul(po[banks[tb]][:, :],
                                                  oslab[oc % 2][:, kc, :],
                                                  src[:, ksl, sl],
                                                  start=(kc == 0), stop=(kc == c.KC - 1))
                            MM.inc(mi)
                            oc_mm4[oc % 2] = MM.n
                            mm_now = MM.n
                            for tb in range(c.TBO):
                                if xres_vec[tb]:
                                    sy.wait_ge(VEC.h, xres_vec[tb])
                                XR[tb].inc(sy.dma_start(
                                    out=xres[tb][:, :],
                                    in_=xresT[oc * 128:(oc + 1) * 128,
                                              tb * 512:(tb + 1) * 512]), 16)
                            sc.wait_ge(MM.h, mm_now)
                            for tb in range(c.TBO):
                                if o1_vec[tb]:
                                    sc.wait_ge(VEC.h, o1_vec[tb])
                                si = sc.activation(t1[tb][:, :], po[banks[tb]][:, :],
                                                   AF.Identity, bias=bout_sb[:, oc:oc + 1])
                                ACT.inc(si)
                                evac_o[banks[tb]] = ACT.n
                                v.wait_ge(ACT.h, ACT.n)
                                v.wait_ge(XR[tb].h, XR[tb].n)
                                if o1_vec[tb]:
                                    v.wait_ge(O1S[tb].h, O1S[tb].n)
                                VEC.inc(v.tensor_add(o1f[tb][:, :], t1[tb][:, :],
                                                     xres[tb][:, :]))
                                o1_vec[tb] = VEC.n
                                xres_vec[tb] = VEC.n
                                sy.wait_ge(VEC.h, VEC.n)
                                O1S[tb].inc(sy.dma_start(
                                    out=o1_scr[oc, :, tb * 512:(tb + 1) * 512],
                                    in_=o1f[tb][:, :]), 16)
                        o1s_final = (O1S[0].n, O1S[1].n)

            fence(O1S[0], O1S[1], XR[0], XR[1], OSL[0], OSL[1], XCH,
                  XL[0], XL[1])

            with ExitStack() as p5o:
                SB5 = lambda n_, s_, d_: p5o.enter_context(nc.sbuf_tensor(n_, s_, d_))
                p_sb = SB5("p_sb", [128, c.RC, c.TOWN], BF16)
                xh2s = ExitStack()
                xhat2 = xh2s.enter_context(nc.sbuf_tensor("xhat2", [128, c.KC, c.TOWN], BF16))
                TO = c.TOWN

                with ExitStack() as pl2:
                    SBL = lambda n_, s_, d_: pl2.enter_context(nc.sbuf_tensor(n_, s_, d_))
                    xs = [SBL(f"xs{i}", [128, c.TOWN], BF16) for i in range(2)]
                    x2 = [SBL(f"x2_{i}", [128, c.TOWN], BF16) for i in range(2)]
                    ab2 = SBL("ab2", [128, c.TOWN], BF16)
                    bb2 = SBL("bb2", [128, c.TOWN], BF16)
                    ntm2 = SBL("ntm2", [128, c.TOWN], BF16)
                    r0 = SBL("r0b", [1, c.TOWN], F32)
                    r1 = SBL("r1b", [1, c.TOWN], F32)
                    r2 = SBL("r2b", [1, c.TOWN], F32)
                    a_bf = SBL("a_bf2", [1, c.TOWN], BF16)
                    b_bf = SBL("b_bf2", [1, c.TOWN], BF16)

                    with ExitStack() as pst2:
                        s2 = pst2.enter_context(nc.psum_tensor("s2", [1, c.TOWN], F32))
                        q2 = pst2.enter_context(nc.psum_tensor("q2", [1, c.TOWN], F32))
                        xs_guard = [0, 0]
                        x2_guard = [0, 0]
                        g.wait_ge(O1S[0].h, o1s_final[0])
                        g.wait_ge(O1S[1].h, o1s_final[1])
                        for kc in range(c.KC):
                            par = kc % 2
                            if xs_guard[par]:
                                g.wait_ge(MM.h, xs_guard[par])
                            XS[par].inc(g.dma_start(out=xs[par][:, :], in_=o1_scr[kc]), 16)
                            sc.wait_ge(XS[par].h, XS[par].n)
                            if x2_guard[par]:
                                sc.wait_ge(MM.h, x2_guard[par])
                            ACT.inc(sc.activation(x2[par][:, :], xs[par][:, :], AF.Square))
                            t.wait_ge(ACT.h, ACT.n)
                            for tb in range(c.TBO):
                                sl = slice(tb * 512, (tb + 1) * 512)
                                t.matmul(s2[0:1, sl], ones128[:, :], xs[par][:, sl],
                                         start=(kc == 0), stop=(kc == c.KC - 1))
                                mi = t.matmul(q2[0:1, sl], ones128[:, :], x2[par][:, sl],
                                              start=(kc == 0), stop=(kc == c.KC - 1))
                            MM.inc(mi)
                            xs_guard[par] = MM.n
                            x2_guard[par] = MM.n

                        v.wait_ge(MM.h, MM.n)
                        v.tensor_scalar_mul(r0[:, :], s2[0:1, :], 1.0 / c.D)
                        v.drain()
                        v.tensor_scalar_mul(r1[:, :], q2[0:1, :], 1.0 / c.D)
                        VEC.inc(v.drain())
                        sc.wait_ge(VEC.h, VEC.n)
                        ACT.inc(sc.activation(r2[:, :], r0[:, :], AF.Square))
                        v.wait_ge(ACT.h, ACT.n)
                        v.tensor_sub(r1[:, :], r1[:, :], r2[:, :])
                        v.drain()
                        v.tensor_scalar_add(r1[:, :], r1[:, :], c.eps)
                        VEC.inc(v.drain())
                        sc.wait_ge(VEC.h, VEC.n)
                        ACT.inc(sc.activation(r2[:, :], r1[:, :], AF.Ln))
                        sc.wait_ge(ACT.h, ACT.n)
                        ACT.inc(sc.activation(r1[:, :], r2[:, :], AF.Exp,
                                              scale=-0.5))
                        v.wait_ge(ACT.h, ACT.n)
                        v.scalar_tensor_tensor(r2[:, :], r0[:, :], -1.0, r1[:, :],
                                               OP.mult, OP.mult)
                        v.drain()
                        v.tensor_copy(a_bf[:, :], r1[:, :])
                        v.drain()
                        v.tensor_copy(b_bf[:, :], r2[:, :])
                        rows2_done = VEC.inc(v.drain())

                    with ExitStack() as pbc2:
                        ab_ps = pbc2.enter_context(nc.psum_tensor("ab2p", [128, 512], F32))
                        bb_ps = pbc2.enter_context(nc.psum_tensor("bb2p", [128, 512], F32))
                        t.wait_ge(VEC.h, rows2_done)
                        for tb in range(c.TBO):
                            sl = slice(tb * 512, (tb + 1) * 512)
                            t.matmul(ab_ps[:, :], ones_col[:, :], a_bf[0:1, sl],
                                     start=True, stop=True)
                            MM.inc(t.matmul(bb_ps[:, :], ones_col[:, :], b_bf[0:1, sl],
                                            start=True, stop=True))
                            sc.wait_ge(MM.h, MM.n)
                            sc.copy(ab2[:, sl], ab_ps[:, :])
                            ACT.inc(sc.copy(bb2[:, sl], bb_ps[:, :]))
                            t.wait_ge(ACT.h, ACT.n)
                        bc2_done = ACT.n

                    xs2_guard = [0, 0]
                    g.wait_ge(MM.h, MM.n)
                    g.wait_ge(ACT.h, ACT.n)
                    for kc in range(c.KC):
                        par = kc % 2
                        if xs2_guard[par]:
                            g.wait_ge(VEC.h, xs2_guard[par])
                        XS[par].inc(g.dma_start(out=xs[par][:, :], in_=o1_scr[kc]), 16)
                        v.wait_ge(XS[par].h, XS[par].n)
                        if kc == 0:
                            v.wait_ge(ACT.h, bc2_done)
                        v.tensor_mul(ntm2[:, :], xs[par][:, :], ab2[:, :])
                        v.drain()
                        v.tensor_add(xhat2[:, kc, :], ntm2[:, :], bb2[:, :])
                        VEC.inc(v.drain())
                        xs2_guard[par] = VEC.n
                    xhat2_done = VEC.n

                fence(XS[0], XS[1])

                with ExitStack() as pf:
                    with ExitStack() as pg4:
                        pg = [pg4.enter_context(nc.psum_tensor(f"pg{i}", [128, 512], F32))
                              for i in range(4)]

                        with ExitStack() as pu1:
                            u1s = [pu1.enter_context(
                                nc.sbuf_tensor(f"u1s{i}", [128, c.KC, 128], BF16))
                                for i in range(2)]
                            U1S[0].inc(g.dma_start(out=u1s[0][:, :, :], in_=u1T[0]), 16)
                            pb_act = [0] * 4
                            u1_mm = [0, 0]
                            for oc in range(c.RC):
                                if oc + 1 < c.RC:
                                    par = (oc + 1) % 2
                                    if u1_mm[par]:
                                        g.wait_ge(MM.h, u1_mm[par])
                                    U1S[par].inc(g.dma_start(out=u1s[par][:, :, :],
                                                             in_=u1T[oc + 1]), 16)
                                t.wait_ge(U1S[oc % 2].h, 16 * (oc // 2 + 1))
                                if oc == 0:
                                    t.wait_ge(VEC.h, xhat2_done)
                                banks = [(oc % 2) * 2 + tb for tb in range(c.TBO)]
                                for bk in banks:
                                    if pb_act[bk]:
                                        t.wait_ge(ACT.h, pb_act[bk])
                                for kc in range(c.KC):
                                    for tb in range(c.TBO):
                                        sl = slice(tb * 512, (tb + 1) * 512)
                                        mi = t.matmul(pg[banks[tb]][:, :],
                                                      u1s[oc % 2][:, kc, :],
                                                      xhat2[:, kc, sl],
                                                      start=(kc == 0), stop=(kc == c.KC - 1))
                                MM.inc(mi)
                                u1_mm[oc % 2] = MM.n
                                sc.wait_ge(MM.h, MM.n)
                                for tb in range(c.TBO):
                                    sl = slice(tb * 512, (tb + 1) * 512)
                                    ACT.inc(sc.copy(p_sb[:, oc, sl], pg[banks[tb]][:, :]))
                                    pb_act[banks[tb]] = ACT.n
                            p_done = ACT.n
                            u1_final_mm = MM.n
                        xh2s.close()

                        with ExitStack() as pv1:
                            SBV = lambda n_, s_, d_: pv1.enter_context(
                                nc.sbuf_tensor(n_, s_, d_))
                            v1s = [SBV(f"v1s{i}", [128, c.RC, 128], BF16) for i in range(4)]
                            u2s = [SBV(f"u2s{i}", [128, c.NH, 128], BF16) for i in range(2)]
                            g_act = [SBV(f"g_act{i}", [128, 512], BF16) for i in range(2)]
                            u_act = [SBV(f"u_act{i}", [128, 512], BF16) for i in range(2)]
                            acth = SBV("acth", [128, c.NH, c.TOWN], BF16)
                            z_sb = SBV("z_sb", [128, c.RC, c.TOWN], F32)
                            zb_sb = SBV("zb_sb", [128, c.RC, c.TOWN], BF16)

                            fence(U1S[0], U1S[1])
                            u2_mm_prev = 0
                            z_prev_vec = 0
                            for half in range(2):
                                v1_mm = [0] * 4
                                ga_vec = 0
                                ga_act = [0, 0]
                                for j in range(c.NH):
                                    ic = half * c.NH + j
                                    bg, bu = 2 * (j % 2), 2 * (j % 2) + 1
                                    for buf, col in ((bg, ic), (bu, c.IC + ic)):
                                        if v1_mm[buf]:
                                            g.wait_ge(MM.h, v1_mm[buf])
                                        V1S[buf].inc(g.dma_start(out=v1s[buf][:, :, :],
                                                                 in_=v1T[col]), 16)
                                    t.wait_ge(V1S[bg].h, V1S[bg].n)
                                    t.wait_ge(V1S[bu].h, V1S[bu].n)
                                    if half == 0 and j == 0:
                                        t.wait_ge(ACT.h, p_done)
                                    if half == 1 and j == 0:
                                        t.wait_ge(VEC.h, z_prev_vec)
                                    for tb in range(c.TBO):
                                        pbg, pbu = tb, 2 + tb
                                        if ga_act[tb]:
                                            t.wait_ge(ACT.h, ga_act[tb])
                                        for kc in range(c.RC):
                                            sl = slice(tb * 512, (tb + 1) * 512)
                                            t.matmul(pg[pbg][:, :], v1s[bg][:, kc, :],
                                                     p_sb[:, kc, sl],
                                                     start=(kc == 0), stop=(kc == c.RC - 1))
                                            mi = t.matmul(pg[pbu][:, :], v1s[bu][:, kc, :],
                                                          p_sb[:, kc, sl],
                                                          start=(kc == 0),
                                                          stop=(kc == c.RC - 1))
                                        MM.inc(mi)
                                        v1_mm[bg] = MM.n
                                        v1_mm[bu] = MM.n
                                        sc.wait_ge(MM.h, MM.n)
                                        if ga_vec:
                                            sc.wait_ge(VEC.h, ga_vec)
                                        sc.activation(g_act[tb][:, :], pg[pbg][:, :],
                                                      AF.Silu, bias=b1_sb[:, ic:ic + 1])
                                        ACT.inc(sc.activation(
                                            u_act[tb][:, :], pg[pbu][:, :], AF.Identity,
                                            bias=b1_sb[:, c.IC + ic:c.IC + ic + 1]))
                                        ga_act[tb] = ACT.n
                                        v.wait_ge(ACT.h, ACT.n)
                                        if half == 1 and j == 0 and tb == 0:
                                            v.wait_ge(MM.h, u2_mm_prev)
                                        sl = slice(tb * 512, (tb + 1) * 512)
                                        VEC.inc(v.tensor_mul(acth[:, j, sl],
                                                             g_act[tb][:, :],
                                                             u_act[tb][:, :]))
                                        ga_vec = VEC.n
                                acth_done = VEC.n

                                g.wait_ge(MM.h, MM.n)
                                U2S[0].inc(g.dma_start(
                                    out=u2s[0][:, :, :],
                                    in_=u2T[0][:, half * c.NH:(half + 1) * c.NH, :]), 16)
                                u2_mm = [0, 0]
                                z_vec = [0, 0]
                                u2s_base = (U2S[0].n - 16, U2S[1].n)
                                for oc in range(c.RC):
                                    if oc + 1 < c.RC:
                                        par = (oc + 1) % 2
                                        if u2_mm[par]:
                                            g.wait_ge(MM.h, u2_mm[par])
                                        U2S[par].inc(g.dma_start(
                                            out=u2s[par][:, :, :],
                                            in_=u2T[oc + 1][:, half * c.NH:(half + 1) * c.NH, :]
                                        ), 16)
                                    t.wait_ge(U2S[oc % 2].h, u2s_base[oc % 2] + 16 * (oc // 2 + 1))
                                    if oc == 0:
                                        t.wait_ge(VEC.h, acth_done)
                                        if half == 1:
                                            VEC.inc(v.drain())
                                    for tb in range(c.TBO):
                                        bk = 2 * (oc % 2) + tb
                                        if z_vec[oc % 2]:
                                            t.wait_ge(VEC.h, z_vec[oc % 2])
                                        for kc in range(c.NH):
                                            sl = slice(tb * 512, (tb + 1) * 512)
                                            mi = t.matmul(pg[bk][:, :], u2s[oc % 2][:, kc, :],
                                                          acth[:, kc, sl],
                                                          start=(kc == 0),
                                                          stop=(kc == c.NH - 1))
                                        MM.inc(mi)
                                        u2_mm[oc % 2] = MM.n
                                        v.wait_ge(MM.h, MM.n)
                                        sl = slice(tb * 512, (tb + 1) * 512)
                                        if half == 0:
                                            vi = v.tensor_copy(z_sb[:, oc, sl], pg[bk][:, :])
                                        else:
                                            vi = v.tensor_add(z_sb[:, oc, sl],
                                                              z_sb[:, oc, sl], pg[bk][:, :])
                                        VEC.inc(vi)
                                        z_vec[oc % 2] = VEC.n
                                u2_mm_prev = MM.n
                                z_prev_vec = VEC.n

                            VEC.inc(v.drain())
                            for oc in range(c.RC):
                                v.drain()
                                VEC.inc(v.tensor_copy(zb_sb[:, oc, :], z_sb[:, oc, :]))
                            z_done = VEC.n

                            with ExitStack() as pv2:
                                SBW = lambda n_, s_, d_: pv2.enter_context(
                                    nc.sbuf_tensor(n_, s_, d_))
                                v2s = [SBW(f"v2s{i}", [128, c.RC, 128], BF16)
                                       for i in range(2)]
                                res2 = [SBW(f"res2{i}", [128, 512], F32) for i in range(2)]
                                t2 = [SBW(f"t2_{i}", [128, 512], F32) for i in range(2)]
                                of = [SBW(f"of{i}", [128, 512], F32) for i in range(2)]
                                V2S[0].inc(g.dma_start(out=v2s[0][:, :, :], in_=v2T[0]), 16)
                                ob_act = [0] * 4
                                of_vec = [0, 0]
                                res_vec = [0, 0]
                                v2_mm = [0, 0]
                                for oc in range(c.KC):
                                    if oc + 1 < c.KC:
                                        par = (oc + 1) % 2
                                        if v2_mm[par]:
                                            g.wait_ge(MM.h, v2_mm[par])
                                        V2S[par].inc(g.dma_start(out=v2s[par][:, :, :],
                                                                 in_=v2T[oc + 1]), 16)
                                    t.wait_ge(V2S[oc % 2].h, 16 * (oc // 2 + 1))
                                    if oc == 0:
                                        t.wait_ge(VEC.h, z_done)
                                    banks = [(oc % 2) * 2 + tb for tb in range(c.TBO)]
                                    for bk in banks:
                                        if ob_act[bk]:
                                            t.wait_ge(ACT.h, ob_act[bk])
                                    for kc in range(c.RC):
                                        for tb in range(c.TBO):
                                            sl = slice(tb * 512, (tb + 1) * 512)
                                            mi = t.matmul(pg[banks[tb]][:, :],
                                                          v2s[oc % 2][:, kc, :],
                                                          zb_sb[:, kc, sl],
                                                          start=(kc == 0),
                                                          stop=(kc == c.RC - 1))
                                    MM.inc(mi)
                                    v2_mm[oc % 2] = MM.n
                                    mm_now = MM.n
                                    for tb in range(c.TBO):
                                        if res_vec[tb]:
                                            sy.wait_ge(VEC.h, res_vec[tb])
                                        RS[tb].inc(sy.dma_start(
                                            out=res2[tb][:, :],
                                            in_=o1_scr[oc, :, tb * 512:(tb + 1) * 512]), 16)
                                    sc.wait_ge(MM.h, mm_now)
                                    for tb in range(c.TBO):
                                        if of_vec[tb]:
                                            sc.wait_ge(VEC.h, of_vec[tb])
                                        si = sc.activation(t2[tb][:, :], pg[banks[tb]][:, :],
                                                           AF.Identity,
                                                           bias=b2_sb[:, oc:oc + 1])
                                        ACT.inc(si)
                                        ob_act[banks[tb]] = ACT.n
                                        v.wait_ge(ACT.h, ACT.n)
                                        v.wait_ge(RS[tb].h, RS[tb].n)
                                        if of_vec[tb]:
                                            v.wait_ge(OF[tb].h, OF[tb].n)
                                        VEC.inc(v.tensor_add(of[tb][:, :], t2[tb][:, :],
                                                             res2[tb][:, :]))
                                        of_vec[tb] = VEC.n
                                        res_vec[tb] = VEC.n
                                        sy.wait_ge(VEC.h, VEC.n)
                                        OF[tb].inc(sy.dma_start(
                                            out=outT[oc * 128:(oc + 1) * 128,
                                                     tb * 512:(tb + 1) * 512],
                                            in_=of[tb][:, :]), 16)

            sy.wait_ge(OF[0].h, OF[0].n)
            sy.wait_ge(OF[1].h, OF[1].n)

    return nc



def _tile_w(w):
    K, N = w.shape
    kc, oc = K // 128, N // 128
    t = w.reshape(kc, 128, oc, 128).transpose(2, 1, 0, 3)
    return np.ascontiguousarray(t, dtype=np.float32)


def _alibi_slopes(n_heads, alibi_bias_max=8):
    _n = 2 ** math.ceil(math.log2(n_heads))
    m = np.arange(1, _n + 1, dtype=np.float32) * (alibi_bias_max / _n)
    slopes = 1.0 / np.power(2.0, m)
    if _n != n_heads:
        slopes = np.concatenate([slopes[1::2], slopes[::2]])[:n_heads]
    return slopes.astype(np.float32)


def host_inputs(c: Cfg, x, w_qkv, b_qkv, w_out, b_out, U1, V1, U2, V2, b1, b2):
    f32 = np.float32
    H = 2 * c.HC
    D, S = c.D, c.S
    slopes = _alibi_slopes(H)
    positions = np.arange(1 - S, 1, dtype=f32)

    wqkvT_g = np.ascontiguousarray(np.asarray(w_qkv, f32).T)
    woutT_g = np.ascontiguousarray(np.asarray(w_out, f32).T)
    x = np.asarray(x, f32)
    u1T = _tile_w(np.asarray(U1, f32))
    v1T = _tile_w(np.asarray(V1, f32))
    u2T = _tile_w(np.asarray(U2, f32))
    v2T = _tile_w(np.asarray(V2, f32))
    b1m = np.ascontiguousarray(np.asarray(b1, f32).reshape(2 * c.IC, 128).T)
    b2m = np.ascontiguousarray(np.asarray(b2, f32).reshape(c.KC, 128).T)
    boutm = np.ascontiguousarray(np.asarray(b_out, f32).reshape(c.KC, 128).T)
    tri = np.where(np.arange(128)[None, :] < np.arange(128)[:, None],
                   np.float32(-1e9), np.float32(0.0))
    tri = np.ascontiguousarray(tri, f32)

    in_maps = []
    for core in range(N_CORES):
        b, hh = core // 2, core % 2
        fh = c.HC * 128
        cols = np.r_[hh * fh:(hh + 1) * fh,
                     D + hh * fh:D + (hh + 1) * fh,
                     2 * D + hh * fh:2 * D + (hh + 1) * fh]
        wq_my = wqkvT_g[:, cols]
        bq_my = np.asarray(b_qkv, f32)[cols]
        perm = np.r_[hh * fh:(hh + 1) * fh, (1 - hh) * fh:(2 - hh) * fh]
        wout_my = woutT_g[perm]

        myheads = np.arange(hh * c.HC, (hh + 1) * c.HC)
        ab = np.zeros((128, c.HC * c.SC), f32)
        nal = np.zeros((c.HC, S), f32)
        for i, h in enumerate(myheads):
            nal[i] = -slopes[h] * positions
            for kc in range(c.SC):
                ab[:, i * c.SC + kc] = slopes[h] * positions[kc * 128:(kc + 1) * 128]

        in_maps.append({
            "xT": np.ascontiguousarray(x[b].T, f32),
            "xresT": np.ascontiguousarray(x[b, hh * c.TOWN:(hh + 1) * c.TOWN].T, f32),
            "wqkvT": _tile_w(wq_my),
            "bqkv": np.ascontiguousarray(bq_my.reshape(c.OCQKV, 128).T),
            "woutT": _tile_w(wout_my),
            "bout": boutm,
            "u1T": u1T, "v1T": v1T, "u2T": u2T, "v2T": v2T,
            "b1": b1m, "b2": b2m,
            "abias": ab, "nalibi": nal, "trineg": tri,
            "othoff": np.array([[1 - hh]], np.uint32),
            "ownoff": np.array([[hh]], np.uint32),
        })
    return in_maps


def kernel(L0_attn_x, L0_attn_w_qkv, L0_attn_b_qkv, L0_attn_w_out, L0_attn_b_out,
           L0_ffn_U1, L0_ffn_V1, L0_ffn_U2, L0_ffn_V2, L0_ffn_b1, L0_ffn_b2):
    c = REAL
    in_maps = host_inputs(c, L0_attn_x, L0_attn_w_qkv, L0_attn_b_qkv,
                          L0_attn_w_out, L0_attn_b_out, L0_ffn_U1, L0_ffn_V1,
                          L0_ffn_U2, L0_ffn_V2, L0_ffn_b1, L0_ffn_b2)
    nc = build_program(c)
    nc.finalize()
    res = run_bass_kernel_spmd(nc, in_maps, core_ids=list(range(N_CORES)))
    x = np.asarray(L0_attn_x)
    B, S, D = x.shape
    out = np.empty((B, S, D), np.float32)
    for core in range(N_CORES):
        b, hh = core // 2, core % 2
        piece = np.asarray(res.results[core]["outT"])
        out[b, hh * c.TOWN:(hh + 1) * c.TOWN, :] = piece.T
    return out


# version 17
# speedup vs baseline: 1.2092x; 1.2092x over previous
import math
from contextlib import ExitStack
from dataclasses import dataclass

import numpy as np

import concourse.bass as bass
import concourse.bacc as bacc
import concourse.mybir as mybir
from concourse.bass_utils import run_bass_kernel_spmd

F32 = mybir.dt.float32
BF16 = mybir.dt.bfloat16
U32 = mybir.dt.uint32
AF = mybir.ActivationFunctionType
OP = mybir.AluOpType

N_CORES = 8
QSCALE = 1.0 / math.sqrt(128.0)


@dataclass(frozen=True)
class Cfg:
    D: int = 4096
    S: int = 2048
    HC: int = 16
    RANK: int = 1024
    INTER: int = 11008
    eps: float = 1e-5

    @property
    def KC(self): return self.D // 128

    @property
    def SC(self): return self.S // 128

    @property
    def TB(self): return self.S // 512

    @property
    def TOWN(self): return self.S // 2

    @property
    def TBO(self): return self.TOWN // 512

    @property
    def RC(self): return self.RANK // 128

    @property
    def IC(self): return self.INTER // 128

    @property
    def NH(self): return self.IC // 2

    @property
    def OCQKV(self): return 3 * self.HC


REAL = Cfg()


class Sem:
    def __init__(self, h):
        self.h = h
        self.n = 0

    def inc(self, instr, amt=1):
        instr.then_inc(self.h, amt)
        self.n += amt
        return self.n


def build_program(c: Cfg):
    nc = bacc.Bacc("TRN2", target_bir_lowering=False)

    din = lambda n, shp, dt=F32: nc.dram_tensor(n, shp, dt, kind="ExternalInput").ap()
    xT = din("xT", [c.D, c.S])
    xresT = din("xresT", [c.D, c.TOWN])
    wqkvT = din("wqkvT", [c.OCQKV, 128, c.KC, 128])
    bqkv = din("bqkv", [128, c.OCQKV])
    woutT = din("woutT", [c.KC, 128, c.KC, 128])
    bout = din("bout", [128, c.KC])
    u1T = din("u1T", [c.RC, 128, c.KC, 128])
    v1T = din("v1T", [2 * c.IC, 128, c.RC, 128])
    u2T = din("u2T", [c.RC, 128, c.IC, 128])
    v2T = din("v2T", [c.KC, 128, c.RC, 128])
    b1 = din("b1", [128, 2 * c.IC])
    b2 = din("b2", [128, c.KC])
    abias = din("abias", [128, c.HC * c.SC])
    nalibi = din("nalibi", [c.HC, c.S])
    trineg = din("trineg", [128, 128])
    othoff = din("othoff", [1, 1], U32)
    ownoff = din("ownoff", [1, 1], U32)

    outT = nc.dram_tensor("outT", [c.D, c.TOWN], F32, kind="ExternalOutput").ap()

    q_scr = nc.dram_tensor("q_scr", [c.HC, 128, c.S], BF16, kind="Internal").ap()
    k_scr = nc.dram_tensor("k_scr", [c.HC, 128, c.S], BF16, kind="Internal").ap()
    v_scr = nc.dram_tensor("v_scr", [c.HC, 128, c.S], BF16, kind="Internal").ap()
    o1_scr = nc.dram_tensor("o1_scr", [c.KC, 128, c.TOWN], F32, kind="Internal").ap()

    with ExitStack() as top:
        sem = lambda name: Sem(top.enter_context(nc.semaphore(name)))
        sem2 = lambda name: [sem(name + "a"), sem(name + "b")]
        MM = sem("MM"); ACT = sem("ACT"); VEC = sem("VEC"); CPI = sem("CPI")
        CLD = sem("CLD")
        XF = sem("XF")
        WQ = sem2("WQ")
        SG = sem2("SG")
        QHs = sem2("QHa") + sem2("QHb")
        KHs = sem2("KHa") + sem2("KHb")
        VHs = sem2("VHa") + sem2("VHb")
        NLs = sem2("NL")
        SD = sem2("SD")
        AO = sem2("AO")
        AOTH = sem("AOTH")
        XL = sem2("XL")
        HSL = sem("HSL")
        RDY = sem("RDY")
        XCH = sem("XCH")
        PREP = sem("PREP")
        OSL = sem2("OSL")
        XR = sem2("XR")
        O1S = sem2("O1S")
        XS = sem2("XS")
        U1S = sem2("U1S")
        V1S = sem2("V1Sa") + sem2("V1Sb")
        U2S = sem2("U2S")
        V2S = sem2("V2S")
        RS = sem2("RS")
        OF = sem2("OF")

        SB = lambda n_, s_, d_: top.enter_context(nc.sbuf_tensor(n_, s_, d_))
        ones_col = SB("ones_col", [1, 128], BF16)
        ones128 = SB("ones128", [128, 1], BF16)
        identb = SB("identb", [128, 128], BF16)
        trineg_sb = SB("trineg_sb", [128, 128], F32)
        abias_sb = SB("abias_sb", [128, c.HC * c.SC], F32)
        bqkv_sb = SB("bqkv_sb", [128, c.OCQKV], F32)
        bqs_sb = SB("bqs_sb", [128, c.HC], F32)
        bout_sb = SB("bout_sb", [128, c.KC], F32)
        b1_sb = SB("b1_sb", [128, 2 * c.IC], F32)
        b2_sb = SB("b2_sb", [128, c.KC], F32)
        own_sb = SB("own_sb", [1, 1], U32)
        oth_sb = SB("oth_sb", [1, 1], U32)

        with nc.Block():
            g, sy, t, sc, v = nc.gpsimd, nc.sync, nc.tensor, nc.scalar, nc.vector

            def fence(*dma_sems):
                for eng in (g, sy, t, sc, v):
                    eng.wait_ge(MM.h, MM.n)
                    eng.wait_ge(ACT.h, ACT.n)
                    eng.wait_ge(VEC.h, VEC.n)
                    for sm in dma_sems:
                        if sm.n:
                            eng.wait_ge(sm.h, sm.n)

            for dst, src in ((trineg_sb[:, :], trineg), (abias_sb[:, :], abias),
                             (bqkv_sb[:, :], bqkv), (bout_sb[:, :], bout),
                             (b1_sb[:, :], b1), (b2_sb[:, :], b2),
                             (own_sb[:, :], ownoff), (oth_sb[:, :], othoff)):
                CLD.inc(g.dma_start(out=dst, in_=src), 16)
            v.memset(ones_col[:, :], 1.0)
            v.memset(ones128[:, :], 1.0)
            ones_done = VEC.inc(v.drain())
            CPI.inc(g.memset(identb[:, :], 0.0))
            g.wait_ge(CPI.h, CPI.n)
            ident_done = CPI.inc(g.affine_select(
                out=identb[:, :], in_=identb[:, :], compare_op=OP.not_equal,
                fill=1.0, base=0, pattern=[[-1, 128]], channel_multiplier=1))
            sc.wait_ge(CLD.h, CLD.n)
            bqs_done = ACT.inc(sc.mul(bqs_sb[:, :], bqkv_sb[:, 0:c.HC], QSCALE))

            with ExitStack() as p12:
                SB2 = lambda n_, s_, d_: p12.enter_context(nc.sbuf_tensor(n_, s_, d_))
                xhat = SB2("xhat", [128, c.KC, c.S], BF16)
                x2t = [SB2(f"x2t{i}", [128, c.S], BF16) for i in range(2)]
                ntmp = SB2("ntmp", [128, c.S], BF16)
                absb = SB2("absb", [128, c.S], BF16)
                bbsb = SB2("bbsb", [128, c.S], BF16)
                wslab = [SB2(f"wslab{i}", [128, c.KC, 128], BF16) for i in range(2)]
                stg = [SB2(f"stg{i}", [128, 512], BF16) for i in range(2)]
                r0 = SB2("r0", [1, c.S], F32)
                r1 = SB2("r1", [1, c.S], F32)
                r2 = SB2("r2", [1, c.S], F32)
                a_bf = SB2("a_bf", [1, c.S], BF16)
                b_bf = SB2("b_bf", [1, c.S], BF16)

                for kc in range(c.KC):
                    XF.inc(g.dma_start(out=xhat[:, kc, :],
                                       in_=xT[kc * 128:(kc + 1) * 128, :]), 16)
                x_ld = XF.n

                with ExitStack() as pst:
                    s_ps = pst.enter_context(nc.psum_tensor("s_ps", [1, c.S], F32))
                    q_ps = pst.enter_context(nc.psum_tensor("q_ps", [1, c.S], F32))
                    x2_guard = [0, 0]
                    sc.wait_ge(XF.h, x_ld)
                    for kc in range(c.KC):
                        if x2_guard[kc % 2]:
                            sc.wait_ge(MM.h, x2_guard[kc % 2])
                        ACT.inc(sc.activation(x2t[kc % 2][:, :], xhat[:, kc, :],
                                              AF.Square))
                        t.wait_ge(ACT.h, ACT.n)
                        if kc == 0:
                            t.wait_ge(VEC.h, ones_done)
                        for tb in range(c.TB):
                            sl = slice(tb * 512, (tb + 1) * 512)
                            t.matmul(s_ps[0:1, sl], ones128[:, :], xhat[:, kc, sl],
                                     start=(kc == 0), stop=(kc == c.KC - 1))
                            mi = t.matmul(q_ps[0:1, sl], ones128[:, :], x2t[kc % 2][:, sl],
                                          start=(kc == 0), stop=(kc == c.KC - 1))
                        MM.inc(mi)
                        x2_guard[kc % 2] = MM.n

                    def ln_rows(sp, qp, T):
                        v.wait_ge(MM.h, MM.n)
                        v.tensor_scalar_mul(r0[:, 0:T], sp, 1.0 / c.D)
                        v.drain()
                        v.tensor_scalar_mul(r1[:, 0:T], qp, 1.0 / c.D)
                        VEC.inc(v.drain())
                        sc.wait_ge(VEC.h, VEC.n)
                        ACT.inc(sc.activation(r2[:, 0:T], r0[:, 0:T], AF.Square))
                        v.wait_ge(ACT.h, ACT.n)
                        v.tensor_sub(r1[:, 0:T], r1[:, 0:T], r2[:, 0:T])
                        v.drain()
                        v.tensor_scalar_add(r1[:, 0:T], r1[:, 0:T], c.eps)
                        VEC.inc(v.drain())
                        sc.wait_ge(VEC.h, VEC.n)
                        ACT.inc(sc.activation(r2[:, 0:T], r1[:, 0:T], AF.Ln))
                        sc.wait_ge(ACT.h, ACT.n)
                        ACT.inc(sc.activation(r1[:, 0:T], r2[:, 0:T], AF.Exp,
                                              scale=-0.5))
                        v.wait_ge(ACT.h, ACT.n)
                        v.scalar_tensor_tensor(r2[:, 0:T], r0[:, 0:T], -1.0, r1[:, 0:T],
                                               OP.mult, OP.mult)
                        v.drain()
                        v.tensor_copy(a_bf[:, 0:T], r1[:, 0:T])
                        v.drain()
                        v.tensor_copy(b_bf[:, 0:T], r2[:, 0:T])
                        return VEC.inc(v.drain())

                    rows_done = ln_rows(s_ps[0:1, :], q_ps[0:1, :], c.S)

                with ExitStack() as pbc:
                    ab_ps = pbc.enter_context(nc.psum_tensor("ab_ps", [128, 512], F32))
                    bb_ps = pbc.enter_context(nc.psum_tensor("bb_ps", [128, 512], F32))
                    t.wait_ge(VEC.h, rows_done)
                    for tb in range(c.TB):
                        sl = slice(tb * 512, (tb + 1) * 512)
                        t.matmul(ab_ps[:, :], ones_col[:, :], a_bf[0:1, sl],
                                 start=True, stop=True)
                        MM.inc(t.matmul(bb_ps[:, :], ones_col[:, :], b_bf[0:1, sl],
                                        start=True, stop=True))
                        sc.wait_ge(MM.h, MM.n)
                        sc.copy(absb[:, sl], ab_ps[:, :])
                        ACT.inc(sc.copy(bbsb[:, sl], bb_ps[:, :]))
                        t.wait_ge(ACT.h, ACT.n)
                    bcast_done = ACT.n

                v.wait_ge(ACT.h, bcast_done)
                for kc in range(c.KC):
                    v.tensor_mul(ntmp[:, :], xhat[:, kc, :], absb[:, :])
                    v.drain()
                    v.tensor_add(xhat[:, kc, :], ntmp[:, :], bbsb[:, :])
                    VEC.inc(v.drain())
                norm_done = VEC.n

                with ExitStack() as ppq:
                    pq = [ppq.enter_context(nc.psum_tensor(f"pq{i}", [128, 512], F32))
                          for i in range(2 * c.TB)]
                    WQ[0].inc(g.dma_start(out=wslab[0][:, :, :], in_=wqkvT[0]), 16)
                    evac_act = [0] * (2 * c.TB)
                    stg_act = [0, 0]
                    oc_mm = [0, 0]
                    for oc in range(c.OCQKV):
                        if oc + 1 < c.OCQKV:
                            par = (oc + 1) % 2
                            if oc_mm[par]:
                                g.wait_ge(MM.h, oc_mm[par])
                            WQ[par].inc(g.dma_start(out=wslab[par][:, :, :],
                                                    in_=wqkvT[oc + 1]), 16)
                        t.wait_ge(WQ[oc % 2].h, 16 * (oc // 2 + 1))
                        if oc == 0:
                            t.wait_ge(VEC.h, norm_done)
                            t.wait_ge(ACT.h, bqs_done)
                        banks = [(oc % 2) * c.TB + tb for tb in range(c.TB)]
                        for bk in banks:
                            if evac_act[bk]:
                                t.wait_ge(ACT.h, evac_act[bk])
                        for kc in range(c.KC):
                            for tb in range(c.TB):
                                sl = slice(tb * 512, (tb + 1) * 512)
                                mi = t.matmul(pq[banks[tb]][:, :], wslab[oc % 2][:, kc, :],
                                              xhat[:, kc, sl],
                                              start=(kc == 0), stop=(kc == c.KC - 1))
                        MM.inc(mi)
                        oc_mm[oc % 2] = MM.n
                        is_q = oc < c.HC
                        hl = oc % c.HC
                        scr = (q_scr, k_scr, v_scr)[oc // c.HC]
                        sc.wait_ge(MM.h, MM.n)
                        for tb in range(c.TB):
                            sbn = tb % 2
                            if SG[sbn].n:
                                sc.wait_ge(SG[sbn].h, SG[sbn].n)
                            if is_q:
                                si = sc.activation(stg[sbn][:, :], pq[banks[tb]][:, :],
                                                   AF.Identity, bias=bqs_sb[:, oc:oc + 1],
                                                   scale=QSCALE)
                            else:
                                si = sc.activation(stg[sbn][:, :], pq[banks[tb]][:, :],
                                                   AF.Identity, bias=bqkv_sb[:, oc:oc + 1])
                            ACT.inc(si)
                            evac_act[banks[tb]] = ACT.n
                            stg_act[sbn] = ACT.n
                            sy.wait_ge(ACT.h, ACT.n)
                            SG[sbn].inc(sy.dma_start(
                                out=scr[hl, :, tb * 512:(tb + 1) * 512],
                                in_=stg[sbn][:, :]), 16)
                    qkv_mm = MM.n
                    qkv_act = ACT.n
                    qkv_sg = (SG[0].n, SG[1].n)

            fence(SG[0], SG[1], WQ[0], WQ[1], XF, CLD)
            PREP.inc(g.remote_sem_update_broadcast(
                RDY.h, HSL.h, rdests=[(0, 1)] + [None] * 7, sim_remote_sem_inc=2))
            g.wait_ge(PREP.h, PREP.n)
            g.trigger_dma(count=1)
            HSL.n += 16

            with ExitStack() as pA:
                SBA = lambda n_, s_, d_: pA.enter_context(nc.sbuf_tensor(n_, s_, d_))
                attn_own = SBA("attn_own", [128, c.HC, c.TOWN], BF16)
                recv_oth = SBA("recv_oth", [128, c.HC, c.TOWN], BF16)
                attn_send = SBA("attn_send", [128, c.HC, c.TOWN], BF16)

                with ExitStack() as p3:
                    SB3 = lambda n_, s_, d_: p3.enter_context(nc.sbuf_tensor(n_, s_, d_))
                    qhb = [SB3(f"qh{i}", [128, c.S], BF16) for i in range(4)]
                    khb = [SB3(f"kh{i}", [128, c.S], BF16) for i in range(4)]
                    vhb = [SB3(f"vh{i}", [128, c.S], BF16) for i in range(2)]
                    vtb = [SB3(f"vt{i}", [128, c.SC, 128], BF16) for i in range(4)]
                    qkel1 = SB3("qkel1", [128, 1024], BF16)
                    mnegs = [SB3(f"mneg{i}", [1, c.S], BF16) for i in range(2)]
                    nalrow = [SB3(f"nal{i}", [1, c.S], F32) for i in range(2)]
                    PT = 3
                    ptb = [SB3(f"pt{i}", [128, PT, 512], BF16) for i in range(2)]
                    attn_h = [SB3(f"attn_h{i}", [128, c.S], BF16) for i in range(2)]
                    rbsb = [SB3(f"rbsb{i}", [128, 512], F32) for i in range(2)]
                    rr32 = [SB3(f"rr32_{i}", [1, 512], F32) for i in range(2)]

                    with ExitStack() as P3s:
                        PS3 = lambda n_, s_, d_=F32: P3s.enter_context(
                            nc.psum_tensor(n_, s_, d_))
                        vt_ps = PS3("vt_ps", [128, 128], BF16)
                        sps = [PS3(f"sps{i}", [128, 512]) for i in range(3)]
                        sum_ps = [PS3(f"sum_ps{i}", [1, 512]) for i in range(2)]
                        o_ps = [PS3(f"o_ps{i}", [128, 512]) for i in range(2)]

                        own_reg = g.value_load(own_sb[0:1, 0:1], min_val=0, max_val=1)
                        oth_reg = g.value_load(oth_sb[0:1, 0:1], min_val=0, max_val=1)

                        NPAIR = c.HC // 2
                        pair_mm = [0] * (NPAIR + 2)
                        pair_vec = [0] * (NPAIR + 2)
                        dsec_done = [0, 0]
                        recip_vec = [0, 0]
                        recip_act = [0, 0]
                        norm_vec = [0, 0]
                        pb_cnt = [0, 0]
                        send_cpy = [0, 0]
                        head_cpy = [0, 0]
                        sps_act = [0, 0, 0]
                        mstep = [0]
                        vt_queue = []
                        vt_state = {"evac": 0}
                        qkel_state = {"mm": 0}
                        vt_mm = [0, 0]
                        PB = [Sem(top.enter_context(nc.semaphore("PBa"))),
                              Sem(top.enter_context(nc.semaphore("PBb")))]

                        def emit_loads(h):
                            sl4 = h % 4
                            gidx = h // 2
                            if gidx >= 2:
                                sy.wait_ge(MM.h, pair_mm[gidx - 2])
                                sy.wait_ge(VEC.h, pair_vec[gidx - 2])
                            QHs[sl4].inc(sy.dma_start(out=qhb[sl4][:, :],
                                                       in_=q_scr[h]), 16)
                            KHs[sl4].inc(sy.dma_start(out=khb[sl4][:, :],
                                                      in_=k_scr[h]), 16)
                            if vt_mm[h % 2]:
                                sy.wait_ge(MM.h, vt_mm[h % 2])
                            VHs[sl4].inc(sy.dma_start(out=vhb[h % 2][:, :],
                                                      in_=v_scr[h]), 16)
                            if dsec_done[h % 2]:
                                sy.wait_ge(VEC.h, dsec_done[h % 2])
                            NLs[h % 2].inc(sy.dma_start(
                                out=nalrow[h % 2][:, :], in_=nalibi[h:h + 1, :]), 16)

                        ld_cnt = lambda h: 16 * (h // 4 + 1)

                        def emit_vt_one():
                            if not vt_queue:
                                return
                            h, i = vt_queue.pop(0)
                            sl4 = h % 4
                            if i == 0:
                                t.wait_ge(VHs[sl4].h, ld_cnt(h))
                                t.wait_ge(CPI.h, ident_done)
                            if vt_state["evac"]:
                                t.wait_ge(VEC.h, vt_state["evac"])
                            MM.inc(t.transpose(vt_ps[:, :],
                                               vhb[h % 2][:, i * 128:(i + 1) * 128],
                                               identb[:, :]))
                            if i == c.SC - 1:
                                vt_mm[h % 2] = MM.n
                            v.wait_ge(MM.h, MM.n)
                            VEC.inc(v.tensor_copy(vtb[sl4][:, i, :], vt_ps[:, :]))
                            vt_state["evac"] = VEC.n

                        def emit_dsec(h):
                            sl = h % 2
                            sl4 = h % 4
                            gidx = h // 2
                            v.wait_ge(QHs[sl4].h, ld_cnt(h))
                            v.wait_ge(KHs[sl4].h, ld_cnt(h))
                            if gidx >= 1:
                                v.wait_ge(MM.h, pair_mm[gidx - 1])
                            v.wait_ge(NLs[sl].h, 16 * (h // 2 + 1))
                            for tb in range(c.TB):
                                slc = slice(tb * 512, (tb + 1) * 512)
                                hslc = slice((tb % 2) * 512, (tb % 2) * 512 + 512)
                                if qkel_state["mm"]:
                                    v.wait_ge(MM.h, qkel_state["mm"])
                                VEC.inc(v.tensor_mul(qkel1[:, hslc],
                                                     qhb[sl4][:, slc],
                                                     khb[sl4][:, slc]))
                                qkel_vec = VEC.n
                                t.wait_ge(VEC.h, qkel_vec)
                                if tb > 0:
                                    t.wait_ge(VEC.h, VEC.n)
                                if recip_vec[sl]:
                                    t.wait_ge(VEC.h, recip_vec[sl])
                                MM.inc(t.matmul(sum_ps[sl][0:1, :], ones128[:, :],
                                                qkel1[:, hslc],
                                                start=True, stop=True))
                                qkel_state["mm"] = MM.n
                                v.wait_ge(MM.h, MM.n)
                                VEC.inc(v.tensor_sub(mnegs[sl][:, slc],
                                                     nalrow[sl][:, slc],
                                                     sum_ps[sl][0:1, :]))
                            dsec_done[sl] = VEC.n

                        sy.wait_ge(SG[0].h, qkv_sg[0])
                        sy.wait_ge(SG[1].h, qkv_sg[1])
                        emit_loads(0)
                        emit_loads(1)
                        vt_queue.extend((h, i) for h in range(min(2, c.HC))
                                        for i in range(c.SC))
                        for _ in range(len(vt_queue)):
                            emit_vt_one()
                        emit_dsec(0)
                        emit_dsec(1)
                        for h in (2, 3):
                            if h < c.HC:
                                emit_loads(h)

                        for gidx in range(NPAIR):
                            hA, hB = 2 * gidx, 2 * gidx + 1
                            slots4 = (hA % 4, hB % 4)
                            if gidx + 1 < NPAIR:
                                vt_queue.extend(
                                    (h, i) for h in (hA + 2, hB + 2)
                                    for i in range(c.SC))
                            exp_act = [{}, {}]
                            for qb in range(c.TB):
                                nkc = (qb + 1) * 4
                                q0 = qb * 512

                                def inline_ba(sl, kc2, first, last):
                                    voff2 = max(0, kc2 * 128 - q0)
                                    if first and recip_vec[sl]:
                                        t.wait_ge(VEC.h, recip_vec[sl])
                                    if first and norm_vec[sl]:
                                        t.wait_ge(VEC.h, norm_vec[sl])
                                    MM.inc(t.matmul(
                                        sum_ps[sl][0:1, voff2:512], ones128[:, :],
                                        ptb[sl][:, kc2 % PT, voff2:512],
                                        start=first, stop=last))
                                    MM.inc(t.matmul(
                                        o_ps[sl][:, voff2:512],
                                        vtb[slots4[sl]][:, kc2, :],
                                        ptb[sl][:, kc2 % PT, voff2:512],
                                        start=first, stop=last))

                                if qb == 0:
                                    t.wait_ge(VEC.h, max(dsec_done))
                                    bkw = (mstep[0] + 1) % 3
                                    if sps_act[bkw]:
                                        t.wait_ge(ACT.h, sps_act[bkw])
                                    for _ in range(18):
                                        t.matmul(sps[bkw][:, :],
                                                 identb[:, :],
                                                 qhb[slots4[0]][:, 0:512],
                                                 start=True, stop=True,
                                                 skip_group_check=True)
                                for kc in range(nkc):
                                    voff = max(0, kc * 128 - q0)
                                    for sl in (0, 1):
                                        h = hA + sl
                                        sl4 = slots4[sl]
                                        mstep[0] += 1
                                        bk = mstep[0] % 3
                                        if sps_act[bk]:
                                            t.wait_ge(ACT.h, sps_act[bk])
                                        t.matmul(sps[bk][:, voff:512],
                                                 khb[sl4][:, kc * 128:(kc + 1) * 128],
                                                 qhb[sl4][:, q0 + voff:q0 + 512],
                                                 start=True, stop=False)
                                        MM.inc(t.matmul(
                                            sps[bk][:, voff:512], ones_col[:, :],
                                            mnegs[sl][0:1, q0 + voff:q0 + 512],
                                            start=False, stop=True))
                                        if kc * 128 >= q0:
                                            v.wait_ge(MM.h, MM.n)
                                            VEC.inc(v.tensor_add(
                                                sps[bk][:, voff:voff + 128],
                                                sps[bk][:, voff:voff + 128],
                                                trineg_sb[:, :]))
                                            sc.wait_ge(VEC.h, VEC.n)
                                        else:
                                            sc.wait_ge(MM.h, MM.n)
                                        col = h * c.SC + kc
                                        ACT.inc(sc.activation(
                                            ptb[sl][:, kc % PT, voff:512],
                                            sps[bk][:, voff:512],
                                            AF.Exp, bias=abias_sb[:, col:col + 1]))
                                        sps_act[bk] = ACT.n
                                        exp_act[sl][kc] = ACT.n
                                        if kc >= 2:
                                            kc2 = kc - 2
                                            t.wait_ge(ACT.h, exp_act[sl][kc2])
                                            inline_ba(sl, kc2, kc2 == 0, False)
                                        if qb == c.TB - 1:
                                            emit_vt_one()
                                for sl in (0, 1):
                                    h = hA + sl
                                    for kc2 in range(max(0, nkc - 2), nkc):
                                        t.wait_ge(ACT.h, exp_act[sl][kc2])
                                        inline_ba(sl, kc2, kc2 == 0, kc2 == nkc - 1)
                                    av_mm = MM.n
                                    v.wait_ge(MM.h, av_mm)
                                    if pb_cnt[sl]:
                                        v.wait_ge(PB[sl].h, pb_cnt[sl])
                                    VEC.inc(v.reciprocal(rr32[sl][0:1, :],
                                                         sum_ps[sl][0:1, :]))
                                    recip_vec[sl] = VEC.n
                                    g.wait_ge(VEC.h, recip_vec[sl])
                                    if pb_cnt[sl]:
                                        g.wait_ge(VEC.h, norm_vec[sl])
                                    PB[sl].inc(g.partition_broadcast(
                                        rbsb[sl][:, :], rr32[sl][0:1, :]))
                                    pb_cnt[sl] = PB[sl].n
                                    v.wait_ge(PB[sl].h, pb_cnt[sl])
                                    if gidx >= 1 and qb == 0:
                                        v.wait_ge(SD[sl].h, send_cpy[sl])
                                        v.wait_ge(AO[sl].h, head_cpy[sl])
                                    VEC.inc(v.tensor_mul(
                                        attn_h[sl][:, q0:q0 + 512],
                                        o_ps[sl][:, :], rbsb[sl][:, :]))
                                    norm_vec[sl] = VEC.n

                            pair_mm[gidx] = MM.n
                            pair_vec[gidx] = VEC.n

                            if hA + 2 < c.HC:
                                for _ in range(len(vt_queue)):
                                    emit_vt_one()
                                emit_dsec(hA + 2)
                                emit_dsec(hB + 2)
                            for h in (hA + 4, hB + 4):
                                if h < c.HC:
                                    emit_loads(h)

                            for sl in (0, 1):
                                h = hA + sl
                                g.wait_ge(VEC.h, norm_vec[sl])
                                SD[sl].inc(g.dma_start(
                                    out=attn_send[:, h, :],
                                    in_=attn_h[sl][:, bass.ts(oth_reg, c.TOWN)]), 16)
                                send_cpy[sl] = SD[sl].n
                                AO[sl].inc(g.dma_start(
                                    out=attn_own[:, h, :],
                                    in_=attn_h[sl][:, bass.ts(own_reg, c.TOWN)]), 16)
                                head_cpy[sl] = AO[sl].n

                        g.wait_ge(SD[0].h, SD[0].n)
                        g.wait_ge(SD[1].h, SD[1].n)
                        g.wait_ge(RDY.h, 2)
                        for h in range(c.HC):
                            PREP.inc(g.remote_dma_broadcast(
                                recv_oth[:, h, :], attn_send[:, h, :],
                                XCH.h, XL[0].h,
                                rdests=[(0, 1)] + [None] * 7,
                                sim_remote_sem_inc=2))
                            g.wait_ge(PREP.h, PREP.n)
                            g.trigger_dma(count=1)
                            XL[0].n += 16
                        ao_final = (AO[0].n, AO[1].n)
                        p3_mm = pair_mm[NPAIR - 1]
                        p3_vec = pair_vec[NPAIR - 1]

                fence(AO[0], AO[1], SD[0], SD[1],
                      *QHs, *KHs, *VHs, NLs[0], NLs[1])

                with ExitStack() as p4:
                    SB4 = lambda n_, s_, d_: p4.enter_context(nc.sbuf_tensor(n_, s_, d_))
                    oslab = [SB4(f"oslab{i}", [128, c.KC, 128], BF16) for i in range(2)]
                    xres = [SB4(f"xres{i}", [128, 512], F32) for i in range(2)]
                    t1 = [SB4(f"t1_{i}", [128, 512], F32) for i in range(2)]
                    o1f = [SB4(f"o1f{i}", [128, 512], F32) for i in range(2)]
                    with ExitStack() as po4:
                        po = [po4.enter_context(nc.psum_tensor(f"po{i}", [128, 512], F32))
                              for i in range(4)]
                        OSL[0].inc(g.dma_start(out=oslab[0][:, :, :], in_=woutT[0]), 16)
                        evac_o = [0] * 4
                        o1_vec = [0, 0]
                        xres_vec = [0, 0]
                        oc_mm4 = [0, 0]
                        for oc in range(c.KC):
                            if oc + 1 < c.KC:
                                par = (oc + 1) % 2
                                if oc_mm4[par]:
                                    g.wait_ge(MM.h, oc_mm4[par])
                                OSL[par].inc(g.dma_start(out=oslab[par][:, :, :],
                                                         in_=woutT[oc + 1]), 16)
                            t.wait_ge(OSL[oc % 2].h, 16 * (oc // 2 + 1))
                            if oc == 0:
                                t.wait_ge(AO[0].h, ao_final[0])
                                t.wait_ge(AO[1].h, ao_final[1])
                                t.wait_ge(XCH.h, 2 * c.HC)
                            banks = [(oc % 2) * 2 + tb for tb in range(c.TBO)]
                            for bk in banks:
                                if evac_o[bk]:
                                    t.wait_ge(ACT.h, evac_o[bk])
                            for kc in range(c.KC):
                                src = attn_own if kc < c.HC else recv_oth
                                ksl = kc if kc < c.HC else kc - c.HC
                                for tb in range(c.TBO):
                                    sl = slice(tb * 512, (tb + 1) * 512)
                                    mi = t.matmul(po[banks[tb]][:, :],
                                                  oslab[oc % 2][:, kc, :],
                                                  src[:, ksl, sl],
                                                  start=(kc == 0), stop=(kc == c.KC - 1))
                            MM.inc(mi)
                            oc_mm4[oc % 2] = MM.n
                            mm_now = MM.n
                            for tb in range(c.TBO):
                                if xres_vec[tb]:
                                    sy.wait_ge(VEC.h, xres_vec[tb])
                                XR[tb].inc(sy.dma_start(
                                    out=xres[tb][:, :],
                                    in_=xresT[oc * 128:(oc + 1) * 128,
                                              tb * 512:(tb + 1) * 512]), 16)
                            sc.wait_ge(MM.h, mm_now)
                            for tb in range(c.TBO):
                                if o1_vec[tb]:
                                    sc.wait_ge(VEC.h, o1_vec[tb])
                                si = sc.activation(t1[tb][:, :], po[banks[tb]][:, :],
                                                   AF.Identity, bias=bout_sb[:, oc:oc + 1])
                                ACT.inc(si)
                                evac_o[banks[tb]] = ACT.n
                                v.wait_ge(ACT.h, ACT.n)
                                v.wait_ge(XR[tb].h, XR[tb].n)
                                if o1_vec[tb]:
                                    v.wait_ge(O1S[tb].h, O1S[tb].n)
                                VEC.inc(v.tensor_add(o1f[tb][:, :], t1[tb][:, :],
                                                     xres[tb][:, :]))
                                o1_vec[tb] = VEC.n
                                xres_vec[tb] = VEC.n
                                sy.wait_ge(VEC.h, VEC.n)
                                O1S[tb].inc(sy.dma_start(
                                    out=o1_scr[oc, :, tb * 512:(tb + 1) * 512],
                                    in_=o1f[tb][:, :]), 16)
                        o1s_final = (O1S[0].n, O1S[1].n)

            fence(O1S[0], O1S[1], XR[0], XR[1], OSL[0], OSL[1], XCH,
                  XL[0], XL[1])

            with ExitStack() as p5o:
                SB5 = lambda n_, s_, d_: p5o.enter_context(nc.sbuf_tensor(n_, s_, d_))
                p_sb = SB5("p_sb", [128, c.RC, c.TOWN], BF16)
                xh2s = ExitStack()
                xhat2 = xh2s.enter_context(nc.sbuf_tensor("xhat2", [128, c.KC, c.TOWN], BF16))
                TO = c.TOWN

                with ExitStack() as pl2:
                    SBL = lambda n_, s_, d_: pl2.enter_context(nc.sbuf_tensor(n_, s_, d_))
                    xs = [SBL(f"xs{i}", [128, c.TOWN], BF16) for i in range(2)]
                    x2 = [SBL(f"x2_{i}", [128, c.TOWN], BF16) for i in range(2)]
                    ab2 = SBL("ab2", [128, c.TOWN], BF16)
                    bb2 = SBL("bb2", [128, c.TOWN], BF16)
                    ntm2 = SBL("ntm2", [128, c.TOWN], BF16)
                    r0 = SBL("r0b", [1, c.TOWN], F32)
                    r1 = SBL("r1b", [1, c.TOWN], F32)
                    r2 = SBL("r2b", [1, c.TOWN], F32)
                    a_bf = SBL("a_bf2", [1, c.TOWN], BF16)
                    b_bf = SBL("b_bf2", [1, c.TOWN], BF16)

                    with ExitStack() as pst2:
                        s2 = pst2.enter_context(nc.psum_tensor("s2", [1, c.TOWN], F32))
                        q2 = pst2.enter_context(nc.psum_tensor("q2", [1, c.TOWN], F32))
                        xs_guard = [0, 0]
                        x2_guard = [0, 0]
                        g.wait_ge(O1S[0].h, o1s_final[0])
                        g.wait_ge(O1S[1].h, o1s_final[1])
                        for kc in range(c.KC):
                            par = kc % 2
                            if xs_guard[par]:
                                g.wait_ge(MM.h, xs_guard[par])
                            XS[par].inc(g.dma_start(out=xs[par][:, :], in_=o1_scr[kc]), 16)
                            sc.wait_ge(XS[par].h, XS[par].n)
                            if x2_guard[par]:
                                sc.wait_ge(MM.h, x2_guard[par])
                            ACT.inc(sc.activation(x2[par][:, :], xs[par][:, :], AF.Square))
                            t.wait_ge(ACT.h, ACT.n)
                            for tb in range(c.TBO):
                                sl = slice(tb * 512, (tb + 1) * 512)
                                t.matmul(s2[0:1, sl], ones128[:, :], xs[par][:, sl],
                                         start=(kc == 0), stop=(kc == c.KC - 1))
                                mi = t.matmul(q2[0:1, sl], ones128[:, :], x2[par][:, sl],
                                              start=(kc == 0), stop=(kc == c.KC - 1))
                            MM.inc(mi)
                            xs_guard[par] = MM.n
                            x2_guard[par] = MM.n

                        v.wait_ge(MM.h, MM.n)
                        v.tensor_scalar_mul(r0[:, :], s2[0:1, :], 1.0 / c.D)
                        v.drain()
                        v.tensor_scalar_mul(r1[:, :], q2[0:1, :], 1.0 / c.D)
                        VEC.inc(v.drain())
                        sc.wait_ge(VEC.h, VEC.n)
                        ACT.inc(sc.activation(r2[:, :], r0[:, :], AF.Square))
                        v.wait_ge(ACT.h, ACT.n)
                        v.tensor_sub(r1[:, :], r1[:, :], r2[:, :])
                        v.drain()
                        v.tensor_scalar_add(r1[:, :], r1[:, :], c.eps)
                        VEC.inc(v.drain())
                        sc.wait_ge(VEC.h, VEC.n)
                        ACT.inc(sc.activation(r2[:, :], r1[:, :], AF.Ln))
                        sc.wait_ge(ACT.h, ACT.n)
                        ACT.inc(sc.activation(r1[:, :], r2[:, :], AF.Exp,
                                              scale=-0.5))
                        v.wait_ge(ACT.h, ACT.n)
                        v.scalar_tensor_tensor(r2[:, :], r0[:, :], -1.0, r1[:, :],
                                               OP.mult, OP.mult)
                        v.drain()
                        v.tensor_copy(a_bf[:, :], r1[:, :])
                        v.drain()
                        v.tensor_copy(b_bf[:, :], r2[:, :])
                        rows2_done = VEC.inc(v.drain())

                    with ExitStack() as pbc2:
                        ab_ps = pbc2.enter_context(nc.psum_tensor("ab2p", [128, 512], F32))
                        bb_ps = pbc2.enter_context(nc.psum_tensor("bb2p", [128, 512], F32))
                        t.wait_ge(VEC.h, rows2_done)
                        for tb in range(c.TBO):
                            sl = slice(tb * 512, (tb + 1) * 512)
                            t.matmul(ab_ps[:, :], ones_col[:, :], a_bf[0:1, sl],
                                     start=True, stop=True)
                            MM.inc(t.matmul(bb_ps[:, :], ones_col[:, :], b_bf[0:1, sl],
                                            start=True, stop=True))
                            sc.wait_ge(MM.h, MM.n)
                            sc.copy(ab2[:, sl], ab_ps[:, :])
                            ACT.inc(sc.copy(bb2[:, sl], bb_ps[:, :]))
                            t.wait_ge(ACT.h, ACT.n)
                        bc2_done = ACT.n

                    xs2_guard = [0, 0]
                    g.wait_ge(MM.h, MM.n)
                    g.wait_ge(ACT.h, ACT.n)
                    for kc in range(c.KC):
                        par = kc % 2
                        if xs2_guard[par]:
                            g.wait_ge(VEC.h, xs2_guard[par])
                        XS[par].inc(g.dma_start(out=xs[par][:, :], in_=o1_scr[kc]), 16)
                        v.wait_ge(XS[par].h, XS[par].n)
                        if kc == 0:
                            v.wait_ge(ACT.h, bc2_done)
                        v.tensor_mul(ntm2[:, :], xs[par][:, :], ab2[:, :])
                        v.drain()
                        v.tensor_add(xhat2[:, kc, :], ntm2[:, :], bb2[:, :])
                        VEC.inc(v.drain())
                        xs2_guard[par] = VEC.n
                    xhat2_done = VEC.n

                fence(XS[0], XS[1])

                with ExitStack() as pf:
                    with ExitStack() as pg4:
                        pg = [pg4.enter_context(nc.psum_tensor(f"pg{i}", [128, 512], F32))
                              for i in range(4)]

                        with ExitStack() as pu1:
                            u1s = [pu1.enter_context(
                                nc.sbuf_tensor(f"u1s{i}", [128, c.KC, 128], BF16))
                                for i in range(2)]
                            U1S[0].inc(g.dma_start(out=u1s[0][:, :, :], in_=u1T[0]), 16)
                            pb_act = [0] * 4
                            u1_mm = [0, 0]
                            for oc in range(c.RC):
                                if oc + 1 < c.RC:
                                    par = (oc + 1) % 2
                                    if u1_mm[par]:
                                        g.wait_ge(MM.h, u1_mm[par])
                                    U1S[par].inc(g.dma_start(out=u1s[par][:, :, :],
                                                             in_=u1T[oc + 1]), 16)
                                t.wait_ge(U1S[oc % 2].h, 16 * (oc // 2 + 1))
                                if oc == 0:
                                    t.wait_ge(VEC.h, xhat2_done)
                                banks = [(oc % 2) * 2 + tb for tb in range(c.TBO)]
                                for bk in banks:
                                    if pb_act[bk]:
                                        t.wait_ge(ACT.h, pb_act[bk])
                                for kc in range(c.KC):
                                    for tb in range(c.TBO):
                                        sl = slice(tb * 512, (tb + 1) * 512)
                                        mi = t.matmul(pg[banks[tb]][:, :],
                                                      u1s[oc % 2][:, kc, :],
                                                      xhat2[:, kc, sl],
                                                      start=(kc == 0), stop=(kc == c.KC - 1))
                                MM.inc(mi)
                                u1_mm[oc % 2] = MM.n
                                sc.wait_ge(MM.h, MM.n)
                                for tb in range(c.TBO):
                                    sl = slice(tb * 512, (tb + 1) * 512)
                                    ACT.inc(sc.copy(p_sb[:, oc, sl], pg[banks[tb]][:, :]))
                                    pb_act[banks[tb]] = ACT.n
                            p_done = ACT.n
                            u1_final_mm = MM.n
                        xh2s.close()

                        with ExitStack() as pv1:
                            SBV = lambda n_, s_, d_: pv1.enter_context(
                                nc.sbuf_tensor(n_, s_, d_))
                            v1s = [SBV(f"v1s{i}", [128, c.RC, 128], BF16) for i in range(4)]
                            u2s = [SBV(f"u2s{i}", [128, c.NH, 128], BF16) for i in range(2)]
                            g_act = [SBV(f"g_act{i}", [128, 512], BF16) for i in range(2)]
                            u_act = [SBV(f"u_act{i}", [128, 512], BF16) for i in range(2)]
                            acth = SBV("acth", [128, c.NH, c.TOWN], BF16)
                            z_sb = SBV("z_sb", [128, c.RC, c.TOWN], F32)
                            zb_sb = SBV("zb_sb", [128, c.RC, c.TOWN], BF16)

                            fence(U1S[0], U1S[1])
                            u2_mm_prev = 0
                            z_prev_vec = 0
                            for half in range(2):
                                v1_mm = [0] * 4
                                ga_vec = 0
                                ga_act = [0, 0]
                                for j in range(c.NH):
                                    ic = half * c.NH + j
                                    bg, bu = 2 * (j % 2), 2 * (j % 2) + 1
                                    for buf, col in ((bg, ic), (bu, c.IC + ic)):
                                        if v1_mm[buf]:
                                            g.wait_ge(MM.h, v1_mm[buf])
                                        V1S[buf].inc(g.dma_start(out=v1s[buf][:, :, :],
                                                                 in_=v1T[col]), 16)
                                    t.wait_ge(V1S[bg].h, V1S[bg].n)
                                    t.wait_ge(V1S[bu].h, V1S[bu].n)
                                    if half == 0 and j == 0:
                                        t.wait_ge(ACT.h, p_done)
                                    if half == 1 and j == 0:
                                        t.wait_ge(VEC.h, z_prev_vec)
                                    for tb in range(c.TBO):
                                        pbg, pbu = tb, 2 + tb
                                        if ga_act[tb]:
                                            t.wait_ge(ACT.h, ga_act[tb])
                                        for kc in range(c.RC):
                                            sl = slice(tb * 512, (tb + 1) * 512)
                                            t.matmul(pg[pbg][:, :], v1s[bg][:, kc, :],
                                                     p_sb[:, kc, sl],
                                                     start=(kc == 0), stop=(kc == c.RC - 1))
                                            mi = t.matmul(pg[pbu][:, :], v1s[bu][:, kc, :],
                                                          p_sb[:, kc, sl],
                                                          start=(kc == 0),
                                                          stop=(kc == c.RC - 1))
                                        MM.inc(mi)
                                        v1_mm[bg] = MM.n
                                        v1_mm[bu] = MM.n
                                        sc.wait_ge(MM.h, MM.n)
                                        if ga_vec:
                                            sc.wait_ge(VEC.h, ga_vec)
                                        sc.activation(g_act[tb][:, :], pg[pbg][:, :],
                                                      AF.Silu, bias=b1_sb[:, ic:ic + 1])
                                        ACT.inc(sc.activation(
                                            u_act[tb][:, :], pg[pbu][:, :], AF.Identity,
                                            bias=b1_sb[:, c.IC + ic:c.IC + ic + 1]))
                                        ga_act[tb] = ACT.n
                                        v.wait_ge(ACT.h, ACT.n)
                                        if half == 1 and j == 0 and tb == 0:
                                            v.wait_ge(MM.h, u2_mm_prev)
                                        sl = slice(tb * 512, (tb + 1) * 512)
                                        VEC.inc(v.tensor_mul(acth[:, j, sl],
                                                             g_act[tb][:, :],
                                                             u_act[tb][:, :]))
                                        ga_vec = VEC.n
                                acth_done = VEC.n

                                g.wait_ge(MM.h, MM.n)
                                U2S[0].inc(g.dma_start(
                                    out=u2s[0][:, :, :],
                                    in_=u2T[0][:, half * c.NH:(half + 1) * c.NH, :]), 16)
                                u2_mm = [0, 0]
                                z_vec = [0, 0]
                                u2s_base = (U2S[0].n - 16, U2S[1].n)
                                for oc in range(c.RC):
                                    if oc + 1 < c.RC:
                                        par = (oc + 1) % 2
                                        if u2_mm[par]:
                                            g.wait_ge(MM.h, u2_mm[par])
                                        U2S[par].inc(g.dma_start(
                                            out=u2s[par][:, :, :],
                                            in_=u2T[oc + 1][:, half * c.NH:(half + 1) * c.NH, :]
                                        ), 16)
                                    t.wait_ge(U2S[oc % 2].h, u2s_base[oc % 2] + 16 * (oc // 2 + 1))
                                    if oc == 0:
                                        t.wait_ge(VEC.h, acth_done)
                                        if half == 1:
                                            VEC.inc(v.drain())
                                    for tb in range(c.TBO):
                                        bk = 2 * (oc % 2) + tb
                                        if z_vec[oc % 2]:
                                            t.wait_ge(VEC.h, z_vec[oc % 2])
                                        for kc in range(c.NH):
                                            sl = slice(tb * 512, (tb + 1) * 512)
                                            mi = t.matmul(pg[bk][:, :], u2s[oc % 2][:, kc, :],
                                                          acth[:, kc, sl],
                                                          start=(kc == 0),
                                                          stop=(kc == c.NH - 1))
                                        MM.inc(mi)
                                        u2_mm[oc % 2] = MM.n
                                        v.wait_ge(MM.h, MM.n)
                                        sl = slice(tb * 512, (tb + 1) * 512)
                                        if half == 0:
                                            vi = v.tensor_copy(z_sb[:, oc, sl], pg[bk][:, :])
                                        else:
                                            vi = v.tensor_add(z_sb[:, oc, sl],
                                                              z_sb[:, oc, sl], pg[bk][:, :])
                                        VEC.inc(vi)
                                        z_vec[oc % 2] = VEC.n
                                u2_mm_prev = MM.n
                                z_prev_vec = VEC.n

                            VEC.inc(v.drain())
                            for oc in range(c.RC):
                                v.drain()
                                VEC.inc(v.tensor_copy(zb_sb[:, oc, :], z_sb[:, oc, :]))
                            z_done = VEC.n

                            with ExitStack() as pv2:
                                SBW = lambda n_, s_, d_: pv2.enter_context(
                                    nc.sbuf_tensor(n_, s_, d_))
                                v2s = [SBW(f"v2s{i}", [128, c.RC, 128], BF16)
                                       for i in range(2)]
                                res2 = [SBW(f"res2{i}", [128, 512], F32) for i in range(2)]
                                t2 = [SBW(f"t2_{i}", [128, 512], F32) for i in range(2)]
                                of = [SBW(f"of{i}", [128, 512], F32) for i in range(2)]
                                V2S[0].inc(g.dma_start(out=v2s[0][:, :, :], in_=v2T[0]), 16)
                                ob_act = [0] * 4
                                of_vec = [0, 0]
                                res_vec = [0, 0]
                                v2_mm = [0, 0]
                                for oc in range(c.KC):
                                    if oc + 1 < c.KC:
                                        par = (oc + 1) % 2
                                        if v2_mm[par]:
                                            g.wait_ge(MM.h, v2_mm[par])
                                        V2S[par].inc(g.dma_start(out=v2s[par][:, :, :],
                                                                 in_=v2T[oc + 1]), 16)
                                    t.wait_ge(V2S[oc % 2].h, 16 * (oc // 2 + 1))
                                    if oc == 0:
                                        t.wait_ge(VEC.h, z_done)
                                    banks = [(oc % 2) * 2 + tb for tb in range(c.TBO)]
                                    for bk in banks:
                                        if ob_act[bk]:
                                            t.wait_ge(ACT.h, ob_act[bk])
                                    for kc in range(c.RC):
                                        for tb in range(c.TBO):
                                            sl = slice(tb * 512, (tb + 1) * 512)
                                            mi = t.matmul(pg[banks[tb]][:, :],
                                                          v2s[oc % 2][:, kc, :],
                                                          zb_sb[:, kc, sl],
                                                          start=(kc == 0),
                                                          stop=(kc == c.RC - 1))
                                    MM.inc(mi)
                                    v2_mm[oc % 2] = MM.n
                                    mm_now = MM.n
                                    for tb in range(c.TBO):
                                        if res_vec[tb]:
                                            sy.wait_ge(VEC.h, res_vec[tb])
                                        RS[tb].inc(sy.dma_start(
                                            out=res2[tb][:, :],
                                            in_=o1_scr[oc, :, tb * 512:(tb + 1) * 512]), 16)
                                    sc.wait_ge(MM.h, mm_now)
                                    for tb in range(c.TBO):
                                        if of_vec[tb]:
                                            sc.wait_ge(VEC.h, of_vec[tb])
                                        si = sc.activation(t2[tb][:, :], pg[banks[tb]][:, :],
                                                           AF.Identity,
                                                           bias=b2_sb[:, oc:oc + 1])
                                        ACT.inc(si)
                                        ob_act[banks[tb]] = ACT.n
                                        v.wait_ge(ACT.h, ACT.n)
                                        v.wait_ge(RS[tb].h, RS[tb].n)
                                        if of_vec[tb]:
                                            v.wait_ge(OF[tb].h, OF[tb].n)
                                        VEC.inc(v.tensor_add(of[tb][:, :], t2[tb][:, :],
                                                             res2[tb][:, :]))
                                        of_vec[tb] = VEC.n
                                        res_vec[tb] = VEC.n
                                        sy.wait_ge(VEC.h, VEC.n)
                                        OF[tb].inc(sy.dma_start(
                                            out=outT[oc * 128:(oc + 1) * 128,
                                                     tb * 512:(tb + 1) * 512],
                                            in_=of[tb][:, :]), 16)

            sy.wait_ge(OF[0].h, OF[0].n)
            sy.wait_ge(OF[1].h, OF[1].n)

    return nc



def _tile_w(w):
    K, N = w.shape
    kc, oc = K // 128, N // 128
    t = w.reshape(kc, 128, oc, 128).transpose(2, 1, 0, 3)
    return np.ascontiguousarray(t, dtype=np.float32)


def _alibi_slopes(n_heads, alibi_bias_max=8):
    _n = 2 ** math.ceil(math.log2(n_heads))
    m = np.arange(1, _n + 1, dtype=np.float32) * (alibi_bias_max / _n)
    slopes = 1.0 / np.power(2.0, m)
    if _n != n_heads:
        slopes = np.concatenate([slopes[1::2], slopes[::2]])[:n_heads]
    return slopes.astype(np.float32)


def host_inputs(c: Cfg, x, w_qkv, b_qkv, w_out, b_out, U1, V1, U2, V2, b1, b2):
    f32 = np.float32
    H = 2 * c.HC
    D, S = c.D, c.S
    slopes = _alibi_slopes(H)
    positions = np.arange(1 - S, 1, dtype=f32)

    wqkvT_g = np.ascontiguousarray(np.asarray(w_qkv, f32).T)
    woutT_g = np.ascontiguousarray(np.asarray(w_out, f32).T)
    x = np.asarray(x, f32)
    u1T = _tile_w(np.asarray(U1, f32))
    v1T = _tile_w(np.asarray(V1, f32))
    u2T = _tile_w(np.asarray(U2, f32))
    v2T = _tile_w(np.asarray(V2, f32))
    b1m = np.ascontiguousarray(np.asarray(b1, f32).reshape(2 * c.IC, 128).T)
    b2m = np.ascontiguousarray(np.asarray(b2, f32).reshape(c.KC, 128).T)
    boutm = np.ascontiguousarray(np.asarray(b_out, f32).reshape(c.KC, 128).T)
    tri = np.where(np.arange(128)[None, :] < np.arange(128)[:, None],
                   np.float32(-1e9), np.float32(0.0))
    tri = np.ascontiguousarray(tri, f32)

    in_maps = []
    for core in range(N_CORES):
        b, hh = core // 2, core % 2
        fh = c.HC * 128
        cols = np.r_[hh * fh:(hh + 1) * fh,
                     D + hh * fh:D + (hh + 1) * fh,
                     2 * D + hh * fh:2 * D + (hh + 1) * fh]
        wq_my = wqkvT_g[:, cols]
        bq_my = np.asarray(b_qkv, f32)[cols]
        perm = np.r_[hh * fh:(hh + 1) * fh, (1 - hh) * fh:(2 - hh) * fh]
        wout_my = woutT_g[perm]

        myheads = np.arange(hh * c.HC, (hh + 1) * c.HC)
        ab = np.zeros((128, c.HC * c.SC), f32)
        nal = np.zeros((c.HC, S), f32)
        for i, h in enumerate(myheads):
            nal[i] = -slopes[h] * positions
            for kc in range(c.SC):
                ab[:, i * c.SC + kc] = slopes[h] * positions[kc * 128:(kc + 1) * 128]

        in_maps.append({
            "xT": np.ascontiguousarray(x[b].T, f32),
            "xresT": np.ascontiguousarray(x[b, hh * c.TOWN:(hh + 1) * c.TOWN].T, f32),
            "wqkvT": _tile_w(wq_my),
            "bqkv": np.ascontiguousarray(bq_my.reshape(c.OCQKV, 128).T),
            "woutT": _tile_w(wout_my),
            "bout": boutm,
            "u1T": u1T, "v1T": v1T, "u2T": u2T, "v2T": v2T,
            "b1": b1m, "b2": b2m,
            "abias": ab, "nalibi": nal, "trineg": tri,
            "othoff": np.array([[1 - hh]], np.uint32),
            "ownoff": np.array([[hh]], np.uint32),
        })
    return in_maps


def kernel(L0_attn_x, L0_attn_w_qkv, L0_attn_b_qkv, L0_attn_w_out, L0_attn_b_out,
           L0_ffn_U1, L0_ffn_V1, L0_ffn_U2, L0_ffn_V2, L0_ffn_b1, L0_ffn_b2):
    c = REAL
    in_maps = host_inputs(c, L0_attn_x, L0_attn_w_qkv, L0_attn_b_qkv,
                          L0_attn_w_out, L0_attn_b_out, L0_ffn_U1, L0_ffn_V1,
                          L0_ffn_U2, L0_ffn_V2, L0_ffn_b1, L0_ffn_b2)
    nc = build_program(c)
    nc.finalize()
    res = run_bass_kernel_spmd(nc, in_maps, core_ids=list(range(N_CORES)))
    x = np.asarray(L0_attn_x)
    B, S, D = x.shape
    out = np.empty((B, S, D), np.float32)
    for core in range(N_CORES):
        b, hh = core // 2, core % 2
        piece = np.asarray(res.results[core]["outT"])
        out[b, hh * c.TOWN:(hh + 1) * c.TOWN, :] = piece.T
    return out


# version 18
# speedup vs baseline: 1.5474x; 1.2796x over previous
import math
from contextlib import ExitStack
from dataclasses import dataclass

import numpy as np

import concourse.bass as bass
import concourse.bacc as bacc
import concourse.mybir as mybir
from concourse.bass_utils import run_bass_kernel_spmd

F32 = mybir.dt.float32
BF16 = mybir.dt.bfloat16
U32 = mybir.dt.uint32
AF = mybir.ActivationFunctionType
OP = mybir.AluOpType

N_CORES = 8
QSCALE = 1.0 / math.sqrt(128.0)


@dataclass(frozen=True)
class Cfg:
    D: int = 4096
    S: int = 2048
    HC: int = 16
    RANK: int = 1024
    INTER: int = 11008
    eps: float = 1e-5

    @property
    def KC(self): return self.D // 128

    @property
    def SC(self): return self.S // 128

    @property
    def TB(self): return self.S // 512

    @property
    def TOWN(self): return self.S // 2

    @property
    def TBO(self): return self.TOWN // 512

    @property
    def RC(self): return self.RANK // 128

    @property
    def IC(self): return self.INTER // 128

    @property
    def NH(self): return self.IC // 2

    @property
    def OCQKV(self): return 3 * self.HC


REAL = Cfg()


class Sem:
    def __init__(self, h):
        self.h = h
        self.n = 0

    def inc(self, instr, amt=1):
        instr.then_inc(self.h, amt)
        self.n += amt
        return self.n


def build_program(c: Cfg):
    nc = bacc.Bacc("TRN2", target_bir_lowering=False)

    din = lambda n, shp, dt=F32: nc.dram_tensor(n, shp, dt, kind="ExternalInput").ap()
    xT = din("xT", [c.D, c.S])
    xresT = din("xresT", [c.D, c.TOWN])
    wqkvT = din("wqkvT", [c.OCQKV, 128, c.KC, 128])
    bqkv = din("bqkv", [128, c.OCQKV])
    woutT = din("woutT", [c.KC, 128, c.KC, 128])
    bout = din("bout", [128, c.KC])
    u1T = din("u1T", [c.RC, 128, c.KC, 128])
    v1T = din("v1T", [2 * c.IC, 128, c.RC, 128])
    u2T = din("u2T", [c.RC, 128, c.IC, 128])
    v2T = din("v2T", [c.KC, 128, c.RC, 128])
    b1 = din("b1", [128, 2 * c.IC])
    b2 = din("b2", [128, c.KC])
    abias = din("abias", [128, c.HC * c.SC])
    nalibi = din("nalibi", [c.HC, c.S])
    trineg = din("trineg", [128, 128])
    othoff = din("othoff", [1, 1], U32)
    ownoff = din("ownoff", [1, 1], U32)

    outT = nc.dram_tensor("outT", [c.D, c.TOWN], F32, kind="ExternalOutput").ap()

    q_scr = nc.dram_tensor("q_scr", [c.HC, 128, c.S], BF16, kind="Internal").ap()
    k_scr = nc.dram_tensor("k_scr", [c.HC, 128, c.S], BF16, kind="Internal").ap()
    v_scr = nc.dram_tensor("v_scr", [c.HC, 128, c.S], BF16, kind="Internal").ap()
    o1_scr = nc.dram_tensor("o1_scr", [c.KC, 128, c.TOWN], F32, kind="Internal").ap()

    with ExitStack() as top:
        sem = lambda name: Sem(top.enter_context(nc.semaphore(name)))
        sem2 = lambda name: [sem(name + "a"), sem(name + "b")]
        MM = sem("MM"); ACT = sem("ACT"); VEC = sem("VEC"); CPI = sem("CPI")
        CLD = sem("CLD")
        XF = sem("XF")
        WQ = sem2("WQ")
        SG = sem2("SG")
        QHs = sem2("QHa") + sem2("QHb")
        KHs = sem2("KHa") + sem2("KHb")
        VHs = sem2("VHa") + sem2("VHb")
        NLs = sem2("NL")
        SD = sem2("SD")
        AO = sem2("AO")
        AOTH = sem("AOTH")
        XL = sem2("XL")
        HSL = sem("HSL")
        RDY = sem("RDY")
        XCH = sem("XCH")
        PREP = sem("PREP")
        OSL = sem2("OSL")
        XR = sem2("XR")
        O1S = sem2("O1S")
        XS = sem2("XS")
        U1S = sem2("U1S")
        V1S = sem2("V1Sa") + sem2("V1Sb")
        U2S = sem2("U2S")
        V2S = sem2("V2S")
        RS = sem2("RS")
        OF = sem2("OF")

        SB = lambda n_, s_, d_: top.enter_context(nc.sbuf_tensor(n_, s_, d_))
        ones_col = SB("ones_col", [1, 128], BF16)
        ones128 = SB("ones128", [128, 1], BF16)
        identb = SB("identb", [128, 128], BF16)
        trineg_sb = SB("trineg_sb", [128, 128], F32)
        abias_sb = SB("abias_sb", [128, c.HC * c.SC], F32)
        bqkv_sb = SB("bqkv_sb", [128, c.OCQKV], F32)
        bqs_sb = SB("bqs_sb", [128, c.HC], F32)
        bout_sb = SB("bout_sb", [128, c.KC], F32)
        b1_sb = SB("b1_sb", [128, 2 * c.IC], F32)
        b2_sb = SB("b2_sb", [128, c.KC], F32)
        own_sb = SB("own_sb", [1, 1], U32)
        oth_sb = SB("oth_sb", [1, 1], U32)

        with nc.Block():
            g, sy, t, sc, v = nc.gpsimd, nc.sync, nc.tensor, nc.scalar, nc.vector

            def fence(*dma_sems):
                for eng in (g, sy, t, sc, v):
                    eng.wait_ge(MM.h, MM.n)
                    eng.wait_ge(ACT.h, ACT.n)
                    eng.wait_ge(VEC.h, VEC.n)
                    for sm in dma_sems:
                        if sm.n:
                            eng.wait_ge(sm.h, sm.n)

            for dst, src in ((trineg_sb[:, :], trineg), (abias_sb[:, :], abias),
                             (bqkv_sb[:, :], bqkv), (bout_sb[:, :], bout),
                             (b1_sb[:, :], b1), (b2_sb[:, :], b2),
                             (own_sb[:, :], ownoff), (oth_sb[:, :], othoff)):
                CLD.inc(g.dma_start(out=dst, in_=src), 16)
            v.memset(ones_col[:, :], 1.0)
            v.memset(ones128[:, :], 1.0)
            ones_done = VEC.inc(v.drain())
            CPI.inc(g.memset(identb[:, :], 0.0))
            g.wait_ge(CPI.h, CPI.n)
            ident_done = CPI.inc(g.affine_select(
                out=identb[:, :], in_=identb[:, :], compare_op=OP.not_equal,
                fill=1.0, base=0, pattern=[[-1, 128]], channel_multiplier=1))
            sc.wait_ge(CLD.h, CLD.n)
            bqs_done = ACT.inc(sc.mul(bqs_sb[:, :], bqkv_sb[:, 0:c.HC], QSCALE))

            with ExitStack() as p12:
                SB2 = lambda n_, s_, d_: p12.enter_context(nc.sbuf_tensor(n_, s_, d_))
                xhat = SB2("xhat", [128, c.KC, c.S], BF16)
                x2t = [SB2(f"x2t{i}", [128, c.S], BF16) for i in range(2)]
                ntmp = SB2("ntmp", [128, c.S], BF16)
                absb = SB2("absb", [128, c.S], BF16)
                bbsb = SB2("bbsb", [128, c.S], BF16)
                wslab = [SB2(f"wslab{i}", [128, c.KC, 128], BF16) for i in range(2)]
                stg = [SB2(f"stg{i}", [128, 512], BF16) for i in range(2)]
                r0 = SB2("r0", [1, c.S], F32)
                r1 = SB2("r1", [1, c.S], F32)
                r2 = SB2("r2", [1, c.S], F32)
                a_bf = SB2("a_bf", [1, c.S], BF16)
                b_bf = SB2("b_bf", [1, c.S], BF16)

                for kc in range(c.KC):
                    XF.inc(g.dma_start(out=xhat[:, kc, :],
                                       in_=xT[kc * 128:(kc + 1) * 128, :]), 16)
                x_ld = XF.n

                with ExitStack() as pst:
                    s_ps = pst.enter_context(nc.psum_tensor("s_ps", [1, c.S], F32))
                    q_ps = pst.enter_context(nc.psum_tensor("q_ps", [1, c.S], F32))
                    x2_guard = [0, 0]
                    sc.wait_ge(XF.h, x_ld)
                    for kc in range(c.KC):
                        if x2_guard[kc % 2]:
                            sc.wait_ge(MM.h, x2_guard[kc % 2])
                        ACT.inc(sc.activation(x2t[kc % 2][:, :], xhat[:, kc, :],
                                              AF.Square))
                        t.wait_ge(ACT.h, ACT.n)
                        if kc == 0:
                            t.wait_ge(VEC.h, ones_done)
                        for tb in range(c.TB):
                            sl = slice(tb * 512, (tb + 1) * 512)
                            t.matmul(s_ps[0:1, sl], ones128[:, :], xhat[:, kc, sl],
                                     start=(kc == 0), stop=(kc == c.KC - 1))
                            mi = t.matmul(q_ps[0:1, sl], ones128[:, :], x2t[kc % 2][:, sl],
                                          start=(kc == 0), stop=(kc == c.KC - 1))
                        MM.inc(mi)
                        x2_guard[kc % 2] = MM.n

                    def ln_rows(sp, qp, T):
                        v.wait_ge(MM.h, MM.n)
                        v.tensor_scalar_mul(r0[:, 0:T], sp, 1.0 / c.D)
                        v.drain()
                        v.tensor_scalar_mul(r1[:, 0:T], qp, 1.0 / c.D)
                        VEC.inc(v.drain())
                        sc.wait_ge(VEC.h, VEC.n)
                        ACT.inc(sc.activation(r2[:, 0:T], r0[:, 0:T], AF.Square))
                        v.wait_ge(ACT.h, ACT.n)
                        v.tensor_sub(r1[:, 0:T], r1[:, 0:T], r2[:, 0:T])
                        v.drain()
                        v.tensor_scalar_add(r1[:, 0:T], r1[:, 0:T], c.eps)
                        VEC.inc(v.drain())
                        sc.wait_ge(VEC.h, VEC.n)
                        ACT.inc(sc.activation(r2[:, 0:T], r1[:, 0:T], AF.Ln))
                        sc.wait_ge(ACT.h, ACT.n)
                        ACT.inc(sc.activation(r1[:, 0:T], r2[:, 0:T], AF.Exp,
                                              scale=-0.5))
                        v.wait_ge(ACT.h, ACT.n)
                        v.scalar_tensor_tensor(r2[:, 0:T], r0[:, 0:T], -1.0, r1[:, 0:T],
                                               OP.mult, OP.mult)
                        v.drain()
                        v.tensor_copy(a_bf[:, 0:T], r1[:, 0:T])
                        v.drain()
                        v.tensor_copy(b_bf[:, 0:T], r2[:, 0:T])
                        return VEC.inc(v.drain())

                    rows_done = ln_rows(s_ps[0:1, :], q_ps[0:1, :], c.S)

                with ExitStack() as pbc:
                    ab_ps = pbc.enter_context(nc.psum_tensor("ab_ps", [128, 512], F32))
                    bb_ps = pbc.enter_context(nc.psum_tensor("bb_ps", [128, 512], F32))
                    t.wait_ge(VEC.h, rows_done)
                    for tb in range(c.TB):
                        sl = slice(tb * 512, (tb + 1) * 512)
                        t.matmul(ab_ps[:, :], ones_col[:, :], a_bf[0:1, sl],
                                 start=True, stop=True)
                        MM.inc(t.matmul(bb_ps[:, :], ones_col[:, :], b_bf[0:1, sl],
                                        start=True, stop=True))
                        sc.wait_ge(MM.h, MM.n)
                        sc.copy(absb[:, sl], ab_ps[:, :])
                        ACT.inc(sc.copy(bbsb[:, sl], bb_ps[:, :]))
                        t.wait_ge(ACT.h, ACT.n)
                    bcast_done = ACT.n

                v.wait_ge(ACT.h, bcast_done)
                for kc in range(c.KC):
                    v.tensor_mul(ntmp[:, :], xhat[:, kc, :], absb[:, :])
                    v.drain()
                    v.tensor_add(xhat[:, kc, :], ntmp[:, :], bbsb[:, :])
                    VEC.inc(v.drain())
                norm_done = VEC.n

                with ExitStack() as ppq:
                    pq = [ppq.enter_context(nc.psum_tensor(f"pq{i}", [128, 512], F32))
                          for i in range(2 * c.TB)]
                    WQ[0].inc(g.dma_start(out=wslab[0][:, :, :], in_=wqkvT[0]), 16)
                    evac_act = [0] * (2 * c.TB)
                    stg_act = [0, 0]
                    oc_mm = [0, 0]
                    for oc in range(c.OCQKV):
                        if oc + 1 < c.OCQKV:
                            par = (oc + 1) % 2
                            if oc_mm[par]:
                                g.wait_ge(MM.h, oc_mm[par])
                            WQ[par].inc(g.dma_start(out=wslab[par][:, :, :],
                                                    in_=wqkvT[oc + 1]), 16)
                        t.wait_ge(WQ[oc % 2].h, 16 * (oc // 2 + 1))
                        if oc == 0:
                            t.wait_ge(VEC.h, norm_done)
                            t.wait_ge(ACT.h, bqs_done)
                        banks = [(oc % 2) * c.TB + tb for tb in range(c.TB)]
                        for bk in banks:
                            if evac_act[bk]:
                                t.wait_ge(ACT.h, evac_act[bk])
                        for kc in range(c.KC):
                            for tb in range(c.TB):
                                sl = slice(tb * 512, (tb + 1) * 512)
                                mi = t.matmul(pq[banks[tb]][:, :], wslab[oc % 2][:, kc, :],
                                              xhat[:, kc, sl],
                                              start=(kc == 0), stop=(kc == c.KC - 1))
                        MM.inc(mi)
                        oc_mm[oc % 2] = MM.n
                        is_q = oc < c.HC
                        hl = oc % c.HC
                        scr = (q_scr, k_scr, v_scr)[oc // c.HC]
                        sc.wait_ge(MM.h, MM.n)
                        for tb in range(c.TB):
                            sbn = tb % 2
                            if SG[sbn].n:
                                sc.wait_ge(SG[sbn].h, SG[sbn].n)
                            if is_q:
                                si = sc.activation(stg[sbn][:, :], pq[banks[tb]][:, :],
                                                   AF.Identity, bias=bqs_sb[:, oc:oc + 1],
                                                   scale=QSCALE)
                            else:
                                si = sc.activation(stg[sbn][:, :], pq[banks[tb]][:, :],
                                                   AF.Identity, bias=bqkv_sb[:, oc:oc + 1])
                            ACT.inc(si)
                            evac_act[banks[tb]] = ACT.n
                            stg_act[sbn] = ACT.n
                            sy.wait_ge(ACT.h, ACT.n)
                            SG[sbn].inc(sy.dma_start(
                                out=scr[hl, :, tb * 512:(tb + 1) * 512],
                                in_=stg[sbn][:, :]), 16)
                    qkv_mm = MM.n
                    qkv_act = ACT.n
                    qkv_sg = (SG[0].n, SG[1].n)

            fence(SG[0], SG[1], WQ[0], WQ[1], XF, CLD)
            PREP.inc(g.remote_sem_update_broadcast(
                RDY.h, HSL.h, rdests=[(0, 1)] + [None] * 7, sim_remote_sem_inc=2))
            g.wait_ge(PREP.h, PREP.n)
            g.trigger_dma(count=1)
            HSL.n += 16

            with ExitStack() as pA:
                SBA = lambda n_, s_, d_: pA.enter_context(nc.sbuf_tensor(n_, s_, d_))
                attn_own = SBA("attn_own", [128, c.HC, c.TOWN], BF16)
                recv_oth = SBA("recv_oth", [128, c.HC, c.TOWN], BF16)
                attn_send = SBA("attn_send", [128, c.HC, c.TOWN], BF16)

                with ExitStack() as p3:
                    SB3 = lambda n_, s_, d_: p3.enter_context(nc.sbuf_tensor(n_, s_, d_))
                    qhb = [SB3(f"qh{i}", [128, c.S], BF16) for i in range(4)]
                    khb = [SB3(f"kh{i}", [128, c.S], BF16) for i in range(4)]
                    vhb = [SB3(f"vh{i}", [128, c.S], BF16) for i in range(2)]
                    vtb = [SB3(f"vt{i}", [128, c.SC, 128], BF16) for i in range(4)]
                    qkel1 = SB3("qkel1", [128, 1024], BF16)
                    mnegs = [SB3(f"mneg{i}", [1, c.S], BF16) for i in range(2)]
                    nalrow = [SB3(f"nal{i}", [1, c.S], F32) for i in range(2)]
                    PT = 3
                    ptb = [SB3(f"pt{i}", [128, PT, 512], BF16) for i in range(2)]
                    attn_h = [SB3(f"attn_h{i}", [128, c.S], BF16) for i in range(2)]
                    rbsb = [SB3(f"rbsb{i}", [128, 512], F32) for i in range(2)]
                    rr32 = [SB3(f"rr32_{i}", [1, 512], F32) for i in range(2)]

                    with ExitStack() as P3s:
                        PS3 = lambda n_, s_, d_=F32: P3s.enter_context(
                            nc.psum_tensor(n_, s_, d_))
                        vt_ps = PS3("vt_ps", [128, 128], BF16)
                        sps = [PS3(f"sps{i}", [128, 512]) for i in range(3)]
                        sum_ps = [PS3(f"sum_ps{i}", [1, 512]) for i in range(2)]
                        o_ps = [PS3(f"o_ps{i}", [128, 512]) for i in range(2)]

                        own_reg = g.value_load(own_sb[0:1, 0:1], min_val=0, max_val=1)
                        oth_reg = g.value_load(oth_sb[0:1, 0:1], min_val=0, max_val=1)

                        NPAIR = c.HC // 2
                        pair_mm = [0] * (NPAIR + 2)
                        pair_vec = [0] * (NPAIR + 2)
                        dsec_done = [0, 0]
                        recip_vec = [0, 0]
                        recip_act = [0, 0]
                        norm_vec = [0, 0]
                        pb_cnt = [0, 0]
                        send_cpy = [0, 0]
                        head_cpy = [0, 0]
                        sps_act = [0, 0, 0]
                        mstep = [0]
                        vt_queue = []
                        vt_state = {"evac": 0}
                        qkel_state = {"mm": 0}
                        vt_mm = [0, 0]
                        PB = [Sem(top.enter_context(nc.semaphore("PBa"))),
                              Sem(top.enter_context(nc.semaphore("PBb")))]

                        def emit_loads(h):
                            sl4 = h % 4
                            gidx = h // 2
                            if gidx >= 2:
                                sy.wait_ge(MM.h, pair_mm[gidx - 2])
                                sy.wait_ge(VEC.h, pair_vec[gidx - 2])
                            QHs[sl4].inc(sy.dma_start(out=qhb[sl4][:, :],
                                                       in_=q_scr[h]), 16)
                            KHs[sl4].inc(sy.dma_start(out=khb[sl4][:, :],
                                                      in_=k_scr[h]), 16)
                            if vt_mm[h % 2]:
                                sy.wait_ge(MM.h, vt_mm[h % 2])
                            VHs[sl4].inc(sy.dma_start(out=vhb[h % 2][:, :],
                                                      in_=v_scr[h]), 16)
                            if dsec_done[h % 2]:
                                sy.wait_ge(VEC.h, dsec_done[h % 2])
                            NLs[h % 2].inc(sy.dma_start(
                                out=nalrow[h % 2][:, :], in_=nalibi[h:h + 1, :]), 16)

                        ld_cnt = lambda h: 16 * (h // 4 + 1)

                        def emit_vt_one():
                            if not vt_queue:
                                return
                            h, i = vt_queue.pop(0)
                            sl4 = h % 4
                            if i == 0:
                                t.wait_ge(VHs[sl4].h, ld_cnt(h))
                                t.wait_ge(CPI.h, ident_done)
                            if vt_state["evac"]:
                                t.wait_ge(VEC.h, vt_state["evac"])
                            MM.inc(t.transpose(vt_ps[:, :],
                                               vhb[h % 2][:, i * 128:(i + 1) * 128],
                                               identb[:, :]))
                            if i == c.SC - 1:
                                vt_mm[h % 2] = MM.n
                            v.wait_ge(MM.h, MM.n)
                            VEC.inc(v.tensor_copy(vtb[sl4][:, i, :], vt_ps[:, :]))
                            vt_state["evac"] = VEC.n

                        def emit_dsec(h):
                            sl = h % 2
                            sl4 = h % 4
                            gidx = h // 2
                            v.wait_ge(QHs[sl4].h, ld_cnt(h))
                            v.wait_ge(KHs[sl4].h, ld_cnt(h))
                            if gidx >= 1:
                                v.wait_ge(MM.h, pair_mm[gidx - 1])
                            v.wait_ge(NLs[sl].h, 16 * (h // 2 + 1))
                            for tb in range(c.TB):
                                slc = slice(tb * 512, (tb + 1) * 512)
                                hslc = slice((tb % 2) * 512, (tb % 2) * 512 + 512)
                                if qkel_state["mm"]:
                                    v.wait_ge(MM.h, qkel_state["mm"])
                                VEC.inc(v.tensor_mul(qkel1[:, hslc],
                                                     qhb[sl4][:, slc],
                                                     khb[sl4][:, slc]))
                                qkel_vec = VEC.n
                                t.wait_ge(VEC.h, qkel_vec)
                                if tb > 0:
                                    t.wait_ge(VEC.h, VEC.n)
                                if recip_vec[sl]:
                                    t.wait_ge(VEC.h, recip_vec[sl])
                                MM.inc(t.matmul(sum_ps[sl][0:1, :], ones128[:, :],
                                                qkel1[:, hslc],
                                                start=True, stop=True))
                                qkel_state["mm"] = MM.n
                                v.wait_ge(MM.h, MM.n)
                                VEC.inc(v.tensor_sub(mnegs[sl][:, slc],
                                                     nalrow[sl][:, slc],
                                                     sum_ps[sl][0:1, :]))
                            dsec_done[sl] = VEC.n

                        sy.wait_ge(SG[0].h, qkv_sg[0])
                        sy.wait_ge(SG[1].h, qkv_sg[1])
                        emit_loads(0)
                        emit_loads(1)
                        vt_queue.extend((h, i) for h in range(min(2, c.HC))
                                        for i in range(c.SC))
                        for _ in range(len(vt_queue)):
                            emit_vt_one()
                        emit_dsec(0)
                        emit_dsec(1)
                        for h in (2, 3):
                            if h < c.HC:
                                emit_loads(h)

                        for gidx in range(NPAIR):
                            hA, hB = 2 * gidx, 2 * gidx + 1
                            slots4 = (hA % 4, hB % 4)
                            if gidx + 1 < NPAIR:
                                vt_queue.extend(
                                    (h, i) for h in (hA + 2, hB + 2)
                                    for i in range(c.SC))
                            exp_act = [{}, {}]
                            for qb in range(c.TB):
                                nkc = (qb + 1) * 4
                                q0 = qb * 512

                                def inline_ba(sl, kc2, first, last):
                                    voff2 = max(0, kc2 * 128 - q0)
                                    if first and recip_vec[sl]:
                                        t.wait_ge(VEC.h, recip_vec[sl])
                                    if first and norm_vec[sl]:
                                        t.wait_ge(VEC.h, norm_vec[sl])
                                    MM.inc(t.matmul(
                                        sum_ps[sl][0:1, voff2:512], ones128[:, :],
                                        ptb[sl][:, kc2 % PT, voff2:512],
                                        start=first, stop=last))
                                    MM.inc(t.matmul(
                                        o_ps[sl][:, voff2:512],
                                        vtb[slots4[sl]][:, kc2, :],
                                        ptb[sl][:, kc2 % PT, voff2:512],
                                        start=first, stop=last))

                                if qb == 0:
                                    t.wait_ge(VEC.h, max(dsec_done))
                                bkw = (mstep[0] + 1) % 3
                                if sps_act[bkw]:
                                    t.wait_ge(ACT.h, sps_act[bkw])
                                for _ in range(18 if qb == 0 else 12):
                                    t.matmul(sps[bkw][:, :],
                                             identb[:, :],
                                             qhb[slots4[0]][:, 0:512],
                                             start=True, stop=True,
                                             skip_group_check=True)
                                for kc in range(nkc):
                                    voff = max(0, kc * 128 - q0)
                                    for sl in (0, 1):
                                        h = hA + sl
                                        sl4 = slots4[sl]
                                        mstep[0] += 1
                                        bk = mstep[0] % 3
                                        if sps_act[bk]:
                                            t.wait_ge(ACT.h, sps_act[bk])
                                        t.matmul(sps[bk][:, voff:512],
                                                 khb[sl4][:, kc * 128:(kc + 1) * 128],
                                                 qhb[sl4][:, q0 + voff:q0 + 512],
                                                 start=True, stop=False)
                                        MM.inc(t.matmul(
                                            sps[bk][:, voff:512], ones_col[:, :],
                                            mnegs[sl][0:1, q0 + voff:q0 + 512],
                                            start=False, stop=True))
                                        if kc * 128 >= q0:
                                            v.wait_ge(MM.h, MM.n)
                                            VEC.inc(v.tensor_add(
                                                sps[bk][:, voff:voff + 128],
                                                sps[bk][:, voff:voff + 128],
                                                trineg_sb[:, :]))
                                            sc.wait_ge(VEC.h, VEC.n)
                                        else:
                                            sc.wait_ge(MM.h, MM.n)
                                        col = h * c.SC + kc
                                        ACT.inc(sc.activation(
                                            ptb[sl][:, kc % PT, voff:512],
                                            sps[bk][:, voff:512],
                                            AF.Exp, bias=abias_sb[:, col:col + 1]))
                                        sps_act[bk] = ACT.n
                                        exp_act[sl][kc] = ACT.n
                                        if kc >= 2:
                                            kc2 = kc - 2
                                            t.wait_ge(ACT.h, exp_act[sl][kc2])
                                            inline_ba(sl, kc2, kc2 == 0, False)
                                        if qb == c.TB - 1:
                                            emit_vt_one()
                                for sl in (0, 1):
                                    h = hA + sl
                                    for kc2 in range(max(0, nkc - 2), nkc):
                                        t.wait_ge(ACT.h, exp_act[sl][kc2])
                                        inline_ba(sl, kc2, kc2 == 0, kc2 == nkc - 1)
                                    av_mm = MM.n
                                    v.wait_ge(MM.h, av_mm)
                                    if pb_cnt[sl]:
                                        v.wait_ge(PB[sl].h, pb_cnt[sl])
                                    VEC.inc(v.reciprocal(rr32[sl][0:1, :],
                                                         sum_ps[sl][0:1, :]))
                                    recip_vec[sl] = VEC.n
                                    g.wait_ge(VEC.h, recip_vec[sl])
                                    if pb_cnt[sl]:
                                        g.wait_ge(VEC.h, norm_vec[sl])
                                    PB[sl].inc(g.partition_broadcast(
                                        rbsb[sl][:, :], rr32[sl][0:1, :]))
                                    pb_cnt[sl] = PB[sl].n
                                    v.wait_ge(PB[sl].h, pb_cnt[sl])
                                    if gidx >= 1 and qb == 0:
                                        v.wait_ge(SD[sl].h, send_cpy[sl])
                                        v.wait_ge(AO[sl].h, head_cpy[sl])
                                    VEC.inc(v.tensor_mul(
                                        attn_h[sl][:, q0:q0 + 512],
                                        o_ps[sl][:, :], rbsb[sl][:, :]))
                                    norm_vec[sl] = VEC.n

                            pair_mm[gidx] = MM.n
                            pair_vec[gidx] = VEC.n

                            if hA + 2 < c.HC:
                                for _ in range(len(vt_queue)):
                                    emit_vt_one()
                                emit_dsec(hA + 2)
                                emit_dsec(hB + 2)
                            for h in (hA + 4, hB + 4):
                                if h < c.HC:
                                    emit_loads(h)

                            for sl in (0, 1):
                                h = hA + sl
                                g.wait_ge(VEC.h, norm_vec[sl])
                                SD[sl].inc(g.dma_start(
                                    out=attn_send[:, h, :],
                                    in_=attn_h[sl][:, bass.ts(oth_reg, c.TOWN)]), 16)
                                send_cpy[sl] = SD[sl].n
                                AO[sl].inc(g.dma_start(
                                    out=attn_own[:, h, :],
                                    in_=attn_h[sl][:, bass.ts(own_reg, c.TOWN)]), 16)
                                head_cpy[sl] = AO[sl].n

                        g.wait_ge(SD[0].h, SD[0].n)
                        g.wait_ge(SD[1].h, SD[1].n)
                        g.wait_ge(RDY.h, 2)
                        for h in range(c.HC):
                            PREP.inc(g.remote_dma_broadcast(
                                recv_oth[:, h, :], attn_send[:, h, :],
                                XCH.h, XL[0].h,
                                rdests=[(0, 1)] + [None] * 7,
                                sim_remote_sem_inc=2))
                            g.wait_ge(PREP.h, PREP.n)
                            g.trigger_dma(count=1)
                            XL[0].n += 16
                        ao_final = (AO[0].n, AO[1].n)
                        p3_mm = pair_mm[NPAIR - 1]
                        p3_vec = pair_vec[NPAIR - 1]

                fence(AO[0], AO[1], SD[0], SD[1],
                      *QHs, *KHs, *VHs, NLs[0], NLs[1])

                with ExitStack() as p4:
                    SB4 = lambda n_, s_, d_: p4.enter_context(nc.sbuf_tensor(n_, s_, d_))
                    oslab = [SB4(f"oslab{i}", [128, c.KC, 128], BF16) for i in range(2)]
                    xres = [SB4(f"xres{i}", [128, 512], F32) for i in range(2)]
                    t1 = [SB4(f"t1_{i}", [128, 512], F32) for i in range(2)]
                    o1f = [SB4(f"o1f{i}", [128, 512], F32) for i in range(2)]
                    with ExitStack() as po4:
                        po = [po4.enter_context(nc.psum_tensor(f"po{i}", [128, 512], F32))
                              for i in range(4)]
                        OSL[0].inc(g.dma_start(out=oslab[0][:, :, :], in_=woutT[0]), 16)
                        evac_o = [0] * 4
                        o1_vec = [0, 0]
                        xres_vec = [0, 0]
                        oc_mm4 = [0, 0]
                        for oc in range(c.KC):
                            if oc + 1 < c.KC:
                                par = (oc + 1) % 2
                                if oc_mm4[par]:
                                    g.wait_ge(MM.h, oc_mm4[par])
                                OSL[par].inc(g.dma_start(out=oslab[par][:, :, :],
                                                         in_=woutT[oc + 1]), 16)
                            t.wait_ge(OSL[oc % 2].h, 16 * (oc // 2 + 1))
                            if oc == 0:
                                t.wait_ge(AO[0].h, ao_final[0])
                                t.wait_ge(AO[1].h, ao_final[1])
                                t.wait_ge(XCH.h, 2 * c.HC)
                            banks = [(oc % 2) * 2 + tb for tb in range(c.TBO)]
                            for bk in banks:
                                if evac_o[bk]:
                                    t.wait_ge(ACT.h, evac_o[bk])
                            for kc in range(c.KC):
                                src = attn_own if kc < c.HC else recv_oth
                                ksl = kc if kc < c.HC else kc - c.HC
                                for tb in range(c.TBO):
                                    sl = slice(tb * 512, (tb + 1) * 512)
                                    mi = t.matmul(po[banks[tb]][:, :],
                                                  oslab[oc % 2][:, kc, :],
                                                  src[:, ksl, sl],
                                                  start=(kc == 0), stop=(kc == c.KC - 1))
                            MM.inc(mi)
                            oc_mm4[oc % 2] = MM.n
                            mm_now = MM.n
                            for tb in range(c.TBO):
                                if xres_vec[tb]:
                                    sy.wait_ge(VEC.h, xres_vec[tb])
                                XR[tb].inc(sy.dma_start(
                                    out=xres[tb][:, :],
                                    in_=xresT[oc * 128:(oc + 1) * 128,
                                              tb * 512:(tb + 1) * 512]), 16)
                            sc.wait_ge(MM.h, mm_now)
                            for tb in range(c.TBO):
                                if o1_vec[tb]:
                                    sc.wait_ge(VEC.h, o1_vec[tb])
                                si = sc.activation(t1[tb][:, :], po[banks[tb]][:, :],
                                                   AF.Identity, bias=bout_sb[:, oc:oc + 1])
                                ACT.inc(si)
                                evac_o[banks[tb]] = ACT.n
                                v.wait_ge(ACT.h, ACT.n)
                                v.wait_ge(XR[tb].h, XR[tb].n)
                                if o1_vec[tb]:
                                    v.wait_ge(O1S[tb].h, O1S[tb].n)
                                VEC.inc(v.tensor_add(o1f[tb][:, :], t1[tb][:, :],
                                                     xres[tb][:, :]))
                                o1_vec[tb] = VEC.n
                                xres_vec[tb] = VEC.n
                                sy.wait_ge(VEC.h, VEC.n)
                                O1S[tb].inc(sy.dma_start(
                                    out=o1_scr[oc, :, tb * 512:(tb + 1) * 512],
                                    in_=o1f[tb][:, :]), 16)
                        o1s_final = (O1S[0].n, O1S[1].n)

            fence(O1S[0], O1S[1], XR[0], XR[1], OSL[0], OSL[1], XCH,
                  XL[0], XL[1])

            with ExitStack() as p5o:
                SB5 = lambda n_, s_, d_: p5o.enter_context(nc.sbuf_tensor(n_, s_, d_))
                p_sb = SB5("p_sb", [128, c.RC, c.TOWN], BF16)
                xh2s = ExitStack()
                xhat2 = xh2s.enter_context(nc.sbuf_tensor("xhat2", [128, c.KC, c.TOWN], BF16))
                TO = c.TOWN

                with ExitStack() as pl2:
                    SBL = lambda n_, s_, d_: pl2.enter_context(nc.sbuf_tensor(n_, s_, d_))
                    xs = [SBL(f"xs{i}", [128, c.TOWN], BF16) for i in range(2)]
                    x2 = [SBL(f"x2_{i}", [128, c.TOWN], BF16) for i in range(2)]
                    ab2 = SBL("ab2", [128, c.TOWN], BF16)
                    bb2 = SBL("bb2", [128, c.TOWN], BF16)
                    ntm2 = SBL("ntm2", [128, c.TOWN], BF16)
                    r0 = SBL("r0b", [1, c.TOWN], F32)
                    r1 = SBL("r1b", [1, c.TOWN], F32)
                    r2 = SBL("r2b", [1, c.TOWN], F32)
                    a_bf = SBL("a_bf2", [1, c.TOWN], BF16)
                    b_bf = SBL("b_bf2", [1, c.TOWN], BF16)

                    with ExitStack() as pst2:
                        s2 = pst2.enter_context(nc.psum_tensor("s2", [1, c.TOWN], F32))
                        q2 = pst2.enter_context(nc.psum_tensor("q2", [1, c.TOWN], F32))
                        xs_guard = [0, 0]
                        x2_guard = [0, 0]
                        g.wait_ge(O1S[0].h, o1s_final[0])
                        g.wait_ge(O1S[1].h, o1s_final[1])
                        for kc in range(c.KC):
                            par = kc % 2
                            if xs_guard[par]:
                                g.wait_ge(MM.h, xs_guard[par])
                            XS[par].inc(g.dma_start(out=xs[par][:, :], in_=o1_scr[kc]), 16)
                            sc.wait_ge(XS[par].h, XS[par].n)
                            if x2_guard[par]:
                                sc.wait_ge(MM.h, x2_guard[par])
                            ACT.inc(sc.activation(x2[par][:, :], xs[par][:, :], AF.Square))
                            t.wait_ge(ACT.h, ACT.n)
                            for tb in range(c.TBO):
                                sl = slice(tb * 512, (tb + 1) * 512)
                                t.matmul(s2[0:1, sl], ones128[:, :], xs[par][:, sl],
                                         start=(kc == 0), stop=(kc == c.KC - 1))
                                mi = t.matmul(q2[0:1, sl], ones128[:, :], x2[par][:, sl],
                                              start=(kc == 0), stop=(kc == c.KC - 1))
                            MM.inc(mi)
                            xs_guard[par] = MM.n
                            x2_guard[par] = MM.n

                        v.wait_ge(MM.h, MM.n)
                        v.tensor_scalar_mul(r0[:, :], s2[0:1, :], 1.0 / c.D)
                        v.drain()
                        v.tensor_scalar_mul(r1[:, :], q2[0:1, :], 1.0 / c.D)
                        VEC.inc(v.drain())
                        sc.wait_ge(VEC.h, VEC.n)
                        ACT.inc(sc.activation(r2[:, :], r0[:, :], AF.Square))
                        v.wait_ge(ACT.h, ACT.n)
                        v.tensor_sub(r1[:, :], r1[:, :], r2[:, :])
                        v.drain()
                        v.tensor_scalar_add(r1[:, :], r1[:, :], c.eps)
                        VEC.inc(v.drain())
                        sc.wait_ge(VEC.h, VEC.n)
                        ACT.inc(sc.activation(r2[:, :], r1[:, :], AF.Ln))
                        sc.wait_ge(ACT.h, ACT.n)
                        ACT.inc(sc.activation(r1[:, :], r2[:, :], AF.Exp,
                                              scale=-0.5))
                        v.wait_ge(ACT.h, ACT.n)
                        v.scalar_tensor_tensor(r2[:, :], r0[:, :], -1.0, r1[:, :],
                                               OP.mult, OP.mult)
                        v.drain()
                        v.tensor_copy(a_bf[:, :], r1[:, :])
                        v.drain()
                        v.tensor_copy(b_bf[:, :], r2[:, :])
                        rows2_done = VEC.inc(v.drain())

                    with ExitStack() as pbc2:
                        ab_ps = pbc2.enter_context(nc.psum_tensor("ab2p", [128, 512], F32))
                        bb_ps = pbc2.enter_context(nc.psum_tensor("bb2p", [128, 512], F32))
                        t.wait_ge(VEC.h, rows2_done)
                        for tb in range(c.TBO):
                            sl = slice(tb * 512, (tb + 1) * 512)
                            t.matmul(ab_ps[:, :], ones_col[:, :], a_bf[0:1, sl],
                                     start=True, stop=True)
                            MM.inc(t.matmul(bb_ps[:, :], ones_col[:, :], b_bf[0:1, sl],
                                            start=True, stop=True))
                            sc.wait_ge(MM.h, MM.n)
                            sc.copy(ab2[:, sl], ab_ps[:, :])
                            ACT.inc(sc.copy(bb2[:, sl], bb_ps[:, :]))
                            t.wait_ge(ACT.h, ACT.n)
                        bc2_done = ACT.n

                    xs2_guard = [0, 0]
                    g.wait_ge(MM.h, MM.n)
                    g.wait_ge(ACT.h, ACT.n)
                    for kc in range(c.KC):
                        par = kc % 2
                        if xs2_guard[par]:
                            g.wait_ge(VEC.h, xs2_guard[par])
                        XS[par].inc(g.dma_start(out=xs[par][:, :], in_=o1_scr[kc]), 16)
                        v.wait_ge(XS[par].h, XS[par].n)
                        if kc == 0:
                            v.wait_ge(ACT.h, bc2_done)
                        v.tensor_mul(ntm2[:, :], xs[par][:, :], ab2[:, :])
                        v.drain()
                        v.tensor_add(xhat2[:, kc, :], ntm2[:, :], bb2[:, :])
                        VEC.inc(v.drain())
                        xs2_guard[par] = VEC.n
                    xhat2_done = VEC.n

                fence(XS[0], XS[1])

                with ExitStack() as pf:
                    with ExitStack() as pg4:
                        pg = [pg4.enter_context(nc.psum_tensor(f"pg{i}", [128, 512], F32))
                              for i in range(4)]

                        with ExitStack() as pu1:
                            u1s = [pu1.enter_context(
                                nc.sbuf_tensor(f"u1s{i}", [128, c.KC, 128], BF16))
                                for i in range(2)]
                            U1S[0].inc(g.dma_start(out=u1s[0][:, :, :], in_=u1T[0]), 16)
                            pb_act = [0] * 4
                            u1_mm = [0, 0]
                            for oc in range(c.RC):
                                if oc + 1 < c.RC:
                                    par = (oc + 1) % 2
                                    if u1_mm[par]:
                                        g.wait_ge(MM.h, u1_mm[par])
                                    U1S[par].inc(g.dma_start(out=u1s[par][:, :, :],
                                                             in_=u1T[oc + 1]), 16)
                                t.wait_ge(U1S[oc % 2].h, 16 * (oc // 2 + 1))
                                if oc == 0:
                                    t.wait_ge(VEC.h, xhat2_done)
                                banks = [(oc % 2) * 2 + tb for tb in range(c.TBO)]
                                for bk in banks:
                                    if pb_act[bk]:
                                        t.wait_ge(ACT.h, pb_act[bk])
                                for kc in range(c.KC):
                                    for tb in range(c.TBO):
                                        sl = slice(tb * 512, (tb + 1) * 512)
                                        mi = t.matmul(pg[banks[tb]][:, :],
                                                      u1s[oc % 2][:, kc, :],
                                                      xhat2[:, kc, sl],
                                                      start=(kc == 0), stop=(kc == c.KC - 1))
                                MM.inc(mi)
                                u1_mm[oc % 2] = MM.n
                                sc.wait_ge(MM.h, MM.n)
                                for tb in range(c.TBO):
                                    sl = slice(tb * 512, (tb + 1) * 512)
                                    ACT.inc(sc.copy(p_sb[:, oc, sl], pg[banks[tb]][:, :]))
                                    pb_act[banks[tb]] = ACT.n
                            p_done = ACT.n
                            u1_final_mm = MM.n
                        xh2s.close()

                        with ExitStack() as pv1:
                            SBV = lambda n_, s_, d_: pv1.enter_context(
                                nc.sbuf_tensor(n_, s_, d_))
                            v1s = [SBV(f"v1s{i}", [128, c.RC, 128], BF16) for i in range(4)]
                            u2s = [SBV(f"u2s{i}", [128, c.NH, 128], BF16) for i in range(2)]
                            g_act = [SBV(f"g_act{i}", [128, 512], BF16) for i in range(2)]
                            u_act = [SBV(f"u_act{i}", [128, 512], BF16) for i in range(2)]
                            acth = SBV("acth", [128, c.NH, c.TOWN], BF16)
                            z_sb = SBV("z_sb", [128, c.RC, c.TOWN], F32)
                            zb_sb = SBV("zb_sb", [128, c.RC, c.TOWN], BF16)

                            fence(U1S[0], U1S[1])
                            u2_mm_prev = 0
                            z_prev_vec = 0
                            for half in range(2):
                                v1_mm = [0] * 4
                                ga_vec = 0
                                ga_act = [0, 0]
                                for j in range(c.NH):
                                    ic = half * c.NH + j
                                    bg, bu = 2 * (j % 2), 2 * (j % 2) + 1
                                    for buf, col in ((bg, ic), (bu, c.IC + ic)):
                                        if v1_mm[buf]:
                                            g.wait_ge(MM.h, v1_mm[buf])
                                        V1S[buf].inc(g.dma_start(out=v1s[buf][:, :, :],
                                                                 in_=v1T[col]), 16)
                                    t.wait_ge(V1S[bg].h, V1S[bg].n)
                                    t.wait_ge(V1S[bu].h, V1S[bu].n)
                                    if half == 0 and j == 0:
                                        t.wait_ge(ACT.h, p_done)
                                    if half == 1 and j == 0:
                                        t.wait_ge(VEC.h, z_prev_vec)
                                    for tb in range(c.TBO):
                                        pbg, pbu = tb, 2 + tb
                                        if ga_act[tb]:
                                            t.wait_ge(ACT.h, ga_act[tb])
                                        for kc in range(c.RC):
                                            sl = slice(tb * 512, (tb + 1) * 512)
                                            t.matmul(pg[pbg][:, :], v1s[bg][:, kc, :],
                                                     p_sb[:, kc, sl],
                                                     start=(kc == 0), stop=(kc == c.RC - 1))
                                            mi = t.matmul(pg[pbu][:, :], v1s[bu][:, kc, :],
                                                          p_sb[:, kc, sl],
                                                          start=(kc == 0),
                                                          stop=(kc == c.RC - 1))
                                        MM.inc(mi)
                                        v1_mm[bg] = MM.n
                                        v1_mm[bu] = MM.n
                                        sc.wait_ge(MM.h, MM.n)
                                        if ga_vec:
                                            sc.wait_ge(VEC.h, ga_vec)
                                        sc.activation(g_act[tb][:, :], pg[pbg][:, :],
                                                      AF.Silu, bias=b1_sb[:, ic:ic + 1])
                                        ACT.inc(sc.activation(
                                            u_act[tb][:, :], pg[pbu][:, :], AF.Identity,
                                            bias=b1_sb[:, c.IC + ic:c.IC + ic + 1]))
                                        ga_act[tb] = ACT.n
                                        v.wait_ge(ACT.h, ACT.n)
                                        if half == 1 and j == 0 and tb == 0:
                                            v.wait_ge(MM.h, u2_mm_prev)
                                        sl = slice(tb * 512, (tb + 1) * 512)
                                        VEC.inc(v.tensor_mul(acth[:, j, sl],
                                                             g_act[tb][:, :],
                                                             u_act[tb][:, :]))
                                        ga_vec = VEC.n
                                acth_done = VEC.n

                                g.wait_ge(MM.h, MM.n)
                                U2S[0].inc(g.dma_start(
                                    out=u2s[0][:, :, :],
                                    in_=u2T[0][:, half * c.NH:(half + 1) * c.NH, :]), 16)
                                u2_mm = [0, 0]
                                z_vec = [0, 0]
                                u2s_base = (U2S[0].n - 16, U2S[1].n)
                                for oc in range(c.RC):
                                    if oc + 1 < c.RC:
                                        par = (oc + 1) % 2
                                        if u2_mm[par]:
                                            g.wait_ge(MM.h, u2_mm[par])
                                        U2S[par].inc(g.dma_start(
                                            out=u2s[par][:, :, :],
                                            in_=u2T[oc + 1][:, half * c.NH:(half + 1) * c.NH, :]
                                        ), 16)
                                    t.wait_ge(U2S[oc % 2].h, u2s_base[oc % 2] + 16 * (oc // 2 + 1))
                                    if oc == 0:
                                        t.wait_ge(VEC.h, acth_done)
                                        if half == 1:
                                            VEC.inc(v.drain())
                                    for tb in range(c.TBO):
                                        bk = 2 * (oc % 2) + tb
                                        if z_vec[oc % 2]:
                                            t.wait_ge(VEC.h, z_vec[oc % 2])
                                        for kc in range(c.NH):
                                            sl = slice(tb * 512, (tb + 1) * 512)
                                            mi = t.matmul(pg[bk][:, :], u2s[oc % 2][:, kc, :],
                                                          acth[:, kc, sl],
                                                          start=(kc == 0),
                                                          stop=(kc == c.NH - 1))
                                        MM.inc(mi)
                                        u2_mm[oc % 2] = MM.n
                                        v.wait_ge(MM.h, MM.n)
                                        sl = slice(tb * 512, (tb + 1) * 512)
                                        if half == 0:
                                            vi = v.tensor_copy(z_sb[:, oc, sl], pg[bk][:, :])
                                        else:
                                            vi = v.tensor_add(z_sb[:, oc, sl],
                                                              z_sb[:, oc, sl], pg[bk][:, :])
                                        VEC.inc(vi)
                                        z_vec[oc % 2] = VEC.n
                                u2_mm_prev = MM.n
                                z_prev_vec = VEC.n

                            VEC.inc(v.drain())
                            for oc in range(c.RC):
                                v.drain()
                                VEC.inc(v.tensor_copy(zb_sb[:, oc, :], z_sb[:, oc, :]))
                            z_done = VEC.n

                            with ExitStack() as pv2:
                                SBW = lambda n_, s_, d_: pv2.enter_context(
                                    nc.sbuf_tensor(n_, s_, d_))
                                v2s = [SBW(f"v2s{i}", [128, c.RC, 128], BF16)
                                       for i in range(2)]
                                res2 = [SBW(f"res2{i}", [128, 512], F32) for i in range(2)]
                                t2 = [SBW(f"t2_{i}", [128, 512], F32) for i in range(2)]
                                of = [SBW(f"of{i}", [128, 512], F32) for i in range(2)]
                                V2S[0].inc(g.dma_start(out=v2s[0][:, :, :], in_=v2T[0]), 16)
                                ob_act = [0] * 4
                                of_vec = [0, 0]
                                res_vec = [0, 0]
                                v2_mm = [0, 0]
                                for oc in range(c.KC):
                                    if oc + 1 < c.KC:
                                        par = (oc + 1) % 2
                                        if v2_mm[par]:
                                            g.wait_ge(MM.h, v2_mm[par])
                                        V2S[par].inc(g.dma_start(out=v2s[par][:, :, :],
                                                                 in_=v2T[oc + 1]), 16)
                                    t.wait_ge(V2S[oc % 2].h, 16 * (oc // 2 + 1))
                                    if oc == 0:
                                        t.wait_ge(VEC.h, z_done)
                                    banks = [(oc % 2) * 2 + tb for tb in range(c.TBO)]
                                    for bk in banks:
                                        if ob_act[bk]:
                                            t.wait_ge(ACT.h, ob_act[bk])
                                    for kc in range(c.RC):
                                        for tb in range(c.TBO):
                                            sl = slice(tb * 512, (tb + 1) * 512)
                                            mi = t.matmul(pg[banks[tb]][:, :],
                                                          v2s[oc % 2][:, kc, :],
                                                          zb_sb[:, kc, sl],
                                                          start=(kc == 0),
                                                          stop=(kc == c.RC - 1))
                                    MM.inc(mi)
                                    v2_mm[oc % 2] = MM.n
                                    mm_now = MM.n
                                    for tb in range(c.TBO):
                                        if res_vec[tb]:
                                            sy.wait_ge(VEC.h, res_vec[tb])
                                        RS[tb].inc(sy.dma_start(
                                            out=res2[tb][:, :],
                                            in_=o1_scr[oc, :, tb * 512:(tb + 1) * 512]), 16)
                                    sc.wait_ge(MM.h, mm_now)
                                    for tb in range(c.TBO):
                                        if of_vec[tb]:
                                            sc.wait_ge(VEC.h, of_vec[tb])
                                        si = sc.activation(t2[tb][:, :], pg[banks[tb]][:, :],
                                                           AF.Identity,
                                                           bias=b2_sb[:, oc:oc + 1])
                                        ACT.inc(si)
                                        ob_act[banks[tb]] = ACT.n
                                        v.wait_ge(ACT.h, ACT.n)
                                        v.wait_ge(RS[tb].h, RS[tb].n)
                                        if of_vec[tb]:
                                            v.wait_ge(OF[tb].h, OF[tb].n)
                                        VEC.inc(v.tensor_add(of[tb][:, :], t2[tb][:, :],
                                                             res2[tb][:, :]))
                                        of_vec[tb] = VEC.n
                                        res_vec[tb] = VEC.n
                                        sy.wait_ge(VEC.h, VEC.n)
                                        OF[tb].inc(sy.dma_start(
                                            out=outT[oc * 128:(oc + 1) * 128,
                                                     tb * 512:(tb + 1) * 512],
                                            in_=of[tb][:, :]), 16)

            sy.wait_ge(OF[0].h, OF[0].n)
            sy.wait_ge(OF[1].h, OF[1].n)

    return nc



def _tile_w(w):
    K, N = w.shape
    kc, oc = K // 128, N // 128
    t = w.reshape(kc, 128, oc, 128).transpose(2, 1, 0, 3)
    return np.ascontiguousarray(t, dtype=np.float32)


def _alibi_slopes(n_heads, alibi_bias_max=8):
    _n = 2 ** math.ceil(math.log2(n_heads))
    m = np.arange(1, _n + 1, dtype=np.float32) * (alibi_bias_max / _n)
    slopes = 1.0 / np.power(2.0, m)
    if _n != n_heads:
        slopes = np.concatenate([slopes[1::2], slopes[::2]])[:n_heads]
    return slopes.astype(np.float32)


def host_inputs(c: Cfg, x, w_qkv, b_qkv, w_out, b_out, U1, V1, U2, V2, b1, b2):
    f32 = np.float32
    H = 2 * c.HC
    D, S = c.D, c.S
    slopes = _alibi_slopes(H)
    positions = np.arange(1 - S, 1, dtype=f32)

    wqkvT_g = np.ascontiguousarray(np.asarray(w_qkv, f32).T)
    woutT_g = np.ascontiguousarray(np.asarray(w_out, f32).T)
    x = np.asarray(x, f32)
    u1T = _tile_w(np.asarray(U1, f32))
    v1T = _tile_w(np.asarray(V1, f32))
    u2T = _tile_w(np.asarray(U2, f32))
    v2T = _tile_w(np.asarray(V2, f32))
    b1m = np.ascontiguousarray(np.asarray(b1, f32).reshape(2 * c.IC, 128).T)
    b2m = np.ascontiguousarray(np.asarray(b2, f32).reshape(c.KC, 128).T)
    boutm = np.ascontiguousarray(np.asarray(b_out, f32).reshape(c.KC, 128).T)
    tri = np.where(np.arange(128)[None, :] < np.arange(128)[:, None],
                   np.float32(-1e9), np.float32(0.0))
    tri = np.ascontiguousarray(tri, f32)

    in_maps = []
    for core in range(N_CORES):
        b, hh = core // 2, core % 2
        fh = c.HC * 128
        cols = np.r_[hh * fh:(hh + 1) * fh,
                     D + hh * fh:D + (hh + 1) * fh,
                     2 * D + hh * fh:2 * D + (hh + 1) * fh]
        wq_my = wqkvT_g[:, cols]
        bq_my = np.asarray(b_qkv, f32)[cols]
        perm = np.r_[hh * fh:(hh + 1) * fh, (1 - hh) * fh:(2 - hh) * fh]
        wout_my = woutT_g[perm]

        myheads = np.arange(hh * c.HC, (hh + 1) * c.HC)
        ab = np.zeros((128, c.HC * c.SC), f32)
        nal = np.zeros((c.HC, S), f32)
        for i, h in enumerate(myheads):
            nal[i] = -slopes[h] * positions
            for kc in range(c.SC):
                ab[:, i * c.SC + kc] = slopes[h] * positions[kc * 128:(kc + 1) * 128]

        in_maps.append({
            "xT": np.ascontiguousarray(x[b].T, f32),
            "xresT": np.ascontiguousarray(x[b, hh * c.TOWN:(hh + 1) * c.TOWN].T, f32),
            "wqkvT": _tile_w(wq_my),
            "bqkv": np.ascontiguousarray(bq_my.reshape(c.OCQKV, 128).T),
            "woutT": _tile_w(wout_my),
            "bout": boutm,
            "u1T": u1T, "v1T": v1T, "u2T": u2T, "v2T": v2T,
            "b1": b1m, "b2": b2m,
            "abias": ab, "nalibi": nal, "trineg": tri,
            "othoff": np.array([[1 - hh]], np.uint32),
            "ownoff": np.array([[hh]], np.uint32),
        })
    return in_maps


def kernel(L0_attn_x, L0_attn_w_qkv, L0_attn_b_qkv, L0_attn_w_out, L0_attn_b_out,
           L0_ffn_U1, L0_ffn_V1, L0_ffn_U2, L0_ffn_V2, L0_ffn_b1, L0_ffn_b2):
    c = REAL
    in_maps = host_inputs(c, L0_attn_x, L0_attn_w_qkv, L0_attn_b_qkv,
                          L0_attn_w_out, L0_attn_b_out, L0_ffn_U1, L0_ffn_V1,
                          L0_ffn_U2, L0_ffn_V2, L0_ffn_b1, L0_ffn_b2)
    nc = build_program(c)
    nc.finalize()
    res = run_bass_kernel_spmd(nc, in_maps, core_ids=list(range(N_CORES)))
    x = np.asarray(L0_attn_x)
    B, S, D = x.shape
    out = np.empty((B, S, D), np.float32)
    for core in range(N_CORES):
        b, hh = core // 2, core % 2
        piece = np.asarray(res.results[core]["outT"])
        out[b, hh * c.TOWN:(hh + 1) * c.TOWN, :] = piece.T
    return out


# version 22 (final)
# speedup vs baseline: 1.6472x; 1.0646x over previous
import math
from contextlib import ExitStack
from dataclasses import dataclass

import numpy as np

import concourse.bass as bass
import concourse.bacc as bacc
import concourse.mybir as mybir
from concourse.bass_utils import run_bass_kernel_spmd

F32 = mybir.dt.float32
BF16 = mybir.dt.bfloat16
U32 = mybir.dt.uint32
AF = mybir.ActivationFunctionType
OP = mybir.AluOpType

N_CORES = 8
QSCALE = 1.0 / math.sqrt(128.0)


@dataclass(frozen=True)
class Cfg:
    D: int = 4096
    S: int = 2048
    HC: int = 16
    RANK: int = 1024
    INTER: int = 11008
    eps: float = 1e-5

    @property
    def KC(self): return self.D // 128

    @property
    def SC(self): return self.S // 128

    @property
    def TB(self): return self.S // 512

    @property
    def TOWN(self): return self.S // 2

    @property
    def TBO(self): return self.TOWN // 512

    @property
    def RC(self): return self.RANK // 128

    @property
    def IC(self): return self.INTER // 128

    @property
    def NH(self): return self.IC // 2

    @property
    def OCQKV(self): return 3 * self.HC


REAL = Cfg()


class Sem:
    def __init__(self, h):
        self.h = h
        self.n = 0

    def inc(self, instr, amt=1):
        instr.then_inc(self.h, amt)
        self.n += amt
        return self.n


def build_program(c: Cfg):
    nc = bacc.Bacc("TRN2", target_bir_lowering=False)

    din = lambda n, shp, dt=F32: nc.dram_tensor(n, shp, dt, kind="ExternalInput").ap()
    xT = din("xT", [c.D, c.S])
    xresT = din("xresT", [c.D, c.TOWN])
    wqkvT = din("wqkvT", [c.OCQKV, 128, c.KC, 128])
    bqkv = din("bqkv", [128, c.OCQKV])
    woutTA = din("woutTA", [c.KC, 128, c.HC, 128])
    woutTB = din("woutTB", [c.KC, 128, c.HC, 128])
    bout = din("bout", [128, c.KC])
    u1T = din("u1T", [c.RC, 128, c.KC, 128])
    v1T = din("v1T", [2 * c.IC, 128, c.RC, 128])
    u2T = din("u2T", [c.RC, 128, c.IC, 128])
    v2T = din("v2T", [c.KC, 128, c.RC, 128])
    b1 = din("b1", [128, 2 * c.IC])
    b2 = din("b2", [128, c.KC])
    abias = din("abias", [128, c.HC * c.SC])
    nalibi = din("nalibi", [c.HC, c.S])
    trineg = din("trineg", [128, 128])
    othoff = din("othoff", [1, 1], U32)
    ownoff = din("ownoff", [1, 1], U32)

    outT = nc.dram_tensor("outT", [c.D, c.TOWN], F32, kind="ExternalOutput").ap()

    q_scr = nc.dram_tensor("q_scr", [c.HC, 128, c.S], BF16, kind="Internal").ap()
    k_scr = nc.dram_tensor("k_scr", [c.HC, 128, c.S], BF16, kind="Internal").ap()
    v_scr = nc.dram_tensor("v_scr", [c.HC, 128, c.S], BF16, kind="Internal").ap()
    o1_scr = nc.dram_tensor("o1_scr", [c.KC, 128, c.TOWN], F32, kind="Internal").ap()
    o1p_scr = nc.dram_tensor("o1p_scr", [c.KC, 128, c.TOWN], F32, kind="Internal").ap()

    with ExitStack() as top:
        sem = lambda name: Sem(top.enter_context(nc.semaphore(name)))
        sem2 = lambda name: [sem(name + "a"), sem(name + "b")]
        MM = sem("MM"); ACT = sem("ACT"); VEC = sem("VEC"); CPI = sem("CPI")
        CLD = sem("CLD")
        XF = sem("XF")
        WQ = sem2("WQ")
        SG = sem2("SG")
        QHs = sem2("QHa") + sem2("QHb")
        KHs = sem2("KHa") + sem2("KHb")
        VHs = sem2("VHa") + sem2("VHb")
        NLs = sem2("NL")
        SD = sem2("SD")
        AO = sem2("AO")
        AOTH = sem("AOTH")
        XL = sem2("XL")
        HSL = sem("HSL")
        RDY = sem("RDY")
        XCH = sem("XCH")
        PREP = sem("PREP")
        OSL = sem2("OSL")
        XR = sem2("XR")
        O1S = sem2("O1S")
        XS = sem2("XS")
        U1S = sem2("U1S")
        V1S = sem2("V1Sa") + sem2("V1Sb")
        U2S = sem2("U2S")
        V2S = sem2("V2S")
        RS = sem2("RS")
        OF = sem2("OF")

        SB = lambda n_, s_, d_: top.enter_context(nc.sbuf_tensor(n_, s_, d_))
        ones_col = SB("ones_col", [1, 128], BF16)
        ones128 = SB("ones128", [128, 1], BF16)
        identb = SB("identb", [128, 128], BF16)
        trineg_sb = SB("trineg_sb", [128, 128], F32)
        abias_sb = SB("abias_sb", [128, c.HC * c.SC], F32)
        bqkv_sb = SB("bqkv_sb", [128, c.OCQKV], F32)
        bqs_sb = SB("bqs_sb", [128, c.HC], F32)
        bout_sb = SB("bout_sb", [128, c.KC], F32)
        b1_sb = SB("b1_sb", [128, 2 * c.IC], F32)
        b2_sb = SB("b2_sb", [128, c.KC], F32)
        own_sb = SB("own_sb", [1, 1], U32)
        oth_sb = SB("oth_sb", [1, 1], U32)

        with nc.Block():
            g, sy, t, sc, v = nc.gpsimd, nc.sync, nc.tensor, nc.scalar, nc.vector

            def fence(*dma_sems):
                for eng in (g, sy, t, sc, v):
                    eng.wait_ge(MM.h, MM.n)
                    eng.wait_ge(ACT.h, ACT.n)
                    eng.wait_ge(VEC.h, VEC.n)
                    for sm in dma_sems:
                        if sm.n:
                            eng.wait_ge(sm.h, sm.n)

            for dst, src in ((trineg_sb[:, :], trineg), (abias_sb[:, :], abias),
                             (bqkv_sb[:, :], bqkv), (bout_sb[:, :], bout),
                             (b1_sb[:, :], b1), (b2_sb[:, :], b2),
                             (own_sb[:, :], ownoff), (oth_sb[:, :], othoff)):
                CLD.inc(g.dma_start(out=dst, in_=src), 16)
            v.memset(ones_col[:, :], 1.0)
            v.memset(ones128[:, :], 1.0)
            ones_done = VEC.inc(v.drain())
            CPI.inc(g.memset(identb[:, :], 0.0))
            g.wait_ge(CPI.h, CPI.n)
            ident_done = CPI.inc(g.affine_select(
                out=identb[:, :], in_=identb[:, :], compare_op=OP.not_equal,
                fill=1.0, base=0, pattern=[[-1, 128]], channel_multiplier=1))
            sc.wait_ge(CLD.h, CLD.n)
            bqs_done = ACT.inc(sc.mul(bqs_sb[:, :], bqkv_sb[:, 0:c.HC], QSCALE))

            with ExitStack() as p12:
                SB2 = lambda n_, s_, d_: p12.enter_context(nc.sbuf_tensor(n_, s_, d_))
                xhat = SB2("xhat", [128, c.KC, c.S], BF16)
                x2t = [SB2(f"x2t{i}", [128, c.S], BF16) for i in range(2)]
                ntmp = SB2("ntmp", [128, c.S], BF16)
                absb = SB2("absb", [128, c.S], BF16)
                bbsb = SB2("bbsb", [128, c.S], BF16)
                wslab = [SB2(f"wslab{i}", [128, c.KC, 128], BF16) for i in range(2)]
                stg = [SB2(f"stg{i}", [128, 512], BF16) for i in range(2)]
                r0 = SB2("r0", [1, c.S], F32)
                r1 = SB2("r1", [1, c.S], F32)
                r2 = SB2("r2", [1, c.S], F32)
                a_bf = SB2("a_bf", [1, c.S], BF16)
                b_bf = SB2("b_bf", [1, c.S], BF16)

                for kc in range(c.KC):
                    XF.inc(g.dma_start(out=xhat[:, kc, :],
                                       in_=xT[kc * 128:(kc + 1) * 128, :]), 16)
                x_ld = XF.n

                with ExitStack() as pst:
                    s_ps = pst.enter_context(nc.psum_tensor("s_ps", [1, c.S], F32))
                    q_ps = pst.enter_context(nc.psum_tensor("q_ps", [1, c.S], F32))
                    x2_guard = [0, 0]
                    sc.wait_ge(XF.h, x_ld)
                    for kc in range(c.KC):
                        if x2_guard[kc % 2]:
                            sc.wait_ge(MM.h, x2_guard[kc % 2])
                        ACT.inc(sc.activation(x2t[kc % 2][:, :], xhat[:, kc, :],
                                              AF.Square))
                        t.wait_ge(ACT.h, ACT.n)
                        if kc == 0:
                            t.wait_ge(VEC.h, ones_done)
                        for tb in range(c.TB):
                            sl = slice(tb * 512, (tb + 1) * 512)
                            t.matmul(s_ps[0:1, sl], ones128[:, :], xhat[:, kc, sl],
                                     start=(kc == 0), stop=(kc == c.KC - 1))
                            mi = t.matmul(q_ps[0:1, sl], ones128[:, :], x2t[kc % 2][:, sl],
                                          start=(kc == 0), stop=(kc == c.KC - 1))
                        MM.inc(mi)
                        x2_guard[kc % 2] = MM.n

                    def ln_rows(sp, qp, T):
                        v.wait_ge(MM.h, MM.n)
                        v.tensor_scalar_mul(r0[:, 0:T], sp, 1.0 / c.D)
                        v.drain()
                        v.tensor_scalar_mul(r1[:, 0:T], qp, 1.0 / c.D)
                        VEC.inc(v.drain())
                        sc.wait_ge(VEC.h, VEC.n)
                        ACT.inc(sc.activation(r2[:, 0:T], r0[:, 0:T], AF.Square))
                        v.wait_ge(ACT.h, ACT.n)
                        v.tensor_sub(r1[:, 0:T], r1[:, 0:T], r2[:, 0:T])
                        v.drain()
                        v.tensor_scalar_add(r1[:, 0:T], r1[:, 0:T], c.eps)
                        VEC.inc(v.drain())
                        sc.wait_ge(VEC.h, VEC.n)
                        ACT.inc(sc.activation(r2[:, 0:T], r1[:, 0:T], AF.Ln))
                        sc.wait_ge(ACT.h, ACT.n)
                        ACT.inc(sc.activation(r1[:, 0:T], r2[:, 0:T], AF.Exp,
                                              scale=-0.5))
                        v.wait_ge(ACT.h, ACT.n)
                        v.scalar_tensor_tensor(r2[:, 0:T], r0[:, 0:T], -1.0, r1[:, 0:T],
                                               OP.mult, OP.mult)
                        v.drain()
                        v.tensor_copy(a_bf[:, 0:T], r1[:, 0:T])
                        v.drain()
                        v.tensor_copy(b_bf[:, 0:T], r2[:, 0:T])
                        return VEC.inc(v.drain())

                    rows_done = ln_rows(s_ps[0:1, :], q_ps[0:1, :], c.S)

                with ExitStack() as pbc:
                    ab_ps = pbc.enter_context(nc.psum_tensor("ab_ps", [128, 512], F32))
                    bb_ps = pbc.enter_context(nc.psum_tensor("bb_ps", [128, 512], F32))
                    t.wait_ge(VEC.h, rows_done)
                    for tb in range(c.TB):
                        sl = slice(tb * 512, (tb + 1) * 512)
                        t.matmul(ab_ps[:, :], ones_col[:, :], a_bf[0:1, sl],
                                 start=True, stop=True)
                        MM.inc(t.matmul(bb_ps[:, :], ones_col[:, :], b_bf[0:1, sl],
                                        start=True, stop=True))
                        sc.wait_ge(MM.h, MM.n)
                        sc.copy(absb[:, sl], ab_ps[:, :])
                        ACT.inc(sc.copy(bbsb[:, sl], bb_ps[:, :]))
                        t.wait_ge(ACT.h, ACT.n)
                    bcast_done = ACT.n

                v.wait_ge(ACT.h, bcast_done)
                for kc in range(c.KC):
                    v.tensor_mul(ntmp[:, :], xhat[:, kc, :], absb[:, :])
                    v.drain()
                    v.tensor_add(xhat[:, kc, :], ntmp[:, :], bbsb[:, :])
                    VEC.inc(v.drain())
                norm_done = VEC.n

                with ExitStack() as ppq:
                    pq = [ppq.enter_context(nc.psum_tensor(f"pq{i}", [128, 512], F32))
                          for i in range(2 * c.TB)]
                    WQ[0].inc(g.dma_start(out=wslab[0][:, :, :], in_=wqkvT[0]), 16)
                    evac_act = [0] * (2 * c.TB)
                    stg_act = [0, 0]
                    oc_mm = [0, 0]
                    for oc in range(c.OCQKV):
                        if oc + 1 < c.OCQKV:
                            par = (oc + 1) % 2
                            if oc_mm[par]:
                                g.wait_ge(MM.h, oc_mm[par])
                            WQ[par].inc(g.dma_start(out=wslab[par][:, :, :],
                                                    in_=wqkvT[oc + 1]), 16)
                        t.wait_ge(WQ[oc % 2].h, 16 * (oc // 2 + 1))
                        if oc == 0:
                            t.wait_ge(VEC.h, norm_done)
                            t.wait_ge(ACT.h, bqs_done)
                        banks = [(oc % 2) * c.TB + tb for tb in range(c.TB)]
                        for bk in banks:
                            if evac_act[bk]:
                                t.wait_ge(ACT.h, evac_act[bk])
                        for kc in range(c.KC):
                            for tb in range(c.TB):
                                sl = slice(tb * 512, (tb + 1) * 512)
                                mi = t.matmul(pq[banks[tb]][:, :], wslab[oc % 2][:, kc, :],
                                              xhat[:, kc, sl],
                                              start=(kc == 0), stop=(kc == c.KC - 1))
                        MM.inc(mi)
                        oc_mm[oc % 2] = MM.n
                        is_q = oc < c.HC
                        hl = oc % c.HC
                        scr = (q_scr, k_scr, v_scr)[oc // c.HC]
                        sc.wait_ge(MM.h, MM.n)
                        for tb in range(c.TB):
                            sbn = tb % 2
                            if SG[sbn].n:
                                sc.wait_ge(SG[sbn].h, SG[sbn].n)
                            if is_q:
                                si = sc.activation(stg[sbn][:, :], pq[banks[tb]][:, :],
                                                   AF.Identity, bias=bqs_sb[:, oc:oc + 1],
                                                   scale=QSCALE)
                            else:
                                si = sc.activation(stg[sbn][:, :], pq[banks[tb]][:, :],
                                                   AF.Identity, bias=bqkv_sb[:, oc:oc + 1])
                            ACT.inc(si)
                            evac_act[banks[tb]] = ACT.n
                            stg_act[sbn] = ACT.n
                            sy.wait_ge(ACT.h, ACT.n)
                            SG[sbn].inc(sy.dma_start(
                                out=scr[hl, :, tb * 512:(tb + 1) * 512],
                                in_=stg[sbn][:, :]), 16)
                    qkv_mm = MM.n
                    qkv_act = ACT.n
                    qkv_sg = (SG[0].n, SG[1].n)

            fence(SG[0], SG[1], WQ[0], WQ[1], XF, CLD)
            PREP.inc(g.remote_sem_update_broadcast(
                RDY.h, HSL.h, rdests=[(0, 1)] + [None] * 7, sim_remote_sem_inc=2))
            g.wait_ge(PREP.h, PREP.n)
            g.trigger_dma(count=1)
            HSL.n += 16

            with ExitStack() as pA:
                SBA = lambda n_, s_, d_: pA.enter_context(nc.sbuf_tensor(n_, s_, d_))
                attn_own = SBA("attn_own", [128, c.HC, c.TOWN], BF16)
                recv_oth = SBA("recv_oth", [128, c.HC, c.TOWN], BF16)
                attn_send = SBA("attn_send", [128, c.HC, c.TOWN], BF16)

                with ExitStack() as p3:
                    SB3 = lambda n_, s_, d_: p3.enter_context(nc.sbuf_tensor(n_, s_, d_))
                    qhb = [SB3(f"qh{i}", [128, c.S], BF16) for i in range(4)]
                    khb = [SB3(f"kh{i}", [128, c.S], BF16) for i in range(4)]
                    vhb = [SB3(f"vh{i}", [128, c.S], BF16) for i in range(2)]
                    vtb = [SB3(f"vt{i}", [128, c.SC, 128], BF16) for i in range(4)]
                    qkel1 = SB3("qkel1", [128, 1024], BF16)
                    mnegs = [SB3(f"mneg{i}", [1, c.S], BF16) for i in range(4)]
                    nalb = [SB3(f"nalb{i}", [1, c.S], BF16) for i in range(2)]
                    PT = 3
                    ptb = [SB3(f"pt{i}", [128, PT, 512], BF16) for i in range(2)]
                    attn_h = [SB3(f"attn_h{i}", [128, c.S], BF16) for i in range(2)]
                    rbsb = [SB3(f"rbsb{i}", [128, 512], F32) for i in range(2)]
                    rr32 = [SB3("rr32s", [1, 512], F32)] * 2

                    with ExitStack() as P3s:
                        PS3 = lambda n_, s_, d_=F32: P3s.enter_context(
                            nc.psum_tensor(n_, s_, d_))
                        vt_ps = PS3("vt_ps", [128, 128], BF16)
                        sps = [PS3(f"sps{i}", [128, 512]) for i in range(3)]
                        sum_ps = [PS3(f"sum_ps{i}", [1, 512]) for i in range(2)]
                        o_ps = [PS3(f"o_ps{i}", [128, 512]) for i in range(2)]

                        own_reg = g.value_load(own_sb[0:1, 0:1], min_val=0, max_val=1)
                        oth_reg = g.value_load(oth_sb[0:1, 0:1], min_val=0, max_val=1)

                        NPAIR = c.HC // 2
                        pair_mm = [0] * (NPAIR + 2)
                        pair_vec = [0] * (NPAIR + 2)
                        dsec_done = [0, 0]
                        dsec_act = [0, 0]
                        mneg_act = [0, 0]
                        recip_vec = [0, 0]
                        recip_act = [0, 0]
                        norm_vec = [0, 0]
                        pb_cnt = [0, 0]
                        send_cpy = [0, 0]
                        head_cpy = [0, 0]
                        sps_act = [0, 0, 0]
                        mstep = [0]
                        vt_queue = []
                        vt_state = {"evac": 0}
                        qkel_state = {"mm": 0}
                        vt_mm = [0, 0]
                        PB = [Sem(top.enter_context(nc.semaphore("PBa"))),
                              Sem(top.enter_context(nc.semaphore("PBb")))]

                        def emit_loads(h):
                            sl4 = h % 4
                            gidx = h // 2
                            if gidx >= 2:
                                sy.wait_ge(MM.h, pair_mm[gidx - 2])
                                sy.wait_ge(VEC.h, pair_vec[gidx - 2])
                            QHs[sl4].inc(sy.dma_start(out=qhb[sl4][:, :],
                                                       in_=q_scr[h]), 16)
                            KHs[sl4].inc(sy.dma_start(out=khb[sl4][:, :],
                                                      in_=k_scr[h]), 16)
                            if vt_mm[h % 2]:
                                sy.wait_ge(MM.h, vt_mm[h % 2])
                            VHs[sl4].inc(sy.dma_start(out=vhb[h % 2][:, :],
                                                      in_=v_scr[h]), 16)
                            if dsec_done[h % 2]:
                                g.wait_ge(ACT.h, mneg_act[h % 2])
                            NLs[h % 2].inc(g.dma_start(
                                out=nalb[h % 2][:, :], in_=nalibi[h:h + 1, :]), 16)

                        ld_cnt = lambda h: 16 * (h // 4 + 1)

                        def emit_vt_one():
                            if not vt_queue:
                                return
                            h, i = vt_queue.pop(0)
                            sl4 = h % 4
                            if i == 0:
                                t.wait_ge(VHs[sl4].h, ld_cnt(h))
                                t.wait_ge(CPI.h, ident_done)
                            if vt_state["evac"]:
                                t.wait_ge(VEC.h, vt_state["evac"])
                            MM.inc(t.transpose(vt_ps[:, :],
                                               vhb[h % 2][:, i * 128:(i + 1) * 128],
                                               identb[:, :]))
                            if i == c.SC - 1:
                                vt_mm[h % 2] = MM.n
                            v.wait_ge(MM.h, MM.n)
                            VEC.inc(v.tensor_copy(vtb[sl4][:, i, :], vt_ps[:, :]))
                            vt_state["evac"] = VEC.n

                        def emit_dsec(h):
                            sl = h % 2
                            sl4 = h % 4
                            gidx = h // 2
                            v.wait_ge(QHs[sl4].h, ld_cnt(h))
                            v.wait_ge(KHs[sl4].h, ld_cnt(h))
                            if gidx >= 2:
                                v.wait_ge(MM.h, pair_mm[gidx - 2])
                            for tb in range(c.TB):
                                slc = slice(tb * 512, (tb + 1) * 512)
                                hslc = slice((tb % 2) * 512, (tb % 2) * 512 + 512)
                                if qkel_state["mm"]:
                                    v.wait_ge(MM.h, qkel_state["mm"])
                                VEC.inc(v.scalar_tensor_tensor(
                                    qkel1[:, hslc], qhb[sl4][:, slc], -1.0,
                                    khb[sl4][:, slc], OP.mult, OP.mult))
                                qkel_vec = VEC.n
                                t.wait_ge(VEC.h, qkel_vec)
                                t.wait_ge(NLs[sl].h, 16 * (h // 2 + 1))
                                if tb > 0:
                                    t.wait_ge(ACT.h, mneg_act[sl])
                                if recip_vec[sl]:
                                    t.wait_ge(VEC.h, recip_vec[sl])
                                t.matmul(sum_ps[sl][0:1, :], ones_col[0:1, 0:1],
                                         nalb[sl][0:1, slc],
                                         start=True, stop=False)
                                MM.inc(t.matmul(sum_ps[sl][0:1, :], ones128[:, :],
                                                qkel1[:, hslc],
                                                start=False, stop=True))
                                qkel_state["mm"] = MM.n
                                sc.wait_ge(MM.h, MM.n)
                                ACT.inc(sc.copy(mnegs[sl4][0:1, slc],
                                                sum_ps[sl][0:1, :]))
                                mneg_act[sl] = ACT.n
                            dsec_done[sl] = VEC.n
                            dsec_act[sl] = mneg_act[sl]

                        sy.wait_ge(SG[0].h, qkv_sg[0])
                        sy.wait_ge(SG[1].h, qkv_sg[1])
                        emit_loads(0)
                        emit_loads(1)
                        vt_queue.extend((h, i) for h in range(min(2, c.HC))
                                        for i in range(c.SC))
                        for _ in range(len(vt_queue)):
                            emit_vt_one()
                        emit_dsec(0)
                        emit_dsec(1)
                        for h in (2, 3):
                            if h < c.HC:
                                emit_loads(h)

                        for gidx in range(NPAIR):
                            hA, hB = 2 * gidx, 2 * gidx + 1
                            slots4 = (hA % 4, hB % 4)
                            if gidx + 1 < NPAIR:
                                vt_queue.extend(
                                    (h, i) for h in (hA + 2, hB + 2)
                                    for i in range(c.SC))
                            exp_act = [{}, {}]
                            for qb in range(c.TB):
                                nkc = (qb + 1) * 4
                                q0 = qb * 512

                                def inline_ba(sl, kc2, first, last):
                                    voff2 = max(0, kc2 * 128 - q0)
                                    if first and recip_vec[sl]:
                                        t.wait_ge(VEC.h, recip_vec[sl])
                                    if first and norm_vec[sl]:
                                        t.wait_ge(VEC.h, norm_vec[sl])
                                    if first and dsec_act[sl]:
                                        t.wait_ge(ACT.h, dsec_act[sl])
                                    MM.inc(t.matmul(
                                        sum_ps[sl][0:1, voff2:512], ones128[:, :],
                                        ptb[sl][:, kc2 % PT, voff2:512],
                                        start=first, stop=last))
                                    MM.inc(t.matmul(
                                        o_ps[sl][:, voff2:512],
                                        vtb[slots4[sl]][:, kc2, :],
                                        ptb[sl][:, kc2 % PT, voff2:512],
                                        start=first, stop=last))

                                if qb == 0:
                                    t.wait_ge(VEC.h, max(dsec_done))
                                    t.wait_ge(ACT.h, max(dsec_act))
                                bkw = (mstep[0] + 1) % 3
                                if sps_act[bkw]:
                                    t.wait_ge(ACT.h, sps_act[bkw])
                                for _ in range(18 if qb == 0 else 12):
                                    t.matmul(sps[bkw][:, :],
                                             identb[:, :],
                                             qhb[slots4[0]][:, 0:512],
                                             start=True, stop=True,
                                             skip_group_check=True)
                                for kc in range(nkc):
                                    voff = max(0, kc * 128 - q0)
                                    for sl in (0, 1):
                                        h = hA + sl
                                        sl4 = slots4[sl]
                                        mstep[0] += 1
                                        bk = mstep[0] % 3
                                        if sps_act[bk]:
                                            t.wait_ge(ACT.h, sps_act[bk])
                                        t.matmul(sps[bk][:, voff:512],
                                                 khb[sl4][:, kc * 128:(kc + 1) * 128],
                                                 qhb[sl4][:, q0 + voff:q0 + 512],
                                                 start=True, stop=False)
                                        MM.inc(t.matmul(
                                            sps[bk][:, voff:512], ones_col[:, :],
                                            mnegs[sl4][0:1, q0 + voff:q0 + 512],
                                            start=False, stop=True))
                                        if kc * 128 >= q0:
                                            v.wait_ge(MM.h, MM.n)
                                            VEC.inc(v.tensor_add(
                                                sps[bk][:, voff:voff + 128],
                                                sps[bk][:, voff:voff + 128],
                                                trineg_sb[:, :]))
                                            sc.wait_ge(VEC.h, VEC.n)
                                        else:
                                            sc.wait_ge(MM.h, MM.n)
                                        col = h * c.SC + kc
                                        ACT.inc(sc.activation(
                                            ptb[sl][:, kc % PT, voff:512],
                                            sps[bk][:, voff:512],
                                            AF.Exp, bias=abias_sb[:, col:col + 1]))
                                        sps_act[bk] = ACT.n
                                        exp_act[sl][kc] = ACT.n
                                        if kc >= 2:
                                            kc2 = kc - 2
                                            t.wait_ge(ACT.h, exp_act[sl][kc2])
                                            inline_ba(sl, kc2, kc2 == 0, False)
                                        if qb == c.TB - 1:
                                            emit_vt_one()
                                    if nkc >= 8 and kc == nkc // 2:
                                        bkw2 = (mstep[0] + 1) % 3
                                        if sps_act[bkw2]:
                                            t.wait_ge(ACT.h, sps_act[bkw2])
                                        for _ in range(10):
                                            t.matmul(sps[bkw2][:, :], identb[:, :],
                                                     qhb[slots4[0]][:, 0:512],
                                                     start=True, stop=True,
                                                     skip_group_check=True)
                                for sl in (0, 1):
                                    h = hA + sl
                                    for kc2 in range(max(0, nkc - 2), nkc):
                                        t.wait_ge(ACT.h, exp_act[sl][kc2])
                                        inline_ba(sl, kc2, kc2 == 0, kc2 == nkc - 1)
                                    av_mm = MM.n
                                    v.wait_ge(MM.h, av_mm)
                                    if pb_cnt[sl]:
                                        v.wait_ge(PB[sl].h, pb_cnt[sl])
                                    VEC.inc(v.reciprocal(rr32[sl][0:1, :],
                                                         sum_ps[sl][0:1, :]))
                                    recip_vec[sl] = VEC.n
                                    g.wait_ge(VEC.h, recip_vec[sl])
                                    if pb_cnt[sl]:
                                        g.wait_ge(VEC.h, norm_vec[sl])
                                    PB[sl].inc(g.partition_broadcast(
                                        rbsb[sl][:, :], rr32[sl][0:1, :]))
                                    pb_cnt[sl] = PB[sl].n
                                    v.wait_ge(PB[sl].h, pb_cnt[sl])
                                    if gidx >= 1 and qb == 0:
                                        v.wait_ge(SD[sl].h, send_cpy[sl])
                                        v.wait_ge(AO[sl].h, head_cpy[sl])
                                    VEC.inc(v.tensor_mul(
                                        attn_h[sl][:, q0:q0 + 512],
                                        o_ps[sl][:, :], rbsb[sl][:, :]))
                                    norm_vec[sl] = VEC.n
                                if qb == 0 and hA + 2 < c.HC:
                                    emit_dsec(hA + 2)
                                    emit_dsec(hB + 2)

                            pair_mm[gidx] = MM.n
                            pair_vec[gidx] = VEC.n

                            if hA + 2 < c.HC:
                                for _ in range(len(vt_queue)):
                                    emit_vt_one()
                            for h in (hA + 4, hB + 4):
                                if h < c.HC:
                                    emit_loads(h)

                            for sl in (0, 1):
                                h = hA + sl
                                g.wait_ge(VEC.h, norm_vec[sl])
                                SD[sl].inc(g.dma_start(
                                    out=attn_send[:, h, :],
                                    in_=attn_h[sl][:, bass.ts(oth_reg, c.TOWN)]), 16)
                                send_cpy[sl] = SD[sl].n
                                AO[sl].inc(g.dma_start(
                                    out=attn_own[:, h, :],
                                    in_=attn_h[sl][:, bass.ts(own_reg, c.TOWN)]), 16)
                                head_cpy[sl] = AO[sl].n

                        g.wait_ge(SD[0].h, SD[0].n)
                        g.wait_ge(SD[1].h, SD[1].n)
                        g.wait_ge(RDY.h, 2)
                        for h in range(c.HC):
                            PREP.inc(g.remote_dma_broadcast(
                                recv_oth[:, h, :], attn_send[:, h, :],
                                XCH.h, XL[0].h,
                                rdests=[(0, 1)] + [None] * 7,
                                sim_remote_sem_inc=2))
                            g.wait_ge(PREP.h, PREP.n)
                            g.trigger_dma(count=1)
                            XL[0].n += 16
                        ao_final = (AO[0].n, AO[1].n)
                        p3_mm = pair_mm[NPAIR - 1]
                        p3_vec = pair_vec[NPAIR - 1]

                fence(AO[0], AO[1], SD[0], SD[1],
                      *QHs, *KHs, *VHs, NLs[0], NLs[1])

                with ExitStack() as p4:
                    SB4 = lambda n_, s_, d_: p4.enter_context(nc.sbuf_tensor(n_, s_, d_))
                    oslab = [SB4(f"oslab{i}", [128, c.HC, 128], BF16) for i in range(2)]
                    xres = [SB4(f"xres{i}", [128, 512], F32) for i in range(2)]
                    t1 = [SB4(f"t1_{i}", [128, 512], F32) for i in range(2)]
                    o1p = [SB4(f"o1p{i}", [128, 512], F32) for i in range(2)]
                    o1f = [SB4(f"o1f{i}", [128, 512], F32) for i in range(2)]
                    with ExitStack() as po4:
                        po = [po4.enter_context(nc.psum_tensor(f"po{i}", [128, 512], F32))
                              for i in range(4)]

                        OSL[0].inc(g.dma_start(out=oslab[0][:, :, :], in_=woutTA[0]), 16)
                        evac_o = [0] * 4
                        pa_vec = [0, 0]
                        oc_mm4 = [0, 0]
                        for oc in range(c.KC):
                            if oc + 1 < c.KC:
                                par = (oc + 1) % 2
                                if oc_mm4[par]:
                                    g.wait_ge(MM.h, oc_mm4[par])
                                OSL[par].inc(g.dma_start(out=oslab[par][:, :, :],
                                                         in_=woutTA[oc + 1]), 16)
                            t.wait_ge(OSL[oc % 2].h, 16 * (oc // 2 + 1))
                            if oc == 0:
                                t.wait_ge(AO[0].h, ao_final[0])
                                t.wait_ge(AO[1].h, ao_final[1])
                            banks = [(oc % 2) * 2 + tb for tb in range(c.TBO)]
                            for bk in banks:
                                if evac_o[bk]:
                                    t.wait_ge(ACT.h, evac_o[bk])
                            for kc in range(c.HC):
                                for tb in range(c.TBO):
                                    sl = slice(tb * 512, (tb + 1) * 512)
                                    mi = t.matmul(po[banks[tb]][:, :],
                                                  oslab[oc % 2][:, kc, :],
                                                  attn_own[:, kc, sl],
                                                  start=(kc == 0), stop=(kc == c.HC - 1))
                            MM.inc(mi)
                            oc_mm4[oc % 2] = MM.n
                            sc.wait_ge(MM.h, MM.n)
                            for tb in range(c.TBO):
                                if pa_vec[tb]:
                                    sc.wait_ge(O1S[tb].h, O1S[tb].n)
                                si = sc.copy(t1[tb][:, :], po[banks[tb]][:, :])
                                ACT.inc(si)
                                evac_o[banks[tb]] = ACT.n
                                pa_vec[tb] = 1
                                sy.wait_ge(ACT.h, ACT.n)
                                O1S[tb].inc(sy.dma_start(
                                    out=o1p_scr[oc, :, tb * 512:(tb + 1) * 512],
                                    in_=t1[tb][:, :]), 16)
                        pa_o1s = (O1S[0].n, O1S[1].n)
                        pa_act = ACT.n

                        g.wait_ge(MM.h, MM.n)
                        oslB = (OSL[0].n, OSL[1].n)
                        OSL[0].inc(g.dma_start(out=oslab[0][:, :, :], in_=woutTB[0]), 16)
                        evac_o = [0] * 4
                        o1_vec = [0, 0]
                        xres_vec = [0, 0]
                        o1p_hld = [0, 0]
                        oc_mm4 = [0, 0]
                        for oc in range(c.KC):
                            if oc + 1 < c.KC:
                                par = (oc + 1) % 2
                                if oc_mm4[par]:
                                    g.wait_ge(MM.h, oc_mm4[par])
                                OSL[par].inc(g.dma_start(out=oslab[par][:, :, :],
                                                         in_=woutTB[oc + 1]), 16)
                            t.wait_ge(OSL[oc % 2].h,
                                      oslB[oc % 2] + 16 * (oc // 2 + 1))
                            if oc == 0:
                                t.wait_ge(ACT.h, pa_act)
                                t.wait_ge(XCH.h, 2 * c.HC)
                            banks = [(oc % 2) * 2 + tb for tb in range(c.TBO)]
                            for bk in banks:
                                if evac_o[bk]:
                                    t.wait_ge(ACT.h, evac_o[bk])
                            for kc in range(c.HC):
                                for tb in range(c.TBO):
                                    sl = slice(tb * 512, (tb + 1) * 512)
                                    mi = t.matmul(po[banks[tb]][:, :],
                                                  oslab[oc % 2][:, kc, :],
                                                  recv_oth[:, kc, sl],
                                                  start=(kc == 0), stop=(kc == c.HC - 1))
                            MM.inc(mi)
                            oc_mm4[oc % 2] = MM.n
                            mm_now = MM.n
                            for tb in range(c.TBO):
                                if xres_vec[tb]:
                                    sy.wait_ge(VEC.h, xres_vec[tb])
                                XR[tb].inc(sy.dma_start(
                                    out=xres[tb][:, :],
                                    in_=xresT[oc * 128:(oc + 1) * 128,
                                              tb * 512:(tb + 1) * 512]), 16)
                                sy.wait_ge(O1S[tb].h, pa_o1s[tb])
                                if o1p_hld[tb]:
                                    sy.wait_ge(VEC.h, o1p_hld[tb])
                                RS[tb].inc(sy.dma_start(
                                    out=o1p[tb][:, :],
                                    in_=o1p_scr[oc, :, tb * 512:(tb + 1) * 512]), 16)
                            sc.wait_ge(MM.h, mm_now)
                            for tb in range(c.TBO):
                                if o1_vec[tb]:
                                    sc.wait_ge(VEC.h, o1_vec[tb])
                                else:
                                    sc.wait_ge(O1S[tb].h, pa_o1s[tb])
                                si = sc.activation(t1[tb][:, :], po[banks[tb]][:, :],
                                                   AF.Identity, bias=bout_sb[:, oc:oc + 1])
                                ACT.inc(si)
                                evac_o[banks[tb]] = ACT.n
                                v.wait_ge(ACT.h, ACT.n)
                                v.wait_ge(XR[tb].h, XR[tb].n)
                                v.wait_ge(RS[tb].h, RS[tb].n)
                                if o1_vec[tb]:
                                    v.wait_ge(O1S[tb].h, O1S[tb].n)
                                v.tensor_add(o1f[tb][:, :], t1[tb][:, :],
                                             o1p[tb][:, :])
                                v.drain()
                                VEC.inc(v.tensor_add(o1f[tb][:, :], o1f[tb][:, :],
                                                     xres[tb][:, :]))
                                o1_vec[tb] = VEC.n
                                xres_vec[tb] = VEC.n
                                o1p_hld[tb] = VEC.n
                                sy.wait_ge(VEC.h, VEC.n)
                                O1S[tb].inc(sy.dma_start(
                                    out=o1_scr[oc, :, tb * 512:(tb + 1) * 512],
                                    in_=o1f[tb][:, :]), 16)
                        o1s_final = (O1S[0].n, O1S[1].n)

            fence(O1S[0], O1S[1], XR[0], XR[1], OSL[0], OSL[1], XCH,
                  XL[0], XL[1])

            with ExitStack() as p5o:
                SB5 = lambda n_, s_, d_: p5o.enter_context(nc.sbuf_tensor(n_, s_, d_))
                p_sb = SB5("p_sb", [128, c.RC, c.TOWN], BF16)
                xh2s = ExitStack()
                xhat2 = xh2s.enter_context(nc.sbuf_tensor("xhat2", [128, c.KC, c.TOWN], BF16))
                TO = c.TOWN

                with ExitStack() as pl2:
                    SBL = lambda n_, s_, d_: pl2.enter_context(nc.sbuf_tensor(n_, s_, d_))
                    xs = [SBL(f"xs{i}", [128, c.TOWN], BF16) for i in range(2)]
                    x2 = [SBL(f"x2_{i}", [128, c.TOWN], BF16) for i in range(2)]
                    ab2 = SBL("ab2", [128, c.TOWN], BF16)
                    bb2 = SBL("bb2", [128, c.TOWN], BF16)
                    ntm2 = SBL("ntm2", [128, c.TOWN], BF16)
                    r0 = SBL("r0b", [1, c.TOWN], F32)
                    r1 = SBL("r1b", [1, c.TOWN], F32)
                    r2 = SBL("r2b", [1, c.TOWN], F32)
                    a_bf = SBL("a_bf2", [1, c.TOWN], BF16)
                    b_bf = SBL("b_bf2", [1, c.TOWN], BF16)

                    with ExitStack() as pst2:
                        s2 = pst2.enter_context(nc.psum_tensor("s2", [1, c.TOWN], F32))
                        q2 = pst2.enter_context(nc.psum_tensor("q2", [1, c.TOWN], F32))
                        xs_guard = [0, 0]
                        x2_guard = [0, 0]
                        g.wait_ge(O1S[0].h, o1s_final[0])
                        g.wait_ge(O1S[1].h, o1s_final[1])
                        for kc in range(c.KC):
                            par = kc % 2
                            if xs_guard[par]:
                                g.wait_ge(MM.h, xs_guard[par])
                            XS[par].inc(g.dma_start(out=xs[par][:, :], in_=o1_scr[kc]), 16)
                            sc.wait_ge(XS[par].h, XS[par].n)
                            if x2_guard[par]:
                                sc.wait_ge(MM.h, x2_guard[par])
                            ACT.inc(sc.activation(x2[par][:, :], xs[par][:, :], AF.Square))
                            t.wait_ge(ACT.h, ACT.n)
                            for tb in range(c.TBO):
                                sl = slice(tb * 512, (tb + 1) * 512)
                                t.matmul(s2[0:1, sl], ones128[:, :], xs[par][:, sl],
                                         start=(kc == 0), stop=(kc == c.KC - 1))
                                mi = t.matmul(q2[0:1, sl], ones128[:, :], x2[par][:, sl],
                                              start=(kc == 0), stop=(kc == c.KC - 1))
                            MM.inc(mi)
                            xs_guard[par] = MM.n
                            x2_guard[par] = MM.n

                        v.wait_ge(MM.h, MM.n)
                        v.tensor_scalar_mul(r0[:, :], s2[0:1, :], 1.0 / c.D)
                        v.drain()
                        v.tensor_scalar_mul(r1[:, :], q2[0:1, :], 1.0 / c.D)
                        VEC.inc(v.drain())
                        sc.wait_ge(VEC.h, VEC.n)
                        ACT.inc(sc.activation(r2[:, :], r0[:, :], AF.Square))
                        v.wait_ge(ACT.h, ACT.n)
                        v.tensor_sub(r1[:, :], r1[:, :], r2[:, :])
                        v.drain()
                        v.tensor_scalar_add(r1[:, :], r1[:, :], c.eps)
                        VEC.inc(v.drain())
                        sc.wait_ge(VEC.h, VEC.n)
                        ACT.inc(sc.activation(r2[:, :], r1[:, :], AF.Ln))
                        sc.wait_ge(ACT.h, ACT.n)
                        ACT.inc(sc.activation(r1[:, :], r2[:, :], AF.Exp,
                                              scale=-0.5))
                        v.wait_ge(ACT.h, ACT.n)
                        v.scalar_tensor_tensor(r2[:, :], r0[:, :], -1.0, r1[:, :],
                                               OP.mult, OP.mult)
                        v.drain()
                        v.tensor_copy(a_bf[:, :], r1[:, :])
                        v.drain()
                        v.tensor_copy(b_bf[:, :], r2[:, :])
                        rows2_done = VEC.inc(v.drain())

                    with ExitStack() as pbc2:
                        ab_ps = pbc2.enter_context(nc.psum_tensor("ab2p", [128, 512], F32))
                        bb_ps = pbc2.enter_context(nc.psum_tensor("bb2p", [128, 512], F32))
                        t.wait_ge(VEC.h, rows2_done)
                        for tb in range(c.TBO):
                            sl = slice(tb * 512, (tb + 1) * 512)
                            t.matmul(ab_ps[:, :], ones_col[:, :], a_bf[0:1, sl],
                                     start=True, stop=True)
                            MM.inc(t.matmul(bb_ps[:, :], ones_col[:, :], b_bf[0:1, sl],
                                            start=True, stop=True))
                            sc.wait_ge(MM.h, MM.n)
                            sc.copy(ab2[:, sl], ab_ps[:, :])
                            ACT.inc(sc.copy(bb2[:, sl], bb_ps[:, :]))
                            t.wait_ge(ACT.h, ACT.n)
                        bc2_done = ACT.n

                    xs2_guard = [0, 0]
                    g.wait_ge(MM.h, MM.n)
                    g.wait_ge(ACT.h, ACT.n)
                    for kc in range(c.KC):
                        par = kc % 2
                        if xs2_guard[par]:
                            g.wait_ge(VEC.h, xs2_guard[par])
                        XS[par].inc(g.dma_start(out=xs[par][:, :], in_=o1_scr[kc]), 16)
                        v.wait_ge(XS[par].h, XS[par].n)
                        if kc == 0:
                            v.wait_ge(ACT.h, bc2_done)
                        v.tensor_mul(ntm2[:, :], xs[par][:, :], ab2[:, :])
                        v.drain()
                        v.tensor_add(xhat2[:, kc, :], ntm2[:, :], bb2[:, :])
                        VEC.inc(v.drain())
                        xs2_guard[par] = VEC.n
                    xhat2_done = VEC.n

                fence(XS[0], XS[1])

                with ExitStack() as pf:
                    with ExitStack() as pg4:
                        pg = [pg4.enter_context(nc.psum_tensor(f"pg{i}", [128, 512], F32))
                              for i in range(4)]

                        with ExitStack() as pu1:
                            u1s = [pu1.enter_context(
                                nc.sbuf_tensor(f"u1s{i}", [128, c.KC, 128], BF16))
                                for i in range(2)]
                            U1S[0].inc(g.dma_start(out=u1s[0][:, :, :], in_=u1T[0]), 16)
                            pb_act = [0] * 4
                            u1_mm = [0, 0]
                            for oc in range(c.RC):
                                if oc + 1 < c.RC:
                                    par = (oc + 1) % 2
                                    if u1_mm[par]:
                                        g.wait_ge(MM.h, u1_mm[par])
                                    U1S[par].inc(g.dma_start(out=u1s[par][:, :, :],
                                                             in_=u1T[oc + 1]), 16)
                                t.wait_ge(U1S[oc % 2].h, 16 * (oc // 2 + 1))
                                if oc == 0:
                                    t.wait_ge(VEC.h, xhat2_done)
                                banks = [(oc % 2) * 2 + tb for tb in range(c.TBO)]
                                for bk in banks:
                                    if pb_act[bk]:
                                        t.wait_ge(ACT.h, pb_act[bk])
                                for kc in range(c.KC):
                                    for tb in range(c.TBO):
                                        sl = slice(tb * 512, (tb + 1) * 512)
                                        mi = t.matmul(pg[banks[tb]][:, :],
                                                      u1s[oc % 2][:, kc, :],
                                                      xhat2[:, kc, sl],
                                                      start=(kc == 0), stop=(kc == c.KC - 1))
                                MM.inc(mi)
                                u1_mm[oc % 2] = MM.n
                                sc.wait_ge(MM.h, MM.n)
                                for tb in range(c.TBO):
                                    sl = slice(tb * 512, (tb + 1) * 512)
                                    ACT.inc(sc.copy(p_sb[:, oc, sl], pg[banks[tb]][:, :]))
                                    pb_act[banks[tb]] = ACT.n
                            p_done = ACT.n
                            u1_final_mm = MM.n
                        xh2s.close()

                        with ExitStack() as pv1:
                            SBV = lambda n_, s_, d_: pv1.enter_context(
                                nc.sbuf_tensor(n_, s_, d_))
                            v1s = [SBV(f"v1s{i}", [128, c.RC, 128], BF16) for i in range(4)]
                            u2s = [SBV(f"u2s{i}", [128, c.NH, 128], BF16) for i in range(2)]
                            g_act = [SBV(f"g_act{i}", [128, 512], BF16) for i in range(2)]
                            u_act = [SBV(f"u_act{i}", [128, 512], BF16) for i in range(2)]
                            acth = SBV("acth", [128, c.NH, c.TOWN], BF16)
                            z_sb = SBV("z_sb", [128, c.RC, c.TOWN], F32)
                            zb_sb = SBV("zb_sb", [128, c.RC, c.TOWN], BF16)

                            fence(U1S[0], U1S[1])
                            u2_mm_prev = 0
                            z_prev_vec = 0
                            for half in range(2):
                                v1_mm = [0] * 4
                                ga_vec = 0
                                ga_act = [0, 0]
                                for j in range(c.NH):
                                    ic = half * c.NH + j
                                    bg, bu = 2 * (j % 2), 2 * (j % 2) + 1
                                    for buf, col in ((bg, ic), (bu, c.IC + ic)):
                                        if v1_mm[buf]:
                                            g.wait_ge(MM.h, v1_mm[buf])
                                        V1S[buf].inc(g.dma_start(out=v1s[buf][:, :, :],
                                                                 in_=v1T[col]), 16)
                                    t.wait_ge(V1S[bg].h, V1S[bg].n)
                                    t.wait_ge(V1S[bu].h, V1S[bu].n)
                                    if half == 0 and j == 0:
                                        t.wait_ge(ACT.h, p_done)
                                    if half == 1 and j == 0:
                                        t.wait_ge(VEC.h, z_prev_vec)
                                    for tb in range(c.TBO):
                                        pbg, pbu = tb, 2 + tb
                                        if ga_act[tb]:
                                            t.wait_ge(ACT.h, ga_act[tb])
                                        for kc in range(c.RC):
                                            sl = slice(tb * 512, (tb + 1) * 512)
                                            t.matmul(pg[pbg][:, :], v1s[bg][:, kc, :],
                                                     p_sb[:, kc, sl],
                                                     start=(kc == 0), stop=(kc == c.RC - 1))
                                            mi = t.matmul(pg[pbu][:, :], v1s[bu][:, kc, :],
                                                          p_sb[:, kc, sl],
                                                          start=(kc == 0),
                                                          stop=(kc == c.RC - 1))
                                        MM.inc(mi)
                                        v1_mm[bg] = MM.n
                                        v1_mm[bu] = MM.n
                                        sc.wait_ge(MM.h, MM.n)
                                        if ga_vec:
                                            sc.wait_ge(VEC.h, ga_vec)
                                        sc.activation(g_act[tb][:, :], pg[pbg][:, :],
                                                      AF.Silu, bias=b1_sb[:, ic:ic + 1])
                                        ACT.inc(sc.activation(
                                            u_act[tb][:, :], pg[pbu][:, :], AF.Identity,
                                            bias=b1_sb[:, c.IC + ic:c.IC + ic + 1]))
                                        ga_act[tb] = ACT.n
                                        v.wait_ge(ACT.h, ACT.n)
                                        if half == 1 and j == 0 and tb == 0:
                                            v.wait_ge(MM.h, u2_mm_prev)
                                        sl = slice(tb * 512, (tb + 1) * 512)
                                        VEC.inc(v.tensor_mul(acth[:, j, sl],
                                                             g_act[tb][:, :],
                                                             u_act[tb][:, :]))
                                        ga_vec = VEC.n
                                acth_done = VEC.n

                                g.wait_ge(MM.h, MM.n)
                                U2S[0].inc(g.dma_start(
                                    out=u2s[0][:, :, :],
                                    in_=u2T[0][:, half * c.NH:(half + 1) * c.NH, :]), 16)
                                u2_mm = [0, 0]
                                z_vec = [0, 0]
                                u2s_base = (U2S[0].n - 16, U2S[1].n)
                                for oc in range(c.RC):
                                    if oc + 1 < c.RC:
                                        par = (oc + 1) % 2
                                        if u2_mm[par]:
                                            g.wait_ge(MM.h, u2_mm[par])
                                        U2S[par].inc(g.dma_start(
                                            out=u2s[par][:, :, :],
                                            in_=u2T[oc + 1][:, half * c.NH:(half + 1) * c.NH, :]
                                        ), 16)
                                    t.wait_ge(U2S[oc % 2].h, u2s_base[oc % 2] + 16 * (oc // 2 + 1))
                                    if oc == 0:
                                        t.wait_ge(VEC.h, acth_done)
                                        if half == 1:
                                            VEC.inc(v.drain())
                                    for tb in range(c.TBO):
                                        bk = 2 * (oc % 2) + tb
                                        if z_vec[oc % 2]:
                                            t.wait_ge(VEC.h, z_vec[oc % 2])
                                        for kc in range(c.NH):
                                            sl = slice(tb * 512, (tb + 1) * 512)
                                            mi = t.matmul(pg[bk][:, :], u2s[oc % 2][:, kc, :],
                                                          acth[:, kc, sl],
                                                          start=(kc == 0),
                                                          stop=(kc == c.NH - 1))
                                        MM.inc(mi)
                                        u2_mm[oc % 2] = MM.n
                                        v.wait_ge(MM.h, MM.n)
                                        sl = slice(tb * 512, (tb + 1) * 512)
                                        if half == 0:
                                            vi = v.tensor_copy(z_sb[:, oc, sl], pg[bk][:, :])
                                        else:
                                            vi = v.tensor_add(z_sb[:, oc, sl],
                                                              z_sb[:, oc, sl], pg[bk][:, :])
                                        VEC.inc(vi)
                                        z_vec[oc % 2] = VEC.n
                                u2_mm_prev = MM.n
                                z_prev_vec = VEC.n

                            VEC.inc(v.drain())
                            for oc in range(c.RC):
                                v.drain()
                                VEC.inc(v.tensor_copy(zb_sb[:, oc, :], z_sb[:, oc, :]))
                            z_done = VEC.n

                            with ExitStack() as pv2:
                                SBW = lambda n_, s_, d_: pv2.enter_context(
                                    nc.sbuf_tensor(n_, s_, d_))
                                v2s = [SBW(f"v2s{i}", [128, c.RC, 128], BF16)
                                       for i in range(2)]
                                res2 = [SBW(f"res2{i}", [128, 512], F32) for i in range(2)]
                                t2 = [SBW(f"t2_{i}", [128, 512], F32) for i in range(2)]
                                of = [SBW(f"of{i}", [128, 512], F32) for i in range(2)]
                                V2S[0].inc(g.dma_start(out=v2s[0][:, :, :], in_=v2T[0]), 16)
                                ob_act = [0] * 4
                                of_vec = [0, 0]
                                res_vec = [0, 0]
                                v2_mm = [0, 0]
                                for oc in range(c.KC):
                                    if oc + 1 < c.KC:
                                        par = (oc + 1) % 2
                                        if v2_mm[par]:
                                            g.wait_ge(MM.h, v2_mm[par])
                                        V2S[par].inc(g.dma_start(out=v2s[par][:, :, :],
                                                                 in_=v2T[oc + 1]), 16)
                                    t.wait_ge(V2S[oc % 2].h, 16 * (oc // 2 + 1))
                                    if oc == 0:
                                        t.wait_ge(VEC.h, z_done)
                                    banks = [(oc % 2) * 2 + tb for tb in range(c.TBO)]
                                    for bk in banks:
                                        if ob_act[bk]:
                                            t.wait_ge(ACT.h, ob_act[bk])
                                    for kc in range(c.RC):
                                        for tb in range(c.TBO):
                                            sl = slice(tb * 512, (tb + 1) * 512)
                                            mi = t.matmul(pg[banks[tb]][:, :],
                                                          v2s[oc % 2][:, kc, :],
                                                          zb_sb[:, kc, sl],
                                                          start=(kc == 0),
                                                          stop=(kc == c.RC - 1))
                                    MM.inc(mi)
                                    v2_mm[oc % 2] = MM.n
                                    mm_now = MM.n
                                    for tb in range(c.TBO):
                                        if res_vec[tb]:
                                            sy.wait_ge(VEC.h, res_vec[tb])
                                        RS[tb].inc(sy.dma_start(
                                            out=res2[tb][:, :],
                                            in_=o1_scr[oc, :, tb * 512:(tb + 1) * 512]), 16)
                                    sc.wait_ge(MM.h, mm_now)
                                    for tb in range(c.TBO):
                                        if of_vec[tb]:
                                            sc.wait_ge(VEC.h, of_vec[tb])
                                        si = sc.activation(t2[tb][:, :], pg[banks[tb]][:, :],
                                                           AF.Identity,
                                                           bias=b2_sb[:, oc:oc + 1])
                                        ACT.inc(si)
                                        ob_act[banks[tb]] = ACT.n
                                        v.wait_ge(ACT.h, ACT.n)
                                        v.wait_ge(RS[tb].h, RS[tb].n)
                                        if of_vec[tb]:
                                            v.wait_ge(OF[tb].h, OF[tb].n)
                                        VEC.inc(v.tensor_add(of[tb][:, :], t2[tb][:, :],
                                                             res2[tb][:, :]))
                                        of_vec[tb] = VEC.n
                                        res_vec[tb] = VEC.n
                                        sy.wait_ge(VEC.h, VEC.n)
                                        OF[tb].inc(sy.dma_start(
                                            out=outT[oc * 128:(oc + 1) * 128,
                                                     tb * 512:(tb + 1) * 512],
                                            in_=of[tb][:, :]), 16)

            sy.wait_ge(OF[0].h, OF[0].n)
            sy.wait_ge(OF[1].h, OF[1].n)

    return nc



def _tile_w(w):
    K, N = w.shape
    kc, oc = K // 128, N // 128
    t = w.reshape(kc, 128, oc, 128).transpose(2, 1, 0, 3)
    return np.ascontiguousarray(t, dtype=np.float32)


def _alibi_slopes(n_heads, alibi_bias_max=8):
    _n = 2 ** math.ceil(math.log2(n_heads))
    m = np.arange(1, _n + 1, dtype=np.float32) * (alibi_bias_max / _n)
    slopes = 1.0 / np.power(2.0, m)
    if _n != n_heads:
        slopes = np.concatenate([slopes[1::2], slopes[::2]])[:n_heads]
    return slopes.astype(np.float32)


def host_inputs(c: Cfg, x, w_qkv, b_qkv, w_out, b_out, U1, V1, U2, V2, b1, b2):
    f32 = np.float32
    H = 2 * c.HC
    D, S = c.D, c.S
    slopes = _alibi_slopes(H)
    positions = np.arange(1 - S, 1, dtype=f32)

    wqkvT_g = np.ascontiguousarray(np.asarray(w_qkv, f32).T)
    woutT_g = np.ascontiguousarray(np.asarray(w_out, f32).T)
    x = np.asarray(x, f32)
    u1T = _tile_w(np.asarray(U1, f32))
    v1T = _tile_w(np.asarray(V1, f32))
    u2T = _tile_w(np.asarray(U2, f32))
    v2T = _tile_w(np.asarray(V2, f32))
    b1m = np.ascontiguousarray(np.asarray(b1, f32).reshape(2 * c.IC, 128).T)
    b2m = np.ascontiguousarray(np.asarray(b2, f32).reshape(c.KC, 128).T)
    boutm = np.ascontiguousarray(np.asarray(b_out, f32).reshape(c.KC, 128).T)
    tri = np.where(np.arange(128)[None, :] < np.arange(128)[:, None],
                   np.float32(-1e9), np.float32(0.0))
    tri = np.ascontiguousarray(tri, f32)

    in_maps = []
    for core in range(N_CORES):
        b, hh = core // 2, core % 2
        fh = c.HC * 128
        cols = np.r_[hh * fh:(hh + 1) * fh,
                     D + hh * fh:D + (hh + 1) * fh,
                     2 * D + hh * fh:2 * D + (hh + 1) * fh]
        wq_my = wqkvT_g[:, cols]
        bq_my = np.asarray(b_qkv, f32)[cols]
        wout_a = woutT_g[hh * fh:(hh + 1) * fh]
        wout_b = woutT_g[(1 - hh) * fh:(2 - hh) * fh]

        myheads = np.arange(hh * c.HC, (hh + 1) * c.HC)
        ab = np.zeros((128, c.HC * c.SC), f32)
        nal = np.zeros((c.HC, S), f32)
        for i, h in enumerate(myheads):
            nal[i] = -slopes[h] * positions
            for kc in range(c.SC):
                ab[:, i * c.SC + kc] = slopes[h] * positions[kc * 128:(kc + 1) * 128]

        in_maps.append({
            "xT": np.ascontiguousarray(x[b].T, f32),
            "xresT": np.ascontiguousarray(x[b, hh * c.TOWN:(hh + 1) * c.TOWN].T, f32),
            "wqkvT": _tile_w(wq_my),
            "bqkv": np.ascontiguousarray(bq_my.reshape(c.OCQKV, 128).T),
            "woutTA": _tile_w(wout_a),
            "woutTB": _tile_w(wout_b),
            "bout": boutm,
            "u1T": u1T, "v1T": v1T, "u2T": u2T, "v2T": v2T,
            "b1": b1m, "b2": b2m,
            "abias": ab, "nalibi": nal, "trineg": tri,
            "othoff": np.array([[1 - hh]], np.uint32),
            "ownoff": np.array([[hh]], np.uint32),
        })
    return in_maps


def kernel(L0_attn_x, L0_attn_w_qkv, L0_attn_b_qkv, L0_attn_w_out, L0_attn_b_out,
           L0_ffn_U1, L0_ffn_V1, L0_ffn_U2, L0_ffn_V2, L0_ffn_b1, L0_ffn_b2):
    c = REAL
    in_maps = host_inputs(c, L0_attn_x, L0_attn_w_qkv, L0_attn_b_qkv,
                          L0_attn_w_out, L0_attn_b_out, L0_ffn_U1, L0_ffn_V1,
                          L0_ffn_U2, L0_ffn_V2, L0_ffn_b1, L0_ffn_b2)
    nc = build_program(c)
    nc.finalize()
    res = run_bass_kernel_spmd(nc, in_maps, core_ids=list(range(N_CORES)))
    x = np.asarray(L0_attn_x)
    B, S, D = x.shape
    out = np.empty((B, S, D), np.float32)
    for core in range(N_CORES):
        b, hh = core // 2, core % 2
        piece = np.asarray(res.results[core]["outT"])
        out[b, hh * c.TOWN:(hh + 1) * c.TOWN, :] = piece.T
    return out
